# version 31
# baseline (speedup 1.0000x reference)
import numpy as np
import ml_dtypes
import concourse.bass as bass
import concourse.mybir as mybir
from concourse.bass_utils import run_bass_kernel_spmd

F32 = mybir.dt.float32
BF16 = mybir.dt.bfloat16
AF = mybir.ActivationFunctionType
ALU = mybir.AluOpType
AX = mybir.AxisListType
BFNP = ml_dtypes.bfloat16

L = 2048
DM = 1024
NCH = 16
NH = 16
H2 = 4
NE = 8
EPS_LN = 1e-5
EPS_RMS = 1e-6
EPS_Z = 1e-6
INV = 1.0 / DM

ENGS = ("sync", "gpsimd", "tensor", "vector", "scalar")

TAPS = ()


class KB:

    DOMS = ENGS + ("prep", "sdma")

    def __init__(self, nc):
        self.nc = nc
        self.sems = {e: nc.alloc_semaphore(name=f"sem_{e}") for e in self.DOMS}
        self.cnt = dict.fromkeys(self.DOMS, 0)
        self.wm = {e: dict.fromkeys(self.DOMS, 0) for e in ENGS}
        self.q = {e: [] for e in ENGS}
        self.lastg = dict.fromkeys(self.DOMS, None)

    def tok(self, e):
        return (e, self.cnt[e])

    def _wait(self, e, toks):
        for t in toks:
            if t is None:
                continue
            oe, v = t
            if v <= 0 or self.wm[e][oe] >= v:
                continue
            self.wm[e][oe] = v
            s = self.sems[oe]
            self.q[e].append(lambda eng, s=s, v=v: eng.wait_ge(s, v))

    def op(self, e, fn, waits=(), inc=0, dom=None):
        self._wait(e, waits)
        if inc:
            dom = dom or e
            self.cnt[dom] += inc
            s = self.sems[dom]
            self.q[e].append(lambda eng, fn=fn, s=s, i=inc: fn(eng).then_inc(s, i))
            return (dom, self.cnt[dom])
        self.q[e].append(fn)
        return None

    def dma(self, e, out, in_, waits=(), dom=None, new_group=True):
        dom = dom or e
        w = list(waits) + ([self.lastg[dom]] if new_group else [])
        t = self.op(e, lambda eng: eng.dma_start(out=out, in_=in_), w, inc=16, dom=dom)
        self.lastg[dom] = t
        return t

    def dmat(self, e, out, in_, waits=(), dom=None, new_group=True):
        dom = dom or e
        w = list(waits) + ([self.lastg[dom]] if new_group else [])
        t = self.op(e, lambda eng: eng.dma_start(out=out, in_=in_, transpose=True),
                    w, inc=16, dom=dom)
        self.lastg[dom] = t
        return t

    def barrier(self):
        toks = [self.tok(e) for e in ENGS]
        for e in ENGS:
            self._wait(e, toks)

    def final_wait(self, e):
        s = self.sems[e]
        v = self.cnt[e]
        self.q[e].append(lambda eng, s=s, v=v: eng.wait_ge(s, v))

    def build(self, block):
        for e in ENGS:
            q = self.q[e]
            getattr(block, e)(lambda eng, q=q: [f(eng) for f in q])


class Pool:
    def __init__(self, nc, n):
        self.tiles = [nc.alloc_sbuf_tensor(f"pl{i}", [128, 2048], BF16) for i in range(n)]
        self.n = n
        self.owner = [None] * n

    def alloc(self, name, count):
        idxs = [i for i in range(self.n) if self.owner[i] is None][:count]
        assert len(idxs) == count, \
            f"pool OOM: {name} wants {count}, free={self.owner.count(None)}"
        for i in idxs:
            self.owner[i] = name
        return [self.tiles[i] for i in idxs]

    def free(self, *names):
        for i in range(self.n):
            if self.owner[i] in names:
                self.owner[i] = None


def f32v(tile):
    return tile.ap().bitcast(F32)


def build_nc(taps=()):
    nc = bass.Bass("TRN2", target_bir_lowering=False)

    din = {}
    def inp(name, shape):
        din[name] = nc.dram_tensor(name, list(shape), F32, kind="ExternalInput")
        return din[name]

    for nm, sh in (("x", (L, DM)), ("xq", (L, DM)), ("xr", (L, DM)),
                   ("sccq", (128, 32)), ("scck", (128, 32)), ("srowq", (2, L)),
                   ("aWq", (DM, DM)), ("abq", (1, DM)), ("aWk", (DM, DM)),
                   ("abk", (1, DM)), ("aWv", (DM, DM)), ("abv", (1, DM)),
                   ("aWo", (DM, DM)), ("abo", (1, DM)), ("g1", (1, DM)),
                   ("g2", (1, DM)), ("fWq", (DM, DM)), ("fWkc", (256, DM)),
                   ("fWvc", (256, DM)), ("fWo", (DM, DM)), ("fWr", (NE, DM)),
                   ("febias", (1, NE)), ("sW1", (DM, DM)), ("sW2", (DM, DM)),
                   ("sW3", (DM, DM)), ("eW1", (NE, DM, DM)), ("eW2", (NE, DM, DM)),
                   ("eW3", (NE, DM, DM))):
        inp(nm, sh)

    out_d = nc.dram_tensor("out", [L, DM], F32, kind="ExternalOutput")
    TAP_SPECS = {
        "x2": ([L, DM], F32), "xff": ([L, DM], F32), "combine": ([128, 128], F32),
        "lnx0": ([128, 1024], BF16), "tch0": ([128, 1024], BF16),
        "k0": ([128, 1024], F32), "eq0": ([128, 1024], F32),
        "kft0": ([128, 2048], BF16), "vex0": ([128, 2048], BF16),
        "kvacc": ([128, 2048], F32), "kfacc32": ([128, 32], F32),
        "kvall0": ([128, 2048], BF16), "qftt0": ([128, 2048], BF16),
        "atn0": ([128, 2048], BF16), "nxt0": ([128, 2048], BF16),
        "qt0": ([128, 2048], BF16), "kct0": ([128, 2048], BF16),
        "a2t0": ([128, 2048], BF16), "cols0": ([128, 32], F32),
    }
    tap_d = {}
    for t in taps:
        shp, dt_ = TAP_SPECS[t]
        tap_d[t] = nc.dram_tensor("tap_" + t, shp, dt_, kind="ExternalOutput")

    R_FWQ, R_FWO, R_FWKC, R_FWVC = 0, 1024, 2048, 2304
    R_SW1, R_SW2, R_SW3 = 2560, 3584, 4608
    R_EW1, R_EW2, R_EW3 = 5632, 13824, 22016
    wsc = nc.dram_tensor("wsc", [30208, DM], BF16)
    cc_in = nc.dram_tensor("cc_in", [128, 2080], BF16)
    cc_out = nc.dram_tensor("cc_out", [1024, 2080], BF16)
    xsp = nc.dram_tensor("xsp", [L, DM], BF16)
    comb_dram = nc.dram_tensor("comb_dram", [NE, L], BF16)
    ident_d = nc.inline_tensor(np.eye(128, dtype=BFNP), name="ident_c")

    ident = nc.alloc_sbuf_tensor("ident", [128, 128], BF16)
    onec_bf = nc.alloc_sbuf_tensor("onec_bf", [1, 128], BF16)
    onec_f = nc.alloc_sbuf_tensor("onec_f", [1, 128], F32)
    brow = nc.alloc_sbuf_tensor("brow", [1, DM], F32)
    srow_sin = nc.alloc_sbuf_tensor("srow_sin", [1, L], BF16)
    srow_cos = nc.alloc_sbuf_tensor("srow_cos", [1, L], BF16)
    sccq_sb = nc.alloc_sbuf_tensor("sccq_sb", [128, 32], F32)
    scck_sb = nc.alloc_sbuf_tensor("scck_sb", [128, 32], F32)
    cols = nc.alloc_sbuf_tensor("cols", [128, 48], F32)
    hsum = nc.alloc_sbuf_tensor("hsum", [128, 96], F32)
    hrcp = nc.alloc_sbuf_tensor("hrcp", [128, 32], F32)
    kfacc = nc.alloc_sbuf_tensor("kfacc", [128, 32], F32)
    kfbf = nc.alloc_sbuf_tensor("kfbf", [128, 32], BF16)
    kvfs = nc.alloc_sbuf_tensor("kvfs", [128, 256], BF16)
    dencol = nc.alloc_sbuf_tensor("dencol", [128, 32], F32)
    zcol = nc.alloc_sbuf_tensor("zcol", [128, 32], F32)
    rt = nc.alloc_sbuf_tensor("rt", [128, 192], F32)
    combine = nc.alloc_sbuf_tensor("combine", [128, 128], F32)
    combine_bf = nc.alloc_sbuf_tensor("combine_bf", [128, 128], BF16)
    combT = nc.alloc_sbuf_tensor("combT", [8, L], BF16)
    wrbf = nc.alloc_sbuf_tensor("wrbf", [8, DM], BF16)
    wrt = nc.alloc_sbuf_tensor("wrt", [128, 64], BF16)
    eb_bc = nc.alloc_sbuf_tensor("eb_bc", [128, 8], F32)
    avz = nc.alloc_sbuf_tensor("avz", [128, 8], F32)
    cst = nc.alloc_sbuf_tensor("cst", [128, 2], F32)
    rowA = nc.alloc_sbuf_tensor("rowA", [1, DM], BF16)
    rowB = nc.alloc_sbuf_tensor("rowB", [1, DM], BF16)
    onescol = nc.alloc_sbuf_tensor("onescol", [128, 1], BF16)

    pool = Pool(nc, 45)

    _psg = []
    ps = []
    for i in range(6):
        g_ = nc.psum_tensor(f"ps{i}", [128, 512], F32)
        _psg.append(g_)
        ps.append(g_.__enter__())
    _ptg = nc.psum_tensor("pt", [128, 1024], BF16)
    _psg.append(_ptg)
    pt = _ptg.__enter__()
    _ps6g = nc.psum_tensor("ps6", [128, 512], F32)
    ps.append(_ps6g.__enter__())
    ptst = {"pt2": None, "w": None}

    k = KB(nc)
    cc_sem = nc.alloc_semaphore(name="cc_sem")

    with nc.Block() as block:
        G = {}

        def gw(name):
            return G.get(name)

        def tap(name, src_ap, tok_, guard=None):
            if name in taps:
                t_ = k.dma("sync", tap_d[name][:, :], src_ap, waits=(tok_,))
                if guard is not None:
                    gset(guard, t_)

        def gset(name, tok_):
            if tok_ is not None:
                G[name] = tok_

        CS = pool.alloc("cs", 4)
        WN = pool.alloc("wn", 8)
        TCH = pool.alloc("tch", 2)
        LNC = pool.alloc("lnc", 2)
        KFT = pool.alloc("kft", 2)
        KFC = pool.alloc("kfc", 2)
        VEX = pool.alloc("vex", 2)
        KVA = pool.alloc("kva", 2)
        BC = pool.alloc("bc", 2)
        SCR = pool.alloc("scr", 2)
        LSC = pool.alloc("lsc", 2) + pool.alloc("lsce", 2)

        BCv = [f32v(BC[0]), f32v(BC[1])]

        crow = brow.ap().bitcast(BF16)
        t_id = k.dma("sync", ident[:, :], ident_d[:, :])
        k.dma("sync", sccq_sb[:, :], din["sccq"][:, :])
        t_scck = k.dma("sync", scck_sb[:, :], din["scck"][:, :])
        t_ones = k.op("vector", lambda e: e.memset(onec_bf.ap(), 1.0), inc=0)
        k.op("vector", lambda e: e.memset(onescol.ap(), 1.0), inc=0)
        t_ones = k.op("vector", lambda e: e.memset(onec_f.ap(), 1.0), inc=1)
        t_srs = k.dma("gpsimd", srow_sin[:, :], din["srowq"][0:1, :])
        t_src = k.dma("gpsimd", srow_cos[:, :], din["srowq"][1:2, :])
        k.op("vector", lambda e: e.memset(cst[:, 0:1], EPS_LN))
        t_cst = k.op("vector", lambda e: e.memset(cst[:, 1:2], EPS_RMS), inc=1)
        k.op("vector", lambda e: e.memset(f32v(KVA[0]), 0.0))
        k.op("vector", lambda e: e.memset(f32v(KVA[1]), 0.0))
        t_kvz = k.op("vector", lambda e: e.memset(kfacc.ap(), 0.0), inc=1)

        def bias_bcast(dst_f32, row_in, psum_guards=(), expize=False):
            tl = k.dma("sync", brow[:, :], row_in, waits=(gw("brow"),))
            tms = []
            for h in range(2):
                tms.append(k.op("tensor", lambda e, h=h: e.matmul(
                    ps[4 + h][:, :], onec_f[0:1, :], brow[0:1, h * 512:(h + 1) * 512],
                    start=True, stop=True),
                    waits=[tl, t_ones] + list(psum_guards), inc=1))
            tv = None
            for h in range(2):
                if expize:
                    tv = k.op("scalar", lambda e, h=h, d=dst_f32: e.activation(
                        d[:, h * 512:(h + 1) * 512], ps[4 + h][:, :], AF.Exp),
                        waits=(tms[h],), inc=(h == 1))
                else:
                    tv = k.op("vector", lambda e, h=h, d=dst_f32: e.tensor_copy(
                        d[:, h * 512:(h + 1) * 512], ps[4 + h][:, :]),
                        waits=(tms[h],), inc=(h == 1))
            gset("brow", tv)
            return tv

        def ln_stats(src_ap, eps, par, waits):
            c = cols
            s1 = c[:, par * 12 + 0: par * 12 + 1]
            s2 = c[:, par * 12 + 1: par * 12 + 2]
            mean = c[:, par * 12 + 2: par * 12 + 3]
            msq = c[:, par * 12 + 3: par * 12 + 4]
            var = c[:, par * 12 + 4: par * 12 + 5]
            rstd = c[:, par * 12 + 5: par * 12 + 6]
            nmr = c[:, par * 12 + 6: par * 12 + 7]
            scrap = f32v(LSC[par])
            w = list(waits) + [gw(f"lsc{par}"), gw(f"cols{par}")]
            tv = k.op("vector", lambda e: e.reduce_sum(s1, src_ap, axis=AX.X,
                                                       negate=True),
                      waits=w, inc=1)
            tsq = k.op("scalar", lambda e: e.activation(scrap, src_ap, AF.Square,
                                                        accum_out=s2),
                       waits=w, inc=1)
            gset(f"lsc{par}", tsq)
            t_ = k.op("vector", lambda e: e.tensor_scalar(mean, s1, INV, None,
                                                          op0=ALU.mult),
                      waits=(tsq, tv), inc=1)
            t_ = k.op("vector", lambda e: e.tensor_tensor(msq, mean, mean, op=ALU.mult),
                      waits=(t_,), inc=1)
            t_ = k.op("vector", lambda e: e.tensor_scalar(var, s2, INV, None,
                                                          op0=ALU.mult),
                      waits=(t_,), inc=1)
            tv = k.op("vector", lambda e: e.tensor_sub(var, var, msq),
                      waits=(t_,), inc=1)
            eap = cst[:, 0:1] if eps == EPS_LN else cst[:, 1:2]
            t_ = k.op("scalar", lambda e: e.activation(rstd, var, AF.Ln, bias=eap),
                      waits=(tv, t_cst), inc=1)
            tl = k.op("scalar", lambda e: e.activation(rstd, rstd, AF.Exp, scale=-0.5),
                      waits=(t_,), inc=1)
            tl = k.op("vector", lambda e: e.tensor_tensor(nmr, mean, rstd, op=ALU.mult),
                      waits=(tl,), inc=1)
            return nmr, rstd, tl

        def rms_stats(src_ap, eps, par, waits):
            c = cols
            s2 = c[:, par * 12 + 7: par * 12 + 8]
            rstd = c[:, par * 12 + 8: par * 12 + 9]
            scrap = f32v(LSC[par])
            w = list(waits) + [gw(f"lsc{par}"), gw(f"cols{par}")]
            tsq = k.op("scalar", lambda e: e.activation(scrap, src_ap, AF.Square,
                                                        accum_out=s2), waits=w, inc=1)
            gset(f"lsc{par}", tsq)
            tv = k.op("vector", lambda e: e.tensor_scalar(rstd, s2, INV, None,
                                                          op0=ALU.mult),
                      waits=(tsq,), inc=1)
            eap = cst[:, 0:1] if eps == EPS_LN else cst[:, 1:2]
            t_ = k.op("scalar", lambda e: e.activation(rstd, rstd, AF.Ln, bias=eap),
                      waits=(tv, t_cst), inc=1)
            tl = k.op("scalar", lambda e: e.activation(rstd, rstd, AF.Exp, scale=-0.5),
                      waits=(t_,), inc=1)
            return rstd, tl

        def transpose_chunk(src_bf_ap, ngrp, dst_fn, src_ready, dst_guard=(),
                            dst_contig=None):
            pt2 = ptst["pt2"]
            tev = None
            tev_other = None
            first = True
            nr = (ngrp + 3) // 4
            for r in range(nr):
                q = r * 4
                n = min(4, ngrp - q)
                if pt2 is None:
                    bank, key = pt, "pt0"
                    eng = "scalar" if dst_contig is not None else "vector"
                else:
                    hb = r % 2
                    bank = pt if hb == 0 else pt2
                    key = f"pt{hb}"
                    eng = "vector"
                tmm = None
                for i in range(n):
                    w = list(src_ready) + [t_id, gw(key)]
                    if bank is not pt:
                        w.append(ptst["w"])
                    tmm = k.op("tensor", lambda e, i=i, q=q, bank=bank: e.transpose(
                        bank[:, i * 128:(i + 1) * 128],
                        src_bf_ap[:, (q + i) * 128:(q + i + 1) * 128],
                        ident[:, :]), waits=w, inc=(1 if i == n - 1 else 0))
                xw = list(dst_guard) if first else []
                if r == nr - 1 and tev_other is not None:
                    xw.append(tev_other)
                if dst_contig is not None:
                    if eng == "scalar":
                        fn = lambda e, q=q, n=n, bank=bank: e.copy(
                            dst_contig[:, q * 128:(q + n) * 128], bank[:, 0:n * 128])
                    else:
                        fn = lambda e, q=q, n=n, bank=bank: e.tensor_copy(
                            dst_contig[:, q * 128:(q + n) * 128], bank[:, 0:n * 128])
                    tev_new = k.op(eng, fn, waits=[tmm] + xw, inc=1)
                else:
                    tev_new = None
                    for i in range(n):
                        w = [tmm] + (xw if i == 0 else [])
                        if eng == "scalar":
                            fn = lambda e, i=i, q=q, bank=bank: e.copy(
                                dst_fn(q + i), bank[:, i * 128:(i + 1) * 128])
                        else:
                            fn = lambda e, i=i, q=q, bank=bank: e.tensor_copy(
                                dst_fn(q + i), bank[:, i * 128:(i + 1) * 128])
                        tev_new = k.op(eng, fn, waits=w, inc=(1 if i == n - 1 else 0))
                tev_other = tev
                tev = tev_new
                gset(key, tev_new)
                first = False
            return tev

        wk_ap = lambda dc: WN[dc // 2][:, (dc % 2) * 1024:(dc % 2 + 1) * 1024]
        wv_ap = lambda dc: WN[4 + dc // 2][:, (dc % 2) * 1024:(dc % 2 + 1) * 1024]

        t_bk = k.dma("gpsimd", rowA[:, :], din["abk"][0:1, :])
        t_bv = k.dma("gpsimd", rowB[:, :], din["abv"][0:1, :])

        wk_t, wv_t = [], []
        for dc in range(8):
            wk_t.append(k.dma("gpsimd", wk_ap(dc), din["aWk"][dc * 128:(dc + 1) * 128, :],
                              new_group=(dc == 0)))
        wk_t = [wk_t[-1]] * 8
        for dc in range(8):
            wv_t.append(k.dma("gpsimd", wv_ap(dc), din["aWv"][dc * 128:(dc + 1) * 128, :],
                              new_group=(dc == 0)))
        wv_t = [wv_t[-1]] * 8

        t_projps = [t_bv, t_bv]
        t_vproj_ps = None
        t_kvps = [None, None]
        t_kfps = None

        def a_stats(ch, src, tap0=False):
            rg = ch % 4
            csl = ch % 4
            xc = f32v(CS[csl])
            tld = k.dma("sync", xc, src[ch * 128:(ch + 1) * 128, :],
                        waits=(gw(f"cs{csl}"),))
            mean, rstd, tst = ln_stats(xc, EPS_LN, rg, (tld,))
            lnc_ap = LNC[rg // 2][:, (rg % 2) * 1024:(rg % 2 + 1) * 1024]
            t_ln = k.op("scalar", lambda e, s=xc, m=mean, r=rstd, d=lnc_ap:
                        e.activation(d, s, AF.Identity, scale=r, bias=m),
                        waits=(tst, gw(f"lnc{rg}")), inc=1)
            gset(f"cs{csl}", t_ln)
            gset(f"cols{rg}", t_ln)
            if tap0 and ch == 0:
                tap("lnx0", lnc_ap, t_ln, guard=f"lnc{rg}")
                tap("cols0", cols[:, 0:32], tst, guard=f"cols{rg}")
            return lnc_ap, t_ln

        def a_tr(ch, st, tap0=False):
            rg = ch % 4
            lnc_ap, t_ln = st
            tch_ap = TCH[rg // 2][:, (rg % 2) * 1024:(rg % 2 + 1) * 1024]
            t_tr = transpose_chunk(lnc_ap, 8, None,
                                   (t_ln,), dst_guard=(gw(f"tch{rg}"),),
                                   dst_contig=tch_ap)
            gset(f"lnc{rg}", t_tr)
            if tap0 and ch == 0:
                tap("tch0", tch_ap, t_tr, guard=f"tch{rg}")
            return tch_ap, t_tr

        sta = {c_: a_stats(c_, din["x"], tap0=True) for c_ in range(3)}
        stq = {0: a_tr(0, sta.pop(0), tap0=True)}
        for ch in range(NCH):
            par = ch % 2
            csl = ch % 4
            if ch + 3 < NCH:
                sta[ch + 3] = a_stats(ch + 3, din["x"], tap0=True)
            tch_ap, t_tr = stq.pop(ch)
            tmmk = [None, None]
            for hf in range(2):
                for dc in range(8):
                    w = [t_tr, wk_t[dc]] + ([t_projps[hf]] if dc == 0 else [])
                    k.op("tensor", lambda e, hf=hf, dc=dc, a=tch_ap: e.matmul(
                        ps[4 + hf][:, :], a[:, dc * 128:(dc + 1) * 128],
                        wk_ap(dc)[:, hf * 512:(hf + 1) * 512],
                        start=(dc == 0), stop=False), waits=w)
                tmmk[hf] = k.op("tensor", lambda e, hf=hf: e.matmul(
                    ps[4 + hf][:, :], onec_bf[0:1, :],
                    rowA[0:1, hf * 512:(hf + 1) * 512],
                    start=False, stop=True), waits=(t_bk, t_ones), inc=1)
            eq = f32v(SCR[par])
            t_exp = None
            for hf in range(2):
                t_exp = k.op("scalar", lambda e, hf=hf, d=eq: e.activation(
                    d[:, hf * 512:(hf + 1) * 512], ps[4 + hf][:, :], AF.Exp),
                    waits=(tmmk[hf], gw(f"scr{par}") if hf == 0 else None),
                    inc=1)
            tevk = t_exp
            gset(f"tch{ch % 4}", None) if False else None
            if ch == 0:
                tap("eq0", eq, t_exp, guard=f"scr{par}")
            hs = hsum[:, par * 16:(par + 1) * 16]
            hr = hrcp[:, par * 16:(par + 1) * 16]
            t_hsr = k.op("vector", lambda e, s=eq, d=hs: e.reduce_sum(
                d.rearrange("p (g o) -> p g o", o=1),
                s.rearrange("p (g i) -> p g i", g=16), axis=AX.X),
                waits=(t_exp, gw(f"hs{par}")), inc=1)
            t_hr = k.op("vector", lambda e, s=hs, d=hr: e.reciprocal(d, s),
                        waits=(t_hsr,), inc=1)
            kft_t, kfc_t = KFT[par], KFC[par]
            sin_c = scck_sb[:, ch:ch + 1]
            cos_c = scck_sb[:, 16 + ch:16 + ch + 1]
            hr2 = hrcp[:, par * 16 + 32:(par + 1) * 16 + 32] if False else None
            t_kft = None
            for half, colp in ((0, sin_c), (1, cos_c)):
                hsc = hsum[:, 32 + par * 32 + half * 16:32 + par * 32 + (half + 1) * 16]
                t_s2 = k.op("vector", lambda e, d=hsc, c=colp, hr=hr: e.tensor_scalar(
                    d, hr, c, None, op0=ALU.mult),
                    waits=(t_hr, t_scck, gw(f"kft{par}")), inc=1)
                t_kft = k.op("vector", lambda e, s=eq, d=kft_t, half=half,
                             hv=hsc.unsqueeze(2).broadcast_to([128, 16, 64])
                             : e.tensor_tensor(
                                 d.ap().rearrange("p (h j) -> p h j", j=128)
                                 [:, :, half * 64:(half + 1) * 64],
                                 s.rearrange("p (h j) -> p h j", j=64),
                                 hv, op=ALU.mult), waits=(t_s2,), inc=1)
            gset(f"scr{par}", t_kft)
            gset(f"hs{par}", t_kft)
            t_projps = [tevk, tevk]
            t_kfc = None
            for half, colp in ((0, sin_c), (1, cos_c)):
                t_kfc = k.op("vector", lambda e, d=kfc_t, s=kft_t, c=colp, half=half:
                             e.tensor_scalar(
                                 d.ap().rearrange("p (h j) -> p h j", j=128)
                                 [:, :, half * 64:(half + 1) * 64],
                                 s.ap().rearrange("p (h j) -> p h j", j=128)
                                 [:, :, half * 64:(half + 1) * 64],
                                 c, -1.0, op0=ALU.subtract, op1=ALU.mult),
                             waits=(t_kft, gw(f"kfc{par}")), inc=(half == 1))
            if ch == 0:
                tap("kft0", kft_t.ap(), t_kfc, guard=f"kft{par}")
            tmmv = [None, None]
            for hf in range(2):
                for dc in range(8):
                    w = [t_tr, wv_t[dc]] + ([t_vproj_ps] if dc == 0 else [])
                    k.op("tensor", lambda e, hf=hf, dc=dc, a=tch_ap: e.matmul(
                        ps[2 + hf][:, :], a[:, dc * 128:(dc + 1) * 128],
                        wv_ap(dc)[:, hf * 512:(hf + 1) * 512],
                        start=(dc == 0), stop=False), waits=w)
                tmmv[hf] = k.op("tensor", lambda e, hf=hf: e.matmul(
                    ps[2 + hf][:, :], onec_bf[0:1, :],
                    rowB[0:1, hf * 512:(hf + 1) * 512],
                    start=False, stop=True), waits=(t_bv, t_ones), inc=1)
            vex_t = VEX[par]
            t_vx = None
            for hf in range(2):
                t_vx = k.op("scalar", lambda e, hf=hf, d=vex_t: e.copy(
                    d[:, hf * 512:(hf + 1) * 512], ps[2 + hf][:, :]),
                    waits=(tmmv[hf], gw(f"vex{par}") if hf == 0 else None),
                    inc=(hf == 1))
            t_vproj_ps = t_vx
            gset(f"tch{ch % 4}", t_vx)
            for batch in range(2):
                last_mm = None
                for hh in range(8):
                    h = batch * 8 + hh
                    for br in range(2):
                        t = h * 2 + br
                        tloc = hh * 2 + br
                        lhs = (kft_t if br == 0 else kfc_t)[:, h * 128:(h + 1) * 128]
                        w = [t_kfc, t_vx]
                        if hh == 0 and br == 0:
                            w += [t_kvps[0] if batch == 1 else t_kvps[1], t_kfps]
                        k.op("tensor", lambda e, tloc=tloc, lhs=lhs, h=h, v=vex_t:
                             e.matmul(
                            ps[tloc // 8][:, (tloc % 8) * 64:(tloc % 8 + 1) * 64],
                            lhs, v[:, h * 64:(h + 1) * 64],
                            start=True, stop=True), waits=w)
                        last_mm = k.op("tensor", lambda e, t=t, lhs=lhs: e.matmul(
                            ps[6][:, t:t + 1], lhs, onescol[:, 0:1],
                            start=True, stop=True),
                            inc=(1 if (hh == 7 and br == 1) else 0))
                tacc = None
                for b in range(2):
                    dst = f32v(KVA[batch])
                    off = b * 512
                    tacc = k.op("vector", lambda e, b=b, dst=dst, off=off:
                                e.tensor_add(dst[:, off:off + 512],
                                             dst[:, off:off + 512], ps[b][:, :]),
                                waits=(last_mm, t_kvz), inc=(b == 1))
                t_kvps[batch] = tacc
                if batch == 1:
                    t_kfps = k.op("vector", lambda e: e.tensor_add(
                        kfacc[:, :], kfacc[:, :], ps[6][:, 0:32]),
                        waits=(last_mm,), inc=1)
            if ch == 0:
                tap("vex0", vex_t.ap(), t_vx, guard=f"vex{par}")
            gset(f"kft{par}", t_kfps)
            gset(f"kfc{par}", t_kfps)
            gset(f"vex{par}", t_kfps)
            if ch + 1 < NCH:
                stq[ch + 1] = a_tr(ch + 1, sta.pop(ch + 1), tap0=True)

        if "kvacc" in taps:
            k.dma("sync", tap_d["kvacc"][:, 0:1024], f32v(KVA[0]),
                  waits=(t_kvps[0], t_kvps[1]))
            k.dma("sync", tap_d["kvacc"][:, 1024:2048], f32v(KVA[1]),
                  waits=(t_kvps[0], t_kvps[1]))
        tap("kfacc32", kfacc[:, :], t_kfps)
        kvbf0 = SCR[0]
        t1 = k.op("vector", lambda e: e.tensor_copy(kvbf0[:, 0:1024], f32v(KVA[0])),
                  waits=(t_kvps[0], t_kvps[1], gw("scr0")), inc=1)
        t1 = k.op("vector", lambda e: e.tensor_copy(kvbf0[:, 1024:2048], f32v(KVA[1])),
                  inc=1)
        t2 = k.op("vector", lambda e: e.tensor_copy(kfbf[:, :], kfacc[:, :]),
                  waits=(t_kfps,), inc=1)
        tci1 = k.dma("sync", cc_in[:, 0:2048], kvbf0[:, :], waits=(t1,))
        tci2 = k.dma("sync", cc_in[:, 2048:2080], kfbf[:, :], waits=(t2,),
                     new_group=False)
        gset("scr0", tci1)

        def do_cc(e):
            return e.collective_compute(
                "AllGather", ALU.bypass, replica_groups=[list(range(8))],
                ins=[cc_in.ap().opt()], outs=[cc_out.ap().opt()]).then_inc(cc_sem, 1)
        k.op("gpsimd", do_cc, waits=(tci1, tci2), inc=0)
        t_cc = None

        _ps6g.__exit__(None, None, None)
        _pt2g = nc.psum_tensor("pt2", [128, 1024], BF16)
        ptst["pt2"] = _pt2g.__enter__()
        ptst["w"] = t_kfps


        pool.free("kft", "kfc", "vex", "kva")
        QFTT = pool.alloc("qftt", 16)
        QFC = pool.alloc("qfc", 2)
        last_kv_mm = k.tok("tensor")
        wq_t = []
        for dc in range(8):
            wq_t.append(k.dma("gpsimd", wk_ap(dc),
                              din["aWq"][dc * 128:(dc + 1) * 128, :],
                              waits=(last_kv_mm,), new_group=(dc == 0)))
        wq_t = [wq_t[-1]] * 8
        t_bq = k.dma("gpsimd", rowA[:, :], din["abq"][0:1, :],
                     waits=(last_kv_mm,))
        sta = {c_: a_stats(c_, din["xq"]) for c_ in range(3)}
        stq = {0: a_tr(0, sta.pop(0))}
        pend_q = []
        for ch in range(NCH):
            par = ch % 2
            csl = ch % 4
            if ch + 3 < NCH:
                sta[ch + 3] = a_stats(ch + 3, din["xq"])
            tch_ap, t_tr = stq.pop(ch)
            tmmq = [None, None]
            for hf in range(2):
                for dc in range(8):
                    w = [t_tr, wq_t[dc]] + ([t_projps[hf]] if dc == 0 else [])
                    k.op("tensor", lambda e, hf=hf, dc=dc, a=tch_ap: e.matmul(
                        ps[4 + hf][:, :], a[:, dc * 128:(dc + 1) * 128],
                        wk_ap(dc)[:, hf * 512:(hf + 1) * 512],
                        start=(dc == 0), stop=False), waits=w)
                tmmq[hf] = k.op("tensor", lambda e, hf=hf: e.matmul(
                    ps[4 + hf][:, :], onec_bf[0:1, :],
                    rowA[0:1, hf * 512:(hf + 1) * 512],
                    start=False, stop=True), waits=(t_bq, t_ones), inc=1)
            if pend_q:
                qsrc, pch, tq = pend_q.pop(0)
                t_trq = transpose_chunk(qsrc.ap(), 16, None, (tq,),
                                        dst_contig=QFTT[pch].ap())
                gset(f"qfc{pch % 2}", t_trq)
            eq = f32v(SCR[par])
            t_exp = None
            for hf in range(2):
                t_exp = k.op("scalar", lambda e, hf=hf, d=eq: e.activation(
                    d[:, hf * 512:(hf + 1) * 512], ps[4 + hf][:, :], AF.Exp),
                    waits=(tmmq[hf], gw(f"scr{par}") if hf == 0 else None),
                    inc=1)
            tevq = t_exp
            t_projps = [tevq, tevq]
            gset(f"tch{ch % 4}", tevq)
            hs = hsum[:, par * 16:(par + 1) * 16]
            hr = hrcp[:, par * 16:(par + 1) * 16]
            t_hsr = k.op("vector", lambda e, s=eq, d=hs: e.reduce_sum(
                d.rearrange("p (g o) -> p g o", o=1),
                s.rearrange("p (g i) -> p g i", g=16), axis=AX.X),
                waits=(t_exp, gw(f"hs{par}")), inc=1)
            t_hr = k.op("vector", lambda e, s=hs, d=hr: e.reciprocal(d, s),
                        waits=(t_hsr,), inc=1)
            qf_t = QFC[par]
            sin_c = sccq_sb[:, ch:ch + 1]
            cos_c = sccq_sb[:, 16 + ch:16 + ch + 1]
            t_qft = None
            for half, colp in ((0, sin_c), (1, cos_c)):
                hsc = hsum[:, 32 + par * 32 + half * 16:32 + par * 32 + (half + 1) * 16]
                t_s2 = k.op("vector", lambda e, d=hsc, c=colp, hr=hr: e.tensor_scalar(
                    d, hr, c, None, op0=ALU.mult),
                    waits=(t_hr, gw(f"qfc{par}")), inc=1)
                t_qft = k.op("vector", lambda e, s=eq, d=qf_t, half=half,
                             hv=hsc.unsqueeze(2).broadcast_to([128, 16, 64])
                             : e.tensor_tensor(
                                 d.ap().rearrange("p (h j) -> p h j", j=128)
                                 [:, :, half * 64:(half + 1) * 64],
                                 s.rearrange("p (h j) -> p h j", j=64),
                                 hv, op=ALU.mult), waits=(t_s2,), inc=1)
            gset(f"scr{par}", t_qft)
            gset(f"hs{par}", t_qft)
            pend_q.append((qf_t, ch, t_qft))
            if ch + 1 < NCH:
                stq[ch + 1] = a_tr(ch + 1, sta.pop(ch + 1))
        while pend_q:
            qsrc, pch, tq = pend_q.pop(0)
            t_trq = transpose_chunk(qsrc.ap(), 16, None, (tq,),
                                    dst_contig=QFTT[pch].ap())
            gset(f"qfc{pch % 2}", t_trq)
        t_qftT_done = k.tok("scalar")

        pool.free("tch", "lnc", "qfc", "cs")
        KVP = pool.alloc("kvp", 8)
        k.q["sync"].append(lambda eng: eng.wait_ge(cc_sem, 1))
        t_a3_all = (k.tok("tensor"), k.tok("vector"), k.tok("scalar"))
        t_kvall = []
        for b in range(8):
            t_kvall.append(k.dma("sync", KVP[b].ap(),
                                 cc_out[b * 128:(b + 1) * 128, 0:2048],
                                 waits=t_a3_all, new_group=(b == 0)))
        t_kvf = []
        for b in range(8):
            t_kvf.append(k.dma("sync", kvfs[:, b * 32:(b + 1) * 32],
                               cc_out[b * 128:(b + 1) * 128, 2048:2080],
                               waits=t_a3_all, new_group=False))
        t_all16 = t_kvf[-1]
        t_kvall = [t_all16] * 8
        t_kvf = [t_all16] * 8

        tap("kvall0", KVP[0].ap(), t_kvall[0])
        tap("qftt0", QFTT[0].ap(), t_qftT_done if False else k.tok("vector"))

        k.barrier()
        pool.free("bc", "wn", "lsce")
        SCB = pool.alloc("scb", 1)
        CMP = pool.alloc("cmp", 2)
        ATN = pool.alloc("atn", 8)

        t_scb = None
        t_ps4 = t_projps[0]
        for blk in range(4):
            tm = None
            for half, row in ((0, srow_sin), (1, srow_cos)):
                tm = k.op("tensor", lambda e, blk=blk, half=half, row=row: e.matmul(
                    ps[4][half * 64:(half + 1) * 64, :], onec_bf[0:1, 0:64],
                    row[0:1, blk * 512:(blk + 1) * 512], start=True, stop=True),
                    waits=(t_srs, t_src, t_ones, t_ps4), inc=(half == 1))
            t_scb = k.op("vector", lambda e, blk=blk: e.tensor_copy(
                SCB[0][:, blk * 512:(blk + 1) * 512], ps[4][:, :]),
                waits=(tm,), inc=1)
            t_ps4 = t_scb

        t_mainps = [None, None]
        t_denps = None
        t_atn = None
        for hg in range(2):
            for ch in range(NCH):
                b0 = ch // 2
                rot = ch % 2
                pm, pc_ = ps[rot], ps[2 + rot]
                cmp_ap = CMP[rot].ap()
                t_cmp = k.op("vector", lambda e, ch=ch, hg=hg, d=cmp_ap:
                             e.tensor_tensor(
                    d[:, 0:1024].rearrange("p (h j) -> p h j", j=128),
                    SCB[0][:, ch * 128:(ch + 1) * 128].unsqueeze(1)
                    .broadcast_to([128, 8, 128]),
                    QFTT[ch][:, hg * 1024:(hg + 1) * 1024]
                    .rearrange("p (h j) -> p h j", j=128),
                    op=ALU.subtract),
                    waits=(t_scb, t_qftT_done, gw(f"cmp{rot}")), inc=1)
                last_mm = None
                for hh in range(8):
                    h = hg * 8 + hh
                    lhs_m = QFTT[ch][:, h * 128:(h + 1) * 128]
                    lhs_c = cmp_ap[:, hh * 128:(hh + 1) * 128]
                    for br, (lhs, pb) in enumerate(((lhs_m, pm), (lhs_c, pc_))):
                        t = h * 2 + br
                        w = [t_cmp, t_kvall[b0], t_kvf[b0]]
                        if hh == 0 and br == 0:
                            w += [t_mainps[rot], t_denps]
                        k.op("tensor", lambda e, t=t, lhs=lhs, pb=pb, hh=hh, b0=b0:
                             e.matmul(pb[:, hh * 64:(hh + 1) * 64], lhs,
                                      KVP[b0][:, t * 64:(t + 1) * 64],
                                      start=True, stop=True), waits=w)
                        last_mm = k.op("tensor",
                                       lambda e, t=t, lhs=lhs, hh=hh, b0=b0, br=br:
                                       e.matmul(ps[4][:, hh * 2 + br:hh * 2 + br + 1],
                                                lhs,
                                                kvfs[:, b0 * 32 + t:b0 * 32 + t + 1],
                                                start=True, stop=True),
                                       inc=(1 if (hh == 7 and br == 1) else 0))
                gset(f"cmp{rot}", last_mm)
                dc_ = dencol[:, rot * 16:(rot + 1) * 16]
                zc_ = zcol[:, rot * 16:(rot + 1) * 16]
                t_dn = k.op("vector", lambda e, d=dc_: e.tensor_scalar(
                    d, ps[4][:, 0:16], EPS_Z, None, op0=ALU.max),
                    waits=(last_mm, gw(f"z{rot}")), inc=1)
                t_denps = k.op("vector", lambda e, d=dc_, z=zc_: e.reciprocal(z, d),
                               waits=(t_dn,), inc=1)
                u = f32v(SCR[rot])
                z1v = zc_[:, 0:16:2].unsqueeze(2).broadcast_to([128, 8, 64])
                z2v = zc_[:, 1:16:2].unsqueeze(2).broadcast_to([128, 8, 64])
                k.op("vector", lambda e, u=u, pc_=pc_, z2v=z2v: e.tensor_tensor(
                    u[:, 0:512].rearrange("p (h j) -> p h j", j=64),
                    pc_[:, :].rearrange("p (h j) -> p h j", j=64), z2v, op=ALU.mult),
                    waits=(gw(f"scr{rot}"), t_denps))
                t_u = k.op("vector", lambda e, u=u, pm=pm, z1v=z1v: e.tensor_tensor(
                    u[:, 512:1024].rearrange("p (h j) -> p h j", j=64),
                    pm[:, :].rearrange("p (h j) -> p h j", j=64), z1v, op=ALU.mult),
                    inc=1)
                t_mainps[rot] = t_u
                gset(f"z{rot}", t_u)
                piece = LSC[rot][:, 0:512]
                t_ab = k.op("vector", lambda e, u=u, piece=piece: e.tensor_add(
                    piece, u[:, 512:1024], u[:, 0:512]),
                    waits=(gw(f"lsc{rot}"), t_u), inc=1)
                gset(f"scr{rot}", t_ab)
                t_atn = transpose_chunk(
                    piece, 4,
                    lambda i, hg=hg, ch=ch: ATN[hg * 4 + i][:, ch * 128:(ch + 1) * 128],
                    (t_ab,))
                gset(f"lsc{rot}", t_atn)

        tap("atn0", ATN[0].ap(), t_atn)

        k.barrier()
        pool.free("qftt", "scb", "cmp", "kvp", "tch")
        BIG16 = pool.alloc("big16", 16)
        CS = pool.alloc("cs", 2)
        WN2 = pool.alloc("wn2", 4)
        BC = pool.alloc("bc", 1)
        NXT = pool.alloc("nxt", 8)
        LNC = pool.alloc("lnc", 1)
        WKV = pool.alloc("wkv", 1)
        BCv = [f32v(BC[0])]
        for key in list(G):
            if key.startswith("cs") or key.startswith("bc") or key.startswith("lnc"):
                G.pop(key)
        wo_ap = lambda ac: WN2[ac // 2][:, (ac % 2) * 1024:(ac % 2 + 1) * 1024]
        wo_t = []
        for ac in range(8):
            wo_t.append(k.dma("gpsimd", wo_ap(ac),
                              din["aWo"][ac * 128:(ac + 1) * 128, :],
                              new_group=(ac == 0)))
        wo_t = [wo_t[-1]] * 8
        t_bo = k.dma("gpsimd", rowA[:, :], din["abo"][0:1, :])
        prep_done = {}

        def prep_tok(row0, rows):
            best = None
            for r, (t, n) in prep_done.items():
                if row0 < r + n and r < row0 + rows:
                    if best is None or t[1] > best[1]:
                        best = t
            return best

        early_prep = [
            (din["fWkc"][:, :], R_FWKC, 256),
            (din["fWvc"][:, :], R_FWVC, 256),
            (din["fWq"][0:256, :], R_FWQ, 256),
            (din["fWq"][256:512, :], R_FWQ + 256, 256),
            (din["fWq"][512:768, :], R_FWQ + 512, 256),
            (din["fWq"][768:1024, :], R_FWQ + 768, 256),
        ]
        t_g1 = bias_bcast(BCv[0], din["g1"][0:1, :])
        t_projps = [t_g1, t_g1]
        t_nxt = None
        pend_x = []
        for ch in range(NCH):
            par = ch % 2
            csl = ch % 2
            half_ap = LNC[0][:, par * 1024:(par + 1) * 1024]
            t_xr = k.dma("gpsimd", half_ap, din["xr"][ch * 128:(ch + 1) * 128, :],
                         waits=(gw(f"half{par}"),))
            tmmo = [None, None]
            for hf in range(2):
                for ac in range(8):
                    w = [t_atn, wo_t[ac]] + ([t_projps[hf]] if ac == 0 else [])
                    k.op("tensor", lambda e, hf=hf, ac=ac, ch=ch: e.matmul(
                        ps[4 + hf][:, :], ATN[ac][:, ch * 128:(ch + 1) * 128],
                        wo_ap(ac)[:, hf * 512:(hf + 1) * 512],
                        start=(ac == 0), stop=False), waits=w)
                tmmo[hf] = k.op("tensor", lambda e, hf=hf: e.matmul(
                    ps[4 + hf][:, :], onec_bf[0:1, :],
                    rowA[0:1, hf * 512:(hf + 1) * 512],
                    start=False, stop=True), waits=(t_bo, t_ones), inc=1)
            if pend_x:
                ha, pch, tnx_p = pend_x.pop(0)
                t_nxt = transpose_chunk(ha, 8,
                                        lambda i, pch=pch: NXT[i][:, pch * 128:
                                                                  (pch + 1) * 128],
                                        (tnx_p,))
                gset(f"half{pch % 2}", t_nxt)
            oc_ = f32v(CS[csl])
            tev = None
            for hf in range(2):
                tev = k.op("vector", lambda e, hf=hf, d=oc_: e.tensor_copy(
                    d[:, hf * 512:(hf + 1) * 512], ps[4 + hf][:, :]),
                    waits=(tmmo[hf], gw(f"cs{csl}")), inc=(hf == 1))
            t_projps = [tev, tev]
            mean, rstd, tst = ln_stats(oc_, EPS_LN, par, (tev,))
            nrm = f32v(SCR[par])
            t_n = k.op("scalar", lambda e, s=oc_, m=mean, r=rstd, d=nrm:
                       e.activation(d, s, AF.Identity, scale=r, bias=m),
                       waits=(tst, gw(f"scr{par}")), inc=1)
            gset(f"cs{csl}", t_n)
            gset(f"cols{par}", t_n)
            t_x2 = k.op("vector", lambda e, d=f32v(BIG16[ch]), n=nrm, x=half_ap:
                        e.tensor_add(d, n, x), waits=(t_xr, t_n), inc=1)
            gset(f"scr{par}", t_x2)
            if "x2" in taps:
                tap("x2_pc", None, None)
                t_tap = k.dma("sync", tap_d["x2"][ch * 128:(ch + 1) * 128, :],
                              f32v(BIG16[ch]), waits=(t_x2,))
            else:
                t_tap = None
            t_spd = k.dma("gpsimd", xsp[ch * 128:(ch + 1) * 128, :], f32v(BIG16[ch]),
                          waits=(t_x2,))
            if early_prep:
                esrc, erow, erows = early_prep.pop(0)
                tsave = k.dma("gpsimd", wsc[erow:erow + erows, :], esrc)
                prep_done[erow] = (tsave, erows)
            mean2, rstd2_, tst2 = ln_stats(f32v(BIG16[ch]), EPS_LN, par, (t_x2,))
            t_ln2 = k.op("vector", lambda e, s=f32v(BIG16[ch]), m=mean2, r=rstd2_:
                         e.tensor_scalar(s, s, m, r, op0=ALU.add, op1=ALU.mult),
                         waits=(tst2, t_spd, t_tap), inc=1)
            gset(f"cols{par}", t_ln2)
            var2 = cols[:, par * 12 + 4: par * 12 + 5]
            ms_ = cols[:, par * 12 + 7: par * 12 + 8]
            rstd1 = cols[:, par * 12 + 8: par * 12 + 9]
            t_ = k.op("vector", lambda e, v=var2, r=rstd2_, d=ms_: e.tensor_tensor(
                d, v, r, op=ALU.mult), waits=(tst2,), inc=1)
            t_ = k.op("vector", lambda e, r=rstd2_, d=ms_: e.tensor_tensor(
                d, d, r, op=ALU.mult), waits=(t_,), inc=1)
            t_ = k.op("scalar", lambda e, d=rstd1, s=ms_: e.activation(
                d, s, AF.Ln, bias=cst[:, 1:2]), waits=(t_, t_cst), inc=1)
            trs = k.op("scalar", lambda e, d=rstd1: e.activation(
                d, d, AF.Exp, scale=-0.5), waits=(t_,), inc=1)
            scr = f32v(SCR[par])
            t_sc = k.op("vector", lambda e, s=f32v(BIG16[ch]), r=rstd1, d=scr:
                        e.tensor_scalar(d, s, r, None, op0=ALU.mult),
                        waits=(trs, t_ln2, gw(f"scr{par}")), inc=1)
            t_nx = k.op("vector", lambda e, s=scr, d=half_ap, b=BCv[0]: e.tensor_tensor(
                d, s, b, op=ALU.mult), waits=(t_g1, t_x2, t_sc), inc=1)
            gset(f"scr{par}", t_nx)
            gset(f"cols{par}", t_nx)
            pend_x.append((half_ap, ch, t_nx))
        while pend_x:
            ha, pch, tnx_p = pend_x.pop(0)
            t_nxt = transpose_chunk(ha, 8,
                                    lambda i, pch=pch: NXT[i][:, pch * 128:
                                                              (pch + 1) * 128],
                                    (tnx_p,))
            gset(f"half{pch % 2}", t_nxt)
        wkct = []
        for dc in range(8):
            wkct.append(k.dmat("sync", WKV[0][:, dc * 256:(dc + 1) * 256],
                               wsc[R_FWKC:R_FWKC + 256, dc * 128:(dc + 1) * 128],
                               waits=(prep_tok(R_FWKC, 256),), new_group=(dc == 0)))
        wkct = [wkct[-1]] * 8
        if "xff" in taps:
            pass

        late_jobs = [(din["fWo"][:, :], R_FWO, DM), (din["sW1"][:, :], R_SW1, DM),
                     (din["sW2"][:, :], R_SW2, DM), (din["sW3"][:, :], R_SW3, DM)]
        for e_i in range(NE):
            for nm, base in (("eW1", R_EW1), ("eW2", R_EW2), ("eW3", R_EW3)):
                late_jobs.append((din[nm][e_i, :, :], base + e_i * 1024, DM))
        for src, row, rows in late_jobs:
            tsave = k.dma("gpsimd", wsc[row:row + rows, :], src, dom="prep")
            prep_done[row] = (tsave, rows)

        k.barrier()
        pool.free("atn", "wn2", "cs", "bc", "lnc")
        KCT = pool.alloc("kct", 2)
        VX2 = pool.alloc("vx2", 4)
        QT = pool.alloc("qt", 8)
        WKV2 = pool.alloc("wkv2", 1)
        WQR = pool.alloc("wqr", 1)
        wvct = []
        for dc in range(8):
            wvct.append(k.dmat("sync", WKV2[0][:, dc * 256:(dc + 1) * 256],
                               wsc[R_FWVC:R_FWVC + 256, dc * 128:(dc + 1) * 128],
                               waits=(prep_tok(R_FWVC, 256),),
                               new_group=(dc == 0)))
        wvct = [wvct[-1]] * 8
        wq_pre = {}
        for oc in (0, 1):
            wdst_ = WQR[0][:, (oc % 2) * 1024:(oc % 2 + 1) * 1024]
            tq_ = []
            for dc in range(8):
                tq_.append(k.dmat("sync", wdst_[:, dc * 128:(dc + 1) * 128],
                                  wsc[R_FWQ + oc * 128:R_FWQ + (oc + 1) * 128,
                                      dc * 128:(dc + 1) * 128],
                                  waits=(prep_tok(R_FWQ + oc * 128, 128),),
                                  new_group=(dc == 0)))
            wq_pre[oc] = [tq_[-1]] * 8
        t_bps = [t_projps[0], t_projps[1]]
        for kc2 in range(2):
            for sb in range(4):
                rot = sb % 2
                last = None
                for dc in range(8):
                    w = [t_nxt, wkct[dc]] + ([t_bps[rot]] if dc == 0 else [])
                    last = k.op("tensor", lambda e, kc2=kc2, sb=sb, dc=dc, rot=rot:
                                e.matmul(ps[rot][:, :],
                                         WKV[0][:, dc * 256 + kc2 * 128:
                                                dc * 256 + (kc2 + 1) * 128],
                                         NXT[dc][:, sb * 512:(sb + 1) * 512],
                                         start=(dc == 0), stop=(dc == 7)),
                                waits=w, inc=(dc == 7))
                t_bps[rot] = k.op("vector", lambda e, kc2=kc2, sb=sb, rot=rot:
                                  e.tensor_copy(KCT[kc2][:, sb * 512:(sb + 1) * 512],
                                                ps[rot][:, :]),
                                  waits=(last,), inc=1)
        for ch in range(NCH):
            rot = ch % 2
            vx = VX2[ch // 4][:, (ch % 4) * 512:(ch % 4) * 512 + 257]
            last = None
            for dc in range(8):
                w = [t_nxt, wvct[dc]] + ([t_bps[rot]] if dc == 0 else [])
                last = k.op("tensor", lambda e, ch=ch, dc=dc, rot=rot: e.matmul(
                    ps[rot][:, 0:256], NXT[dc][:, ch * 128:(ch + 1) * 128],
                    WKV2[0][:, dc * 256:(dc + 1) * 256],
                    start=(dc == 0), stop=(dc == 7)), waits=w, inc=(dc == 7))
            k.op("vector", lambda e, vx=vx: e.memset(vx[:, 256:257], 1.0))
            t_bps[rot] = k.op("vector", lambda e, vx=vx, rot=rot: e.tensor_copy(
                vx[:, 0:256], ps[rot][:, 0:256]), waits=(last,), inc=1)
        for oc in range(8):
            wpar = oc % 2
            wdst = WQR[0][:, wpar * 1024:(wpar + 1) * 1024]
            if oc in wq_pre:
                tq = wq_pre[oc]
            else:
                tq = []
                for dc in range(8):
                    tq.append(k.dmat("sync", wdst[:, dc * 128:(dc + 1) * 128],
                                     wsc[R_FWQ + oc * 128:R_FWQ + (oc + 1) * 128,
                                         dc * 128:(dc + 1) * 128],
                                     waits=(prep_tok(R_FWQ + oc * 128, 128),
                                            gw(f"wqr{wpar}")),
                                     new_group=(dc == 0)))
                tq = [tq[-1]] * 8
            for sb in range(4):
                rot = sb % 2
                last = None
                for dc in range(8):
                    w = [t_nxt, tq[dc]] + ([t_bps[rot]] if dc == 0 else [])
                    last = k.op("tensor",
                                lambda e, sb=sb, dc=dc, rot=rot, wdst=wdst: e.matmul(
                                    ps[rot][:, :], wdst[:, dc * 128:(dc + 1) * 128],
                                    NXT[dc][:, sb * 512:(sb + 1) * 512],
                                    start=(dc == 0), stop=(dc == 7)),
                                waits=w, inc=(dc == 7))
                t_bps[rot] = k.op("vector", lambda e, oc=oc, sb=sb, rot=rot:
                                  e.tensor_copy(QT[oc][:, sb * 512:(sb + 1) * 512],
                                                ps[rot][:, :]),
                                  waits=(last,), inc=1)
                gset(f"wqr{wpar}", t_bps[rot])
        t_b1 = k.tok("vector")

        k.barrier()
        pool.free("nxt", "wkv", "wkv2", "wqr", "scr")
        PTL = pool.alloc("ptl", 4)
        A2T = pool.alloc("a2t", 8)
        t_scps = [None, None]
        t_avps = [None, None]
        t_ptl_guard = None
        t_a2t = None
        for h in range(H2):
            for sb in range(4):
                t_pexp = [None] * 16
                for tc in range(16):
                    rot = tc % 2
                    last = None
                    for kc2 in range(2):
                        w = [t_b1] + ([t_scps[rot], t_ptl_guard] if kc2 == 0 else [])
                        last = k.op("tensor",
                                    lambda e, h=h, sb=sb, tc=tc, kc2=kc2, rot=rot:
                                    e.matmul(ps[rot][:, :],
                                             KCT[kc2][:, tc * 128:(tc + 1) * 128],
                                             QT[2 * h + kc2][:, sb * 512:(sb + 1) * 512],
                                             start=(kc2 == 0), stop=(kc2 == 1)),
                                    waits=w, inc=(kc2 == 1))
                    dst = PTL[tc // 4][:, (tc % 4) * 512:(tc % 4 + 1) * 512]
                    t_pexp[tc] = k.op("scalar", lambda e, dst=dst, rot=rot: e.activation(
                        dst, ps[rot][:, :], AF.Exp, scale=0.0625),
                        waits=(last,), inc=1)
                    t_scps[rot] = t_pexp[tc]
                for ss in range(4):
                    ch = sb * 4 + ss
                    rot = ss % 2
                    last = None
                    for tc in range(16):
                        w = [t_pexp[tc]] + ([t_avps[rot]] if tc == 0 else [])
                        last = k.op("tensor", lambda e, ss=ss, tc=tc, rot=rot: e.matmul(
                            ps[2 + rot][:, 0:257],
                            PTL[tc // 4][:, (tc % 4) * 512 + ss * 128:
                                         (tc % 4) * 512 + (ss + 1) * 128],
                            VX2[tc // 4][:, (tc % 4) * 512:(tc % 4) * 512 + 257],
                            start=(tc == 0), stop=(tc == 15)), waits=w, inc=(tc == 15))
                    zz = avz[:, rot:rot + 1]
                    t_zz = k.op("vector", lambda e, zz=zz, rot=rot: e.reciprocal(
                        zz, ps[2 + rot][:, 256:257]),
                        waits=(last, gw(f"avz{rot}")), inc=1)
                    piece = LSC[rot][:, 0:256]
                    t_pc = k.op("vector", lambda e, piece=piece, zz=zz, rot=rot:
                                e.tensor_scalar(piece, ps[2 + rot][:, 0:256],
                                                zz, None, op0=ALU.mult),
                                waits=(gw(f"lsc{rot}"), t_zz), inc=1)
                    t_avps[rot] = t_pc
                    gset(f"avz{rot}", t_pc)
                    t_a2t = transpose_chunk(
                        piece, 2, None, (t_pc,),
                        dst_contig=A2T[ch // 2][:, (ch % 2) * 1024 + 2 * h * 128:
                                               (ch % 2) * 1024 + (2 * h + 2) * 128])
                    gset(f"lsc{rot}", t_a2t)
            t_ptl_guard = t_a2t

        k.barrier()
        pool.free("kct", "vx2", "qt", "ptl")
        WOT = pool.alloc("wot", 4)
        NX2T = pool.alloc("nx2t", 8)
        LNC = pool.alloc("lnc2", 1)
        SCR = pool.alloc("scr", 2)
        BC = pool.alloc("bc", 2)
        BCv = [f32v(BC[0]), f32v(BC[1])]
        for key_ in list(G):
            if key_.startswith("scr") or key_.startswith("bc"):
                G.pop(key_)
        wot_ap = lambda ac: WOT[ac // 2][:, (ac % 2) * 1024:(ac % 2 + 1) * 1024]
        wot_t = []
        for ac in range(8):
            wot_t.append(k.dmat("sync", wot_ap(ac),
                                wsc[R_FWO:R_FWO + 1024, ac * 128:(ac + 1) * 128],
                                waits=(prep_tok(R_FWO, 1024),), new_group=(ac == 0)))
        wot_t = [wot_t[-1]] * 8
        t_g2 = bias_bcast(BCv[1], din["g2"][0:1, :])
        tl = k.dma("sync", brow[:, 0:8], din["febias"][0:1, :], waits=(gw("brow"),))
        tm = k.op("tensor", lambda e: e.matmul(ps[4][:, 0:8], onec_f[0:1, :],
                                               brow[0:1, 0:8], start=True, stop=True),
                  waits=(tl, t_g2), inc=1)
        t_eb = k.op("vector", lambda e: e.tensor_copy(eb_bc[:, :], ps[4][:, 0:8]),
                    waits=(tm,), inc=1)
        gset("brow", t_eb)
        t_wr = k.dma("gpsimd", wrbf[:, :], din["fWr"][:, :])
        t_wrt = None
        for dc in range(8):
            tmw = k.op("tensor", lambda e, dc=dc: e.transpose(
                pt[:, 0:8], wrbf[:, dc * 128:(dc + 1) * 128], ident[0:8, 0:8]),
                waits=(t_wr, t_id, gw("pt0")), inc=1)
            t_wrt = k.op("vector", lambda e, dc=dc: e.tensor_copy(
                wrt[:, dc * 8:(dc + 1) * 8], pt[:, 0:8]), waits=(tmw,), inc=1)
            gset("pt0", t_wrt)

        t_xff = [None] * NCH
        st = {"rps": t_eb}
        t_comb_ch = [None] * NCH
        t_ct = None
        pend_b = []

        def c1_tail(pch, tnx_p, nxc_p):
            par = pch % 2
            ch = pch
            t_nx = tnx_p
            t_nx2 = transpose_chunk(nxc_p, 8,
                                    lambda i, ch=ch: NX2T[i][:, ch * 128:(ch + 1) * 128],
                                    (t_nx,))
            gset(f"l2c{par}", t_nx2)
            o = par * 96
            last = None
            for dc in range(8):
                w = [t_nx2, t_wrt] + ([st["rps"]] if dc == 0 else [])
                last = k.op("tensor", lambda e, ch=ch, dc=dc: e.matmul(
                    ps[4][:, 0:8], NX2T[dc][:, ch * 128:(ch + 1) * 128],
                    wrt[:, dc * 8:(dc + 1) * 8],
                    start=(dc == 0), stop=(dc == 7)), waits=w, inc=(dc == 7))
            lg = rt[:, o:o + 8]
            rw = rt[:, o + 8:o + 16]
            i1 = rt[:, o + 16:o + 24]
            rw2 = rt[:, o + 24:o + 32]
            i2 = rt[:, o + 32:o + 40]
            t3 = rt[:, o + 40:o + 48]
            nm = rt[:, o + 80:o + 81]
            sm = rt[:, o + 81:o + 82]
            rs = rt[:, o + 82:o + 83]
            m1 = rt[:, o + 83:o + 84]
            m2 = rt[:, o + 84:o + 85]
            dd = rt[:, o + 85:o + 86]
            w1 = rt[:, o + 86:o + 87]
            w2 = rt[:, o + 87:o + 88]
            t1_ = k.op("vector", lambda e, lg=lg: e.tensor_add(lg, ps[4][:, 0:8],
                                                               eb_bc[:, :]),
                       waits=(last, t_eb, gw(f"rt{par}")), inc=1)
            st["rps"] = t1_
            t_ = k.op("vector", lambda e, lg=lg, nm=nm: e.reduce_max(
                nm, lg, axis=AX.X, negate=True), waits=(t1_,), inc=1)
            tsf = k.op("scalar", lambda e, lg=lg, nm=nm, rw=rw, sm=sm: e.activation(
                rw, lg, AF.Exp, bias=nm, accum_out=sm), waits=(t_,), inc=1)
            t_ = k.op("vector", lambda e, sm=sm, rs=rs: e.reciprocal(rs, sm),
                      waits=(tsf,), inc=1)
            t_ = k.op("vector", lambda e, rw=rw, rs=rs: e.tensor_scalar(
                rw, rw, rs, None, op0=ALU.mult), waits=(t_,), inc=1)
            t_ = k.op("vector", lambda e, rw=rw, m1=m1: e.reduce_max(
                m1, rw, axis=AX.X), waits=(t_,), inc=1)
            t_ = k.op("vector", lambda e, rw=rw, m1=m1, i1=i1: e.tensor_scalar(
                i1, rw, m1, None, op0=ALU.is_equal), waits=(t_,), inc=1)
            t_ = k.op("vector", lambda e, i1=i1, rw2=rw2: e.tensor_scalar(
                rw2, i1, -1e9, None, op0=ALU.mult), waits=(t_,), inc=1)
            t_ = k.op("vector", lambda e, rw=rw, rw2=rw2: e.tensor_add(rw2, rw2, rw),
                      waits=(t_,), inc=1)
            t_ = k.op("vector", lambda e, rw2=rw2, m2=m2: e.reduce_max(
                m2, rw2, axis=AX.X), waits=(t_,), inc=1)
            t_ = k.op("vector", lambda e, rw2=rw2, m2=m2, i2=i2: e.tensor_scalar(
                i2, rw2, m2, None, op0=ALU.is_equal), waits=(t_,), inc=1)
            t2_ = k.op("vector", lambda e, m1=m1, m2=m2, dd=dd: e.tensor_sub(dd, m2, m1),
                       waits=(t_,), inc=1)
            ted = k.op("scalar", lambda e, dd=dd: e.activation(dd, dd, AF.Exp),
                       waits=(t2_,), inc=1)
            t_ = k.op("vector", lambda e, dd=dd: e.tensor_scalar(
                dd, dd, 1.0, None, op0=ALU.add), waits=(ted,), inc=1)
            t_ = k.op("vector", lambda e, dd=dd, w1=w1: e.reciprocal(w1, dd),
                      waits=(t_,), inc=1)
            t_ = k.op("vector", lambda e, w1=w1, w2=w2: e.tensor_scalar(
                w2, w1, -1.0, 1.0, op0=ALU.mult, op1=ALU.add), waits=(t_,), inc=1)
            cc_ = combine[:, ch * 8:(ch + 1) * 8]
            t_ = k.op("vector", lambda e, i1=i1, w1=w1, cc_=cc_: e.tensor_scalar(
                cc_, i1, w1, None, op0=ALU.mult), waits=(t_,), inc=1)
            t_ = k.op("vector", lambda e, i2=i2, w2=w2, t3=t3: e.tensor_scalar(
                t3, i2, w2, None, op0=ALU.mult), waits=(t_,), inc=1)
            t_comb = k.op("vector", lambda e, cc_=cc_, t3=t3: e.tensor_add(cc_, cc_, t3),
                          waits=(t_,), inc=1)
            gset(f"rt{par}", t_comb)
            t_comb_ch[ch] = t_comb

        for ch in range(NCH):
            par = ch % 2
            tofs = []
            for hf in range(2):
                rot = hf
                last = None
                for ac in range(8):
                    w = [t_a2t, wot_t[ac]] + ([t_bps[rot]] if ac == 0 else [])
                    last = k.op("tensor", lambda e, ch=ch, hf=hf, ac=ac, rot=rot:
                                e.matmul(ps[rot][:, :],
                                         A2T[ch // 2][:, (ch % 2) * 1024 + ac * 128:
                                                      (ch % 2) * 1024 + (ac + 1) * 128],
                                         wot_ap(ac)[:, hf * 512:(hf + 1) * 512],
                                         start=(ac == 0), stop=(ac == 7)),
                                waits=w, inc=(ac == 7))
                t_bps[rot] = k.op("vector", lambda e, ch=ch, hf=hf, rot=rot:
                                  e.tensor_add(
                                      f32v(BIG16[ch])[:, hf * 512:(hf + 1) * 512],
                                      f32v(BIG16[ch])[:, hf * 512:(hf + 1) * 512],
                                      ps[rot][:, :]),
                                  waits=(last,), inc=1)
                tofs.append(t_bps[rot])
            t_xff[ch] = tofs[1]
            if "xff" in taps:
                k.dma("sync", tap_d["xff"][ch * 128:(ch + 1) * 128, :], f32v(BIG16[ch]),
                      waits=(tofs[0], tofs[1]))
            if pend_b:
                c1_tail(*pend_b.pop(0))
            bsrc = f32v(BIG16[ch])
            rstd2, trs = rms_stats(bsrc, EPS_RMS, par, (t_xff[ch],))
            scr = f32v(SCR[par])
            t_sc = k.op("vector", lambda e, s=bsrc, r=rstd2, d=scr: e.tensor_scalar(
                d, s, r, None, op0=ALU.mult), waits=(trs, gw(f"scr{par}")), inc=1)
            nxc = LNC[0][:, par * 1024:(par + 1) * 1024]
            t_nx = k.op("vector", lambda e, s=scr, d=nxc, b=BCv[1]: e.tensor_tensor(
                d, s, b, op=ALU.mult), waits=(t_g2, gw(f"l2c{par}"), t_sc), inc=1)
            gset(f"scr{par}", t_nx)
            gset(f"cols{par}", t_nx)
            pend_b.append((ch, t_nx, nxc))
        while pend_b:
            c1_tail(*pend_b.pop(0))
        t_nx2 = k.tok("vector")
        if "combine" in taps:
            k.dma("sync", tap_d["combine"][:, :], combine[:, :],
                  waits=(t_comb_ch[NCH - 1],))

        k.barrier()
        pool.free("lnc2", "bc", "scr", "lsc", "a2t", "wot")
        W1TA = pool.alloc("w1ta", 4)
        W1TB = pool.alloc("w1tb", 4)
        W3T = pool.alloc("w3t", 4)
        W2T = pool.alloc("w2t", 4)
        SG = pool.alloc("sg", 2)
        HP = pool.alloc("hp", 2)
        TMP = pool.alloc("tmp", 1)
        tmpv = f32v(TMP[0])
        w_ap = lambda W, dc: W[dc // 2][:, (dc % 2) * 1024:(dc % 2 + 1) * 1024]

        t_wmm_guard = None
        t_w1_guard = None
        t_w3_guard = None
        t_w2_guard = None
        w1_pref = None
        t_cb = None
        t_gps = [None, None]
        t_ups = [None, None]
        t_eops = [st["rps"], st["rps"]]
        t_addg = [None, None]
        t_sg_guard = None
        t_hp_guard = None
        last_eo = None
        for w_i in range(9):
            shared = (w_i == 0)
            if shared:
                rows1, rows2, rows3 = R_SW1, R_SW2, R_SW3
            else:
                e_i = w_i - 1
                rows1 = R_EW1 + e_i * 1024
                rows2 = R_EW2 + e_i * 1024
                rows3 = R_EW3 + e_i * 1024
            W1T = W1TA if w_i % 2 == 0 else W1TB
            wt = {}
            if w_i == 0:
                tt_ = None
                for dc in range(8):
                    tt_ = k.dmat("sync", w_ap(W1T, dc),
                                 wsc[rows1:rows1 + 1024, dc * 128:(dc + 1) * 128],
                                 waits=(prep_tok(rows1, 1024),), new_group=(dc == 0))
                wt["w1"] = [tt_] * 8
            else:
                wt["w1"] = w1_pref
            for nm_, rows, W, grd in (("w3", rows3, W3T, t_w3_guard),
                                      ("w2", rows2, W2T, t_w2_guard)):
                tt_ = None
                for dc in range(8):
                    tt_ = k.dmat("sync", w_ap(W, dc),
                                 wsc[rows:rows + 1024, dc * 128:(dc + 1) * 128],
                                 waits=(prep_tok(rows, 1024), grd),
                                 new_group=(dc == 0))
                wt[nm_] = [tt_] * 8
            if w_i < 8:
                nrows1 = R_EW1 + w_i * 1024
                WN1 = W1TB if w_i % 2 == 0 else W1TA
                tt_ = None
                for dc in range(8):
                    tt_ = k.dmat(
                        "sync", w_ap(WN1, dc),
                        wsc[nrows1:nrows1 + 1024, dc * 128:(dc + 1) * 128],
                        waits=(prep_tok(nrows1, 1024), t_w1_guard),
                        new_group=(dc == 0))
                w1_pref = [tt_] * 8
            last_g_mm = None
            last_u_mm = None
            last_eo_mm = None
            for tb in range(4):
                t_hp = [None] * 8
                for fc in range(8):
                    rot = fc % 2
                    lastg = None
                    for dc in range(8):
                        w = [t_nx2, wt["w1"][dc]] + ([t_gps[rot]] if dc == 0 else [])
                        lastg = k.op("tensor",
                                     lambda e, fc=fc, dc=dc, tb=tb, rot=rot,
                                     wap=w_ap(W1T, dc): e.matmul(
                                         ps[rot][:, :],
                                         wap[:, fc * 128:(fc + 1) * 128],
                                         NX2T[dc][:, tb * 512:(tb + 1) * 512],
                                         start=(dc == 0), stop=(dc == 7)),
                                     waits=w, inc=(dc == 7))
                    last_g_mm = lastg
                    sg_ap = SG[fc // 4][:, (fc % 4) * 512:(fc % 4 + 1) * 512]
                    t_sg = k.op("scalar", lambda e, sg_ap=sg_ap, rot=rot: e.activation(
                        sg_ap, ps[rot][:, :], AF.Silu),
                        waits=(lastg, t_sg_guard), inc=1)
                    t_gps[rot] = t_sg
                    lastu = None
                    for dc in range(8):
                        w = [t_nx2, wt["w3"][dc]] + ([t_ups[rot]] if dc == 0 else [])
                        lastu = k.op("tensor",
                                     lambda e, fc=fc, dc=dc, tb=tb, rot=rot: e.matmul(
                                         ps[2 + rot][:, :],
                                         w_ap(W3T, dc)[:, fc * 128:(fc + 1) * 128],
                                         NX2T[dc][:, tb * 512:(tb + 1) * 512],
                                         start=(dc == 0), stop=(dc == 7)),
                                     waits=w, inc=(dc == 7))
                    last_u_mm = lastu
                    hp_ap = HP[fc // 4][:, (fc % 4) * 512:(fc % 4 + 1) * 512]
                    t_u = k.op("vector",
                               lambda e, hp_ap=hp_ap, sg_ap=sg_ap, rot=rot:
                               e.tensor_tensor(hp_ap, sg_ap, ps[2 + rot][:, :],
                                               op=ALU.mult),
                               waits=(lastu, t_sg, t_hp_guard), inc=1)
                    t_ups[rot] = t_u
                    t_hp[fc] = t_u
                for ss in range(4):
                    ch = tb * 4 + ss
                    for hf in range(2):
                        rot = (ss * 2 + hf) % 2
                        last = None
                        for fc in range(8):
                            w = [t_hp[fc], wt["w2"][fc]] + \
                                ([t_eops[rot]] if fc == 0 else [])
                            last = k.op("tensor",
                                        lambda e, ss=ss, hf=hf, fc=fc, rot=rot:
                                        e.matmul(
                                            ps[4 + rot][:, :],
                                            HP[fc // 4][:, (fc % 4) * 512 + ss * 128:
                                                        (fc % 4) * 512 + (ss + 1) * 128],
                                            w_ap(W2T, fc)[:, hf * 512:(hf + 1) * 512],
                                            start=(fc == 0), stop=(fc == 7)),
                                        waits=w, inc=(fc == 7))
                        last_eo_mm = last
                        if shared:
                            t_eops[rot] = k.op("vector",
                                               lambda e, ch=ch, hf=hf, rot=rot:
                                               e.tensor_add(
                                                   f32v(BIG16[ch])
                                                   [:, hf * 512:(hf + 1) * 512],
                                                   f32v(BIG16[ch])
                                                   [:, hf * 512:(hf + 1) * 512],
                                                   ps[4 + rot][:, :]),
                                               waits=(last,), inc=1)
                            t_addg[rot] = t_eops[rot]
                        else:
                            cbc = combine[:, ch * 8 + e_i:ch * 8 + e_i + 1]
                            th = tmpv[:, rot * 512:(rot + 1) * 512]
                            t_m = k.op("vector",
                                       lambda e, th=th, rot=rot, cbc=cbc:
                                       e.tensor_scalar(th, ps[4 + rot][:, :],
                                                       cbc, None, op0=ALU.mult),
                                       waits=(last, t_addg[rot]), inc=1)
                            t_eops[rot] = t_m
                            t_addg[rot] = k.op("vector",
                                               lambda e, ch=ch, hf=hf, th=th:
                                               e.tensor_add(
                                                   f32v(BIG16[ch])
                                                   [:, hf * 512:(hf + 1) * 512],
                                                   f32v(BIG16[ch])
                                                   [:, hf * 512:(hf + 1) * 512],
                                                   th), waits=(t_m,), inc=1)
                        last_eo = t_addg[rot]
                t_sg_guard = last_eo
                t_hp_guard = last_eo
            t_wmm_guard = last_eo
            t_w1_guard = last_g_mm
            t_w3_guard = last_u_mm
            t_w2_guard = last_eo_mm

        k.barrier()
        pool.free("w1ta", "w1tb", "w3t", "w2t", "sg", "hp", "tmp", "nx2t")
        SCR = pool.alloc("scr", 2)
        LSC = pool.alloc("lsc", 2)
        for key_ in list(G):
            if key_.startswith("scr") or key_.startswith("lsc"):
                G.pop(key_)
        XRL = pool.alloc("xrl", 1)
        CSO = pool.alloc("cso", 2)
        for ch in range(NCH):
            par = ch % 2
            xrl_ap = XRL[0][:, par * 1024:(par + 1) * 1024]
            t_x2l = k.dma("sync", xrl_ap, xsp[ch * 128:(ch + 1) * 128, :],
                          waits=(gw(f"xrl{par}"),))
            bsrc = f32v(BIG16[ch])
            mean, rstd, tst = ln_stats(bsrc, EPS_LN, par, (last_eo,))
            nrm = f32v(SCR[par])
            t_nr = k.op("scalar", lambda e, s=bsrc, m=mean, r=rstd, d=nrm:
                        e.activation(d, s, AF.Identity, scale=r, bias=m),
                        waits=(tst, gw(f"scr{par}")), inc=1)
            ot = f32v(CSO[par])
            t_o = k.op("vector", lambda e, n=nrm, x=xrl_ap, d=ot: e.tensor_add(d, n, x),
                       waits=(t_x2l, gw(f"cso{par}"), t_nr), inc=1)
            gset(f"xrl{par}", t_o)
            gset(f"scr{par}", t_o)
            gset(f"cols{par}", t_o)
            b0, jj = ch // 2, ch % 2
            off = (jj * 1024 + b0) * 1024
            w_ = [t_o] + ([k.lastg["sync"]] if ch == 0 else [])
            t_w = k.op("sync", lambda e, ot=ot, off=off: e.dma_start(
                out=bass.AP(out_d, off, [[8 * 1024, 128], [1, 1024]]), in_=ot),
                waits=w_, inc=16)
            k.lastg["sync"] = t_w
            gset(f"cso{par}", t_w)
        k.final_wait("sync")

        k.build(block)
    return nc


_CACHE = {}


def kernel(**inp):
    x = np.ascontiguousarray(inp["L0_attn_query"], dtype=np.float32)
    key = tuple(sorted(TAPS))
    if key not in _CACHE:
        _CACHE[key] = build_nc(tuple(TAPS))
    nc = _CACHE[key]

    idx = (np.pi / 2) * (np.arange(1, L + 1, dtype=np.float64) / L)
    sin_t = np.sin(idx).astype(np.float32)
    cos_t = np.cos(idx).astype(np.float32)

    w = {
        "aWq": inp["L0_attn_Wq"], "abq": inp["L0_attn_bq"].reshape(1, DM),
        "aWk": inp["L0_attn_Wk"], "abk": inp["L0_attn_bk"].reshape(1, DM),
        "aWv": inp["L0_attn_Wv"], "abv": inp["L0_attn_bv"].reshape(1, DM),
        "aWo": inp["L0_attn_Wo"], "abo": inp["L0_attn_bo"].reshape(1, DM),
        "g1": inp["L0_ffn_g1"].reshape(1, DM), "g2": inp["L0_ffn_g2"].reshape(1, DM),
        "fWq": inp["L0_ffn_Wq"], "fWkc": inp["L0_ffn_Wkc"], "fWvc": inp["L0_ffn_Wvc"],
        "fWo": inp["L0_ffn_Wo"], "fWr": inp["L0_ffn_Wr"],
        "febias": inp["L0_ffn_expert_bias"].reshape(1, NE),
        "sW1": inp["L0_ffn_sW1"], "sW2": inp["L0_ffn_sW2"], "sW3": inp["L0_ffn_sW3"],
        "eW1": inp["L0_ffn_eW1"], "eW2": inp["L0_ffn_eW2"], "eW3": inp["L0_ffn_eW3"],
    }
    w = {kk: np.ascontiguousarray(vv, dtype=np.float32) for kk, vv in w.items()}

    in_maps = []
    for c in range(8):
        xq = np.ascontiguousarray(x[:, c * 256:(c + 1) * 256, :].reshape(L, DM))
        xr = np.ascontiguousarray(
            x[c].reshape(256, 8, DM).transpose(1, 0, 2).reshape(L, DM))
        l0q = np.empty((128, 16), np.int64)
        l0k = np.empty((128, 16), np.int64)
        for ch in range(16):
            l0q[:, ch] = c * 256 + (ch % 2) * 128 + np.arange(128)
            l0k[:, ch] = ch * 128 + np.arange(128)
        sccq = np.concatenate([sin_t[l0q], cos_t[l0q]], axis=1).astype(np.float32)
        scck = np.concatenate([sin_t[l0k], cos_t[l0k]], axis=1).astype(np.float32)
        rq = c * 256 + (np.arange(L) % 256)
        srowq = np.stack([sin_t[rq], cos_t[rq]], axis=0).astype(np.float32)
        m = {"x": np.ascontiguousarray(x[c]), "xq": xq, "xr": xr,
             "sccq": np.ascontiguousarray(sccq), "scck": np.ascontiguousarray(scck),
             "srowq": np.ascontiguousarray(srowq)}
        m.update(w)
        in_maps.append(m)

    res = run_bass_kernel_spmd(nc, in_maps, core_ids=list(range(8)))
    out = np.stack([np.asarray(res.results[c]["out"], dtype=np.float32)
                    for c in range(8)], axis=0)
    if TAPS:
        kernel.tap_results = [{t: np.asarray(res.results[c]["tap_" + t])
                               for t in TAPS} for c in range(8)]
    return out


# version 32
# speedup vs baseline: 1.0204x; 1.0204x over previous
import numpy as np
import ml_dtypes
import concourse.bass as bass
import concourse.mybir as mybir
from concourse.bass_utils import run_bass_kernel_spmd

F32 = mybir.dt.float32
BF16 = mybir.dt.bfloat16
AF = mybir.ActivationFunctionType
ALU = mybir.AluOpType
AX = mybir.AxisListType
BFNP = ml_dtypes.bfloat16

L = 2048
DM = 1024
NCH = 16
NH = 16
H2 = 4
NE = 8
EPS_LN = 1e-5
EPS_RMS = 1e-6
EPS_Z = 1e-6
INV = 1.0 / DM

ENGS = ("sync", "gpsimd", "tensor", "vector", "scalar")

TAPS = ()


class KB:

    DOMS = ENGS + ("prep", "sdma")

    def __init__(self, nc):
        self.nc = nc
        self.sems = {e: nc.alloc_semaphore(name=f"sem_{e}") for e in self.DOMS}
        self.cnt = dict.fromkeys(self.DOMS, 0)
        self.wm = {e: dict.fromkeys(self.DOMS, 0) for e in ENGS}
        self.q = {e: [] for e in ENGS}
        self.lastg = dict.fromkeys(self.DOMS, None)

    def tok(self, e):
        return (e, self.cnt[e])

    def _wait(self, e, toks):
        for t in toks:
            if t is None:
                continue
            oe, v = t
            if v <= 0 or self.wm[e][oe] >= v:
                continue
            self.wm[e][oe] = v
            s = self.sems[oe]
            self.q[e].append(lambda eng, s=s, v=v: eng.wait_ge(s, v))

    def op(self, e, fn, waits=(), inc=0, dom=None):
        self._wait(e, waits)
        if inc:
            dom = dom or e
            self.cnt[dom] += inc
            s = self.sems[dom]
            self.q[e].append(lambda eng, fn=fn, s=s, i=inc: fn(eng).then_inc(s, i))
            return (dom, self.cnt[dom])
        self.q[e].append(fn)
        return None

    def dma(self, e, out, in_, waits=(), dom=None, new_group=True):
        dom = dom or e
        w = list(waits) + ([self.lastg[dom]] if new_group else [])
        t = self.op(e, lambda eng: eng.dma_start(out=out, in_=in_), w, inc=16, dom=dom)
        self.lastg[dom] = t
        return t

    def dmat(self, e, out, in_, waits=(), dom=None, new_group=True):
        dom = dom or e
        w = list(waits) + ([self.lastg[dom]] if new_group else [])
        t = self.op(e, lambda eng: eng.dma_start(out=out, in_=in_, transpose=True),
                    w, inc=16, dom=dom)
        self.lastg[dom] = t
        return t

    def barrier(self):
        toks = [self.tok(e) for e in ENGS]
        for e in ENGS:
            self._wait(e, toks)

    def final_wait(self, e):
        s = self.sems[e]
        v = self.cnt[e]
        self.q[e].append(lambda eng, s=s, v=v: eng.wait_ge(s, v))

    def build(self, block):
        for e in ENGS:
            q = self.q[e]
            getattr(block, e)(lambda eng, q=q: [f(eng) for f in q])


class Pool:
    def __init__(self, nc, n):
        self.tiles = [nc.alloc_sbuf_tensor(f"pl{i}", [128, 2048], BF16) for i in range(n)]
        self.n = n
        self.owner = [None] * n

    def alloc(self, name, count):
        idxs = [i for i in range(self.n) if self.owner[i] is None][:count]
        assert len(idxs) == count, \
            f"pool OOM: {name} wants {count}, free={self.owner.count(None)}"
        for i in idxs:
            self.owner[i] = name
        return [self.tiles[i] for i in idxs]

    def free(self, *names):
        for i in range(self.n):
            if self.owner[i] in names:
                self.owner[i] = None


def f32v(tile):
    return tile.ap().bitcast(F32)


def build_nc(taps=()):
    nc = bass.Bass("TRN2", target_bir_lowering=False)

    din = {}
    def inp(name, shape):
        din[name] = nc.dram_tensor(name, list(shape), F32, kind="ExternalInput")
        return din[name]

    for nm, sh in (("x", (L, DM)), ("xq", (L, DM)), ("xr", (L, DM)),
                   ("sccq", (128, 32)), ("scck", (128, 32)), ("srowq", (2, L)),
                   ("aWq", (DM, DM)), ("abq", (1, DM)), ("aWk", (DM, DM)),
                   ("abk", (1, DM)), ("aWv", (DM, DM)), ("abv", (1, DM)),
                   ("aWo", (DM, DM)), ("abo", (1, DM)), ("g1", (1, DM)),
                   ("g2", (1, DM)), ("fWq", (DM, DM)), ("fWkc", (256, DM)),
                   ("fWvc", (256, DM)), ("fWo", (DM, DM)), ("fWr", (NE, DM)),
                   ("febias", (1, NE)), ("sW1", (DM, DM)), ("sW2", (DM, DM)),
                   ("sW3", (DM, DM)), ("eW1", (NE, DM, DM)), ("eW2", (NE, DM, DM)),
                   ("eW3", (NE, DM, DM))):
        inp(nm, sh)

    out_d = nc.dram_tensor("out", [L, DM], F32, kind="ExternalOutput")
    TAP_SPECS = {
        "x2": ([L, DM], F32), "xff": ([L, DM], F32), "combine": ([128, 128], F32),
        "lnx0": ([128, 1024], BF16), "tch0": ([128, 1024], BF16),
        "k0": ([128, 1024], F32), "eq0": ([128, 1024], F32),
        "kft0": ([128, 2048], BF16), "vex0": ([128, 2048], BF16),
        "kvacc": ([128, 2048], F32), "kfacc32": ([128, 32], F32),
        "kvall0": ([128, 2048], BF16), "qftt0": ([128, 2048], BF16),
        "atn0": ([128, 2048], BF16), "nxt0": ([128, 2048], BF16),
        "qt0": ([128, 2048], BF16), "kct0": ([128, 2048], BF16),
        "a2t0": ([128, 2048], BF16), "cols0": ([128, 32], F32),
    }
    tap_d = {}
    for t in taps:
        shp, dt_ = TAP_SPECS[t]
        tap_d[t] = nc.dram_tensor("tap_" + t, shp, dt_, kind="ExternalOutput")

    R_FWQ, R_FWO, R_FWKC, R_FWVC = 0, 1024, 2048, 2304
    R_SW1, R_SW2, R_SW3 = 2560, 3584, 4608
    R_EW1, R_EW2, R_EW3 = 5632, 13824, 22016
    wsc = nc.dram_tensor("wsc", [30208, DM], BF16)
    cc_in = nc.dram_tensor("cc_in", [128, 2080], BF16)
    cc_out = nc.dram_tensor("cc_out", [1024, 2080], BF16)
    xsp = nc.dram_tensor("xsp", [L, DM], BF16)
    comb_dram = nc.dram_tensor("comb_dram", [NE, L], BF16)
    ident_d = nc.inline_tensor(np.eye(128, dtype=BFNP), name="ident_c")

    ident = nc.alloc_sbuf_tensor("ident", [128, 128], BF16)
    onec_bf = nc.alloc_sbuf_tensor("onec_bf", [1, 128], BF16)
    onec_f = nc.alloc_sbuf_tensor("onec_f", [1, 128], F32)
    brow = nc.alloc_sbuf_tensor("brow", [1, DM], F32)
    srow_sin = nc.alloc_sbuf_tensor("srow_sin", [1, L], BF16)
    srow_cos = nc.alloc_sbuf_tensor("srow_cos", [1, L], BF16)
    sccq_sb = nc.alloc_sbuf_tensor("sccq_sb", [128, 32], F32)
    scck_sb = nc.alloc_sbuf_tensor("scck_sb", [128, 32], F32)
    cols = nc.alloc_sbuf_tensor("cols", [128, 48], F32)
    hsum = nc.alloc_sbuf_tensor("hsum", [128, 96], F32)
    hrcp = nc.alloc_sbuf_tensor("hrcp", [128, 32], F32)
    kfacc = nc.alloc_sbuf_tensor("kfacc", [128, 32], F32)
    kfbf = nc.alloc_sbuf_tensor("kfbf", [128, 32], BF16)
    kvfs = nc.alloc_sbuf_tensor("kvfs", [128, 256], BF16)
    dencol = nc.alloc_sbuf_tensor("dencol", [128, 32], F32)
    zcol = nc.alloc_sbuf_tensor("zcol", [128, 32], F32)
    rt = nc.alloc_sbuf_tensor("rt", [128, 192], F32)
    combine = nc.alloc_sbuf_tensor("combine", [128, 128], F32)
    combine_bf = nc.alloc_sbuf_tensor("combine_bf", [128, 128], BF16)
    combT = nc.alloc_sbuf_tensor("combT", [8, L], BF16)
    wrbf = nc.alloc_sbuf_tensor("wrbf", [8, DM], BF16)
    wrt = nc.alloc_sbuf_tensor("wrt", [128, 64], BF16)
    eb_bc = nc.alloc_sbuf_tensor("eb_bc", [128, 8], F32)
    avz = nc.alloc_sbuf_tensor("avz", [128, 8], F32)
    cst = nc.alloc_sbuf_tensor("cst", [128, 2], F32)
    rowA = nc.alloc_sbuf_tensor("rowA", [1, DM], BF16)
    rowB = nc.alloc_sbuf_tensor("rowB", [1, DM], BF16)
    onescol = nc.alloc_sbuf_tensor("onescol", [128, 1], BF16)

    pool = Pool(nc, 45)

    _psg = []
    ps = []
    for i in range(6):
        g_ = nc.psum_tensor(f"ps{i}", [128, 512], F32)
        _psg.append(g_)
        ps.append(g_.__enter__())
    _ptg = nc.psum_tensor("pt", [128, 1024], BF16)
    _psg.append(_ptg)
    pt = _ptg.__enter__()
    _ps6g = nc.psum_tensor("ps6", [128, 512], F32)
    ps.append(_ps6g.__enter__())
    ptst = {"pt2": None, "w": None}

    k = KB(nc)
    cc_sem = nc.alloc_semaphore(name="cc_sem")

    with nc.Block() as block:
        G = {}

        def gw(name):
            return G.get(name)

        def tap(name, src_ap, tok_, guard=None):
            if name in taps:
                t_ = k.dma("sync", tap_d[name][:, :], src_ap, waits=(tok_,))
                if guard is not None:
                    gset(guard, t_)

        def gset(name, tok_):
            if tok_ is not None:
                G[name] = tok_

        CS = pool.alloc("cs", 4)
        WN = pool.alloc("wn", 8)
        TCH = pool.alloc("tch", 2)
        LNC = pool.alloc("lnc", 2)
        KFT = pool.alloc("kft", 2)
        KFC = pool.alloc("kfc", 2)
        VEX = pool.alloc("vex", 2)
        KVA = pool.alloc("kva", 2)
        BC = pool.alloc("bc", 2)
        SCR = pool.alloc("scr", 2)
        LSC = pool.alloc("lsc", 2) + pool.alloc("lsce", 2)

        BCv = [f32v(BC[0]), f32v(BC[1])]

        crow = brow.ap().bitcast(BF16)
        t_id = k.dma("sync", ident[:, :], ident_d[:, :])
        k.dma("sync", sccq_sb[:, :], din["sccq"][:, :])
        t_scck = k.dma("sync", scck_sb[:, :], din["scck"][:, :])
        t_ones = k.op("vector", lambda e: e.memset(onec_bf.ap(), 1.0), inc=0)
        k.op("vector", lambda e: e.memset(onescol.ap(), 1.0), inc=0)
        t_ones = k.op("vector", lambda e: e.memset(onec_f.ap(), 1.0), inc=1)
        t_srs = k.dma("gpsimd", srow_sin[:, :], din["srowq"][0:1, :])
        t_src = k.dma("gpsimd", srow_cos[:, :], din["srowq"][1:2, :])
        k.op("vector", lambda e: e.memset(cst[:, 0:1], EPS_LN))
        t_cst = k.op("vector", lambda e: e.memset(cst[:, 1:2], EPS_RMS), inc=1)
        k.op("vector", lambda e: e.memset(f32v(KVA[0]), 0.0))
        k.op("vector", lambda e: e.memset(f32v(KVA[1]), 0.0))
        t_kvz = k.op("vector", lambda e: e.memset(kfacc.ap(), 0.0), inc=1)

        def bias_bcast(dst_f32, row_in, psum_guards=(), expize=False):
            tl = k.dma("sync", brow[:, :], row_in, waits=(gw("brow"),))
            tms = []
            for h in range(2):
                tms.append(k.op("tensor", lambda e, h=h: e.matmul(
                    ps[4 + h][:, :], onec_f[0:1, :], brow[0:1, h * 512:(h + 1) * 512],
                    start=True, stop=True),
                    waits=[tl, t_ones] + list(psum_guards), inc=1))
            tv = None
            for h in range(2):
                if expize:
                    tv = k.op("scalar", lambda e, h=h, d=dst_f32: e.activation(
                        d[:, h * 512:(h + 1) * 512], ps[4 + h][:, :], AF.Exp),
                        waits=(tms[h],), inc=(h == 1))
                else:
                    tv = k.op("vector", lambda e, h=h, d=dst_f32: e.tensor_copy(
                        d[:, h * 512:(h + 1) * 512], ps[4 + h][:, :]),
                        waits=(tms[h],), inc=(h == 1))
            gset("brow", tv)
            return tv

        def ln_stats(src_ap, eps, par, waits):
            c = cols
            s1 = c[:, par * 12 + 0: par * 12 + 1]
            s2 = c[:, par * 12 + 1: par * 12 + 2]
            mean = c[:, par * 12 + 2: par * 12 + 3]
            msq = c[:, par * 12 + 3: par * 12 + 4]
            var = c[:, par * 12 + 4: par * 12 + 5]
            rstd = c[:, par * 12 + 5: par * 12 + 6]
            nmr = c[:, par * 12 + 6: par * 12 + 7]
            scrap = f32v(LSC[par])
            w = list(waits) + [gw(f"lsc{par}"), gw(f"cols{par}")]
            tv = k.op("vector", lambda e: e.reduce_sum(s1, src_ap, axis=AX.X,
                                                       negate=True),
                      waits=w, inc=1)
            tsq = k.op("scalar", lambda e: e.activation(scrap, src_ap, AF.Square,
                                                        accum_out=s2),
                       waits=w, inc=1)
            gset(f"lsc{par}", tsq)
            t_ = k.op("vector", lambda e: e.tensor_scalar(mean, s1, INV, None,
                                                          op0=ALU.mult),
                      waits=(tsq, tv), inc=1)
            t_ = k.op("vector", lambda e: e.tensor_tensor(msq, mean, mean, op=ALU.mult),
                      waits=(t_,), inc=1)
            t_ = k.op("vector", lambda e: e.tensor_scalar(var, s2, INV, None,
                                                          op0=ALU.mult),
                      waits=(t_,), inc=1)
            tv = k.op("vector", lambda e: e.tensor_sub(var, var, msq),
                      waits=(t_,), inc=1)
            eap = cst[:, 0:1] if eps == EPS_LN else cst[:, 1:2]
            t_ = k.op("scalar", lambda e: e.activation(rstd, var, AF.Ln, bias=eap),
                      waits=(tv, t_cst), inc=1)
            tl = k.op("scalar", lambda e: e.activation(rstd, rstd, AF.Exp, scale=-0.5),
                      waits=(t_,), inc=1)
            tl = k.op("vector", lambda e: e.tensor_tensor(nmr, mean, rstd, op=ALU.mult),
                      waits=(tl,), inc=1)
            return nmr, rstd, tl

        def rms_stats(src_ap, eps, par, waits):
            c = cols
            s2 = c[:, par * 12 + 7: par * 12 + 8]
            rstd = c[:, par * 12 + 8: par * 12 + 9]
            scrap = f32v(LSC[par])
            w = list(waits) + [gw(f"lsc{par}"), gw(f"cols{par}")]
            tsq = k.op("scalar", lambda e: e.activation(scrap, src_ap, AF.Square,
                                                        accum_out=s2), waits=w, inc=1)
            gset(f"lsc{par}", tsq)
            tv = k.op("vector", lambda e: e.tensor_scalar(rstd, s2, INV, None,
                                                          op0=ALU.mult),
                      waits=(tsq,), inc=1)
            eap = cst[:, 0:1] if eps == EPS_LN else cst[:, 1:2]
            t_ = k.op("scalar", lambda e: e.activation(rstd, rstd, AF.Ln, bias=eap),
                      waits=(tv, t_cst), inc=1)
            tl = k.op("scalar", lambda e: e.activation(rstd, rstd, AF.Exp, scale=-0.5),
                      waits=(t_,), inc=1)
            return rstd, tl

        def transpose_chunk(src_bf_ap, ngrp, dst_fn, src_ready, dst_guard=(),
                            dst_contig=None):
            pt2 = ptst["pt2"]
            tev = None
            tev_other = None
            first = True
            nr = (ngrp + 3) // 4
            for r in range(nr):
                q = r * 4
                n = min(4, ngrp - q)
                if pt2 is None:
                    bank, key = pt, "pt0"
                    eng = "scalar" if dst_contig is not None else "vector"
                else:
                    hb = r % 2
                    bank = pt if hb == 0 else pt2
                    key = f"pt{hb}"
                    eng = "vector"
                tmm = None
                for i in range(n):
                    w = list(src_ready) + [t_id, gw(key)]
                    if bank is not pt:
                        w.append(ptst["w"])
                    tmm = k.op("tensor", lambda e, i=i, q=q, bank=bank: e.transpose(
                        bank[:, i * 128:(i + 1) * 128],
                        src_bf_ap[:, (q + i) * 128:(q + i + 1) * 128],
                        ident[:, :]), waits=w, inc=(1 if i == n - 1 else 0))
                xw = list(dst_guard) if first else []
                if r == nr - 1 and tev_other is not None:
                    xw.append(tev_other)
                if dst_contig is not None:
                    if eng == "scalar":
                        fn = lambda e, q=q, n=n, bank=bank: e.copy(
                            dst_contig[:, q * 128:(q + n) * 128], bank[:, 0:n * 128])
                    else:
                        fn = lambda e, q=q, n=n, bank=bank: e.tensor_copy(
                            dst_contig[:, q * 128:(q + n) * 128], bank[:, 0:n * 128])
                    tev_new = k.op(eng, fn, waits=[tmm] + xw, inc=1)
                else:
                    tev_new = None
                    for i in range(n):
                        w = [tmm] + (xw if i == 0 else [])
                        if eng == "scalar":
                            fn = lambda e, i=i, q=q, bank=bank: e.copy(
                                dst_fn(q + i), bank[:, i * 128:(i + 1) * 128])
                        else:
                            fn = lambda e, i=i, q=q, bank=bank: e.tensor_copy(
                                dst_fn(q + i), bank[:, i * 128:(i + 1) * 128])
                        tev_new = k.op(eng, fn, waits=w, inc=(1 if i == n - 1 else 0))
                tev_other = tev
                tev = tev_new
                gset(key, tev_new)
                first = False
            return tev

        wk_ap = lambda dc: WN[dc // 2][:, (dc % 2) * 1024:(dc % 2 + 1) * 1024]
        wv_ap = lambda dc: WN[4 + dc // 2][:, (dc % 2) * 1024:(dc % 2 + 1) * 1024]

        t_bk = k.dma("gpsimd", rowA[:, :], din["abk"][0:1, :])
        t_bv = k.dma("gpsimd", rowB[:, :], din["abv"][0:1, :])

        wk_t, wv_t = [], []
        for dc in range(8):
            wk_t.append(k.dma("gpsimd", wk_ap(dc), din["aWk"][dc * 128:(dc + 1) * 128, :],
                              new_group=(dc == 0)))
        wk_t = [wk_t[-1]] * 8
        for dc in range(8):
            wv_t.append(k.dma("gpsimd", wv_ap(dc), din["aWv"][dc * 128:(dc + 1) * 128, :],
                              new_group=(dc == 0)))
        wv_t = [wv_t[-1]] * 8

        t_projps = [t_bv, t_bv]
        t_vproj_ps = None
        t_kvps = [None, None]
        t_kfps = None

        def a_stats(ch, src, tap0=False):
            rg = ch % 4
            csl = ch % 4
            xc = f32v(CS[csl])
            tld = k.dma("sync", xc, src[ch * 128:(ch + 1) * 128, :],
                        waits=(gw(f"cs{csl}"),))
            mean, rstd, tst = ln_stats(xc, EPS_LN, rg, (tld,))
            lnc_ap = LNC[rg // 2][:, (rg % 2) * 1024:(rg % 2 + 1) * 1024]
            t_ln = k.op("scalar", lambda e, s=xc, m=mean, r=rstd, d=lnc_ap:
                        e.activation(d, s, AF.Identity, scale=r, bias=m),
                        waits=(tst, gw(f"lnc{rg}")), inc=1)
            gset(f"cs{csl}", t_ln)
            gset(f"cols{rg}", t_ln)
            if tap0 and ch == 0:
                tap("lnx0", lnc_ap, t_ln, guard=f"lnc{rg}")
                tap("cols0", cols[:, 0:32], tst, guard=f"cols{rg}")
            return lnc_ap, t_ln

        def a_tr(ch, st, tap0=False):
            rg = ch % 4
            lnc_ap, t_ln = st
            tch_ap = TCH[rg // 2][:, (rg % 2) * 1024:(rg % 2 + 1) * 1024]
            t_tr = transpose_chunk(lnc_ap, 8, None,
                                   (t_ln,), dst_guard=(gw(f"tch{rg}"),),
                                   dst_contig=tch_ap)
            gset(f"lnc{rg}", t_tr)
            if tap0 and ch == 0:
                tap("tch0", tch_ap, t_tr, guard=f"tch{rg}")
            return tch_ap, t_tr

        sta = {c_: a_stats(c_, din["x"], tap0=True) for c_ in range(3)}
        stq = {0: a_tr(0, sta.pop(0), tap0=True)}
        for ch in range(NCH):
            par = ch % 2
            csl = ch % 4
            if ch + 3 < NCH:
                sta[ch + 3] = a_stats(ch + 3, din["x"], tap0=True)
            tch_ap, t_tr = stq.pop(ch)
            tmmk = [None, None]
            for hf in range(2):
                for dc in range(8):
                    w = [t_tr, wk_t[dc]] + ([t_projps[hf]] if dc == 0 else [])
                    k.op("tensor", lambda e, hf=hf, dc=dc, a=tch_ap: e.matmul(
                        ps[4 + hf][:, :], a[:, dc * 128:(dc + 1) * 128],
                        wk_ap(dc)[:, hf * 512:(hf + 1) * 512],
                        start=(dc == 0), stop=False), waits=w)
                tmmk[hf] = k.op("tensor", lambda e, hf=hf: e.matmul(
                    ps[4 + hf][:, :], onec_bf[0:1, :],
                    rowA[0:1, hf * 512:(hf + 1) * 512],
                    start=False, stop=True), waits=(t_bk, t_ones), inc=1)
            eq = f32v(SCR[par])
            t_exp = None
            for hf in range(2):
                t_exp = k.op("scalar", lambda e, hf=hf, d=eq: e.activation(
                    d[:, hf * 512:(hf + 1) * 512], ps[4 + hf][:, :], AF.Exp),
                    waits=(tmmk[hf], gw(f"scr{par}") if hf == 0 else None),
                    inc=1)
            tevk = t_exp
            gset(f"tch{ch % 4}", None) if False else None
            if ch == 0:
                tap("eq0", eq, t_exp, guard=f"scr{par}")
            hs = hsum[:, par * 16:(par + 1) * 16]
            hr = hrcp[:, par * 16:(par + 1) * 16]
            t_hsr = k.op("vector", lambda e, s=eq, d=hs: e.reduce_sum(
                d.rearrange("p (g o) -> p g o", o=1),
                s.rearrange("p (g i) -> p g i", g=16), axis=AX.X),
                waits=(t_exp, gw(f"hs{par}")), inc=1)
            t_hr = k.op("vector", lambda e, s=hs, d=hr: e.reciprocal(d, s),
                        waits=(t_hsr,), inc=1)
            kft_t, kfc_t = KFT[par], KFC[par]
            sin_c = scck_sb[:, ch:ch + 1]
            cos_c = scck_sb[:, 16 + ch:16 + ch + 1]
            hr2 = hrcp[:, par * 16 + 32:(par + 1) * 16 + 32] if False else None
            t_kft = None
            for half, colp in ((0, sin_c), (1, cos_c)):
                hsc = hsum[:, 32 + par * 32 + half * 16:32 + par * 32 + (half + 1) * 16]
                t_s2 = k.op("vector", lambda e, d=hsc, c=colp, hr=hr: e.tensor_scalar(
                    d, hr, c, None, op0=ALU.mult),
                    waits=(t_hr, t_scck, gw(f"kft{par}")), inc=1)
                t_kft = k.op("vector", lambda e, s=eq, d=kft_t, half=half,
                             hv=hsc.unsqueeze(2).broadcast_to([128, 16, 64])
                             : e.tensor_tensor(
                                 d.ap().rearrange("p (h j) -> p h j", j=128)
                                 [:, :, half * 64:(half + 1) * 64],
                                 s.rearrange("p (h j) -> p h j", j=64),
                                 hv, op=ALU.mult), waits=(t_s2,), inc=1)
            gset(f"scr{par}", t_kft)
            gset(f"hs{par}", t_kft)
            t_projps = [tevk, tevk]
            t_kfc = None
            for half, colp in ((0, sin_c), (1, cos_c)):
                t_kfc = k.op("vector", lambda e, d=kfc_t, s=kft_t, c=colp, half=half:
                             e.tensor_scalar(
                                 d.ap().rearrange("p (h j) -> p h j", j=128)
                                 [:, :, half * 64:(half + 1) * 64],
                                 s.ap().rearrange("p (h j) -> p h j", j=128)
                                 [:, :, half * 64:(half + 1) * 64],
                                 c, -1.0, op0=ALU.subtract, op1=ALU.mult),
                             waits=(t_kft, gw(f"kfc{par}")), inc=(half == 1))
            if ch == 0:
                tap("kft0", kft_t.ap(), t_kfc, guard=f"kft{par}")
            tmmv = [None, None]
            for hf in range(2):
                for dc in range(8):
                    w = [t_tr, wv_t[dc]] + ([t_vproj_ps] if dc == 0 else [])
                    k.op("tensor", lambda e, hf=hf, dc=dc, a=tch_ap: e.matmul(
                        ps[2 + hf][:, :], a[:, dc * 128:(dc + 1) * 128],
                        wv_ap(dc)[:, hf * 512:(hf + 1) * 512],
                        start=(dc == 0), stop=False), waits=w)
                tmmv[hf] = k.op("tensor", lambda e, hf=hf: e.matmul(
                    ps[2 + hf][:, :], onec_bf[0:1, :],
                    rowB[0:1, hf * 512:(hf + 1) * 512],
                    start=False, stop=True), waits=(t_bv, t_ones), inc=1)
            vex_t = VEX[par]
            t_vx = None
            for hf in range(2):
                t_vx = k.op("scalar", lambda e, hf=hf, d=vex_t: e.copy(
                    d[:, hf * 512:(hf + 1) * 512], ps[2 + hf][:, :]),
                    waits=(tmmv[hf], gw(f"vex{par}") if hf == 0 else None),
                    inc=(hf == 1))
            t_vproj_ps = t_vx
            gset(f"tch{ch % 4}", t_vx)
            for batch in range(2):
                last_mm = None
                for hh in range(8):
                    h = batch * 8 + hh
                    for br in range(2):
                        t = h * 2 + br
                        tloc = hh * 2 + br
                        lhs = (kft_t if br == 0 else kfc_t)[:, h * 128:(h + 1) * 128]
                        w = [t_kfc, t_vx]
                        if hh == 0 and br == 0:
                            w += [t_kvps[0] if batch == 1 else t_kvps[1], t_kfps]
                        k.op("tensor", lambda e, tloc=tloc, lhs=lhs, h=h, v=vex_t:
                             e.matmul(
                            ps[tloc // 8][:, (tloc % 8) * 64:(tloc % 8 + 1) * 64],
                            lhs, v[:, h * 64:(h + 1) * 64],
                            start=True, stop=True), waits=w)
                        last_mm = k.op("tensor", lambda e, t=t, lhs=lhs: e.matmul(
                            ps[6][:, t:t + 1], lhs, onescol[:, 0:1],
                            start=True, stop=True),
                            inc=(1 if (hh == 7 and br == 1) else 0))
                tacc = None
                for b in range(2):
                    dst = f32v(KVA[batch])
                    off = b * 512
                    tacc = k.op("vector", lambda e, b=b, dst=dst, off=off:
                                e.tensor_add(dst[:, off:off + 512],
                                             dst[:, off:off + 512], ps[b][:, :]),
                                waits=(last_mm, t_kvz), inc=(b == 1))
                t_kvps[batch] = tacc
                if batch == 1:
                    t_kfps = k.op("vector", lambda e: e.tensor_add(
                        kfacc[:, :], kfacc[:, :], ps[6][:, 0:32]),
                        waits=(last_mm,), inc=1)
            if ch == 0:
                tap("vex0", vex_t.ap(), t_vx, guard=f"vex{par}")
            gset(f"kft{par}", t_kfps)
            gset(f"kfc{par}", t_kfps)
            gset(f"vex{par}", t_kfps)
            if ch + 1 < NCH:
                stq[ch + 1] = a_tr(ch + 1, sta.pop(ch + 1), tap0=True)

        if "kvacc" in taps:
            k.dma("sync", tap_d["kvacc"][:, 0:1024], f32v(KVA[0]),
                  waits=(t_kvps[0], t_kvps[1]))
            k.dma("sync", tap_d["kvacc"][:, 1024:2048], f32v(KVA[1]),
                  waits=(t_kvps[0], t_kvps[1]))
        tap("kfacc32", kfacc[:, :], t_kfps)
        kvbf0 = SCR[0]
        t1 = k.op("vector", lambda e: e.tensor_copy(kvbf0[:, 0:1024], f32v(KVA[0])),
                  waits=(t_kvps[0], t_kvps[1], gw("scr0")), inc=1)
        t1 = k.op("vector", lambda e: e.tensor_copy(kvbf0[:, 1024:2048], f32v(KVA[1])),
                  inc=1)
        t2 = k.op("vector", lambda e: e.tensor_copy(kfbf[:, :], kfacc[:, :]),
                  waits=(t_kfps,), inc=1)
        tci1 = k.dma("sync", cc_in[:, 0:2048], kvbf0[:, :], waits=(t1,))
        tci2 = k.dma("sync", cc_in[:, 2048:2080], kfbf[:, :], waits=(t2,),
                     new_group=False)
        gset("scr0", tci1)

        def do_cc(e):
            return e.collective_compute(
                "AllGather", ALU.bypass, replica_groups=[list(range(8))],
                ins=[cc_in.ap().opt()], outs=[cc_out.ap().opt()]).then_inc(cc_sem, 1)
        k.op("gpsimd", do_cc, waits=(tci1, tci2), inc=0)
        t_cc = None

        _ps6g.__exit__(None, None, None)
        _pt2g = nc.psum_tensor("pt2", [128, 1024], BF16)
        ptst["pt2"] = _pt2g.__enter__()
        ptst["w"] = t_kfps


        pool.free("kft", "kfc", "vex", "kva")
        QFTT = pool.alloc("qftt", 16)
        QFC = pool.alloc("qfc", 2)
        last_kv_mm = k.tok("tensor")
        wq_t = []
        for dc in range(8):
            wq_t.append(k.dma("gpsimd", wk_ap(dc),
                              din["aWq"][dc * 128:(dc + 1) * 128, :],
                              waits=(last_kv_mm,), new_group=(dc == 0)))
        wq_t = [wq_t[-1]] * 8
        t_bq = k.dma("gpsimd", rowA[:, :], din["abq"][0:1, :],
                     waits=(last_kv_mm,))
        sta = {c_: a_stats(c_, din["xq"]) for c_ in range(3)}
        stq = {0: a_tr(0, sta.pop(0))}
        pend_q = []
        for ch in range(NCH):
            par = ch % 2
            csl = ch % 4
            if ch + 3 < NCH:
                sta[ch + 3] = a_stats(ch + 3, din["xq"])
            tch_ap, t_tr = stq.pop(ch)
            tmmq = [None, None]
            for hf in range(2):
                for dc in range(8):
                    w = [t_tr, wq_t[dc]] + ([t_projps[hf]] if dc == 0 else [])
                    k.op("tensor", lambda e, hf=hf, dc=dc, a=tch_ap: e.matmul(
                        ps[4 + hf][:, :], a[:, dc * 128:(dc + 1) * 128],
                        wk_ap(dc)[:, hf * 512:(hf + 1) * 512],
                        start=(dc == 0), stop=False), waits=w)
                tmmq[hf] = k.op("tensor", lambda e, hf=hf: e.matmul(
                    ps[4 + hf][:, :], onec_bf[0:1, :],
                    rowA[0:1, hf * 512:(hf + 1) * 512],
                    start=False, stop=True), waits=(t_bq, t_ones), inc=1)
            if pend_q:
                qsrc, pch, tq = pend_q.pop(0)
                t_trq = transpose_chunk(qsrc.ap(), 16, None, (tq,),
                                        dst_contig=QFTT[pch].ap())
                gset(f"qfc{pch % 2}", t_trq)
            eq = f32v(SCR[par])
            t_exp = None
            for hf in range(2):
                t_exp = k.op("scalar", lambda e, hf=hf, d=eq: e.activation(
                    d[:, hf * 512:(hf + 1) * 512], ps[4 + hf][:, :], AF.Exp),
                    waits=(tmmq[hf], gw(f"scr{par}") if hf == 0 else None),
                    inc=1)
            tevq = t_exp
            t_projps = [tevq, tevq]
            gset(f"tch{ch % 4}", tevq)
            hs = hsum[:, par * 16:(par + 1) * 16]
            hr = hrcp[:, par * 16:(par + 1) * 16]
            t_hsr = k.op("vector", lambda e, s=eq, d=hs: e.reduce_sum(
                d.rearrange("p (g o) -> p g o", o=1),
                s.rearrange("p (g i) -> p g i", g=16), axis=AX.X),
                waits=(t_exp, gw(f"hs{par}")), inc=1)
            t_hr = k.op("vector", lambda e, s=hs, d=hr: e.reciprocal(d, s),
                        waits=(t_hsr,), inc=1)
            qf_t = QFC[par]
            sin_c = sccq_sb[:, ch:ch + 1]
            cos_c = sccq_sb[:, 16 + ch:16 + ch + 1]
            t_qft = None
            for half, colp in ((0, sin_c), (1, cos_c)):
                hsc = hsum[:, 32 + par * 32 + half * 16:32 + par * 32 + (half + 1) * 16]
                t_s2 = k.op("vector", lambda e, d=hsc, c=colp, hr=hr: e.tensor_scalar(
                    d, hr, c, None, op0=ALU.mult),
                    waits=(t_hr, gw(f"qfc{par}")), inc=1)
                t_qft = k.op("vector", lambda e, s=eq, d=qf_t, half=half,
                             hv=hsc.unsqueeze(2).broadcast_to([128, 16, 64])
                             : e.tensor_tensor(
                                 d.ap().rearrange("p (h j) -> p h j", j=128)
                                 [:, :, half * 64:(half + 1) * 64],
                                 s.rearrange("p (h j) -> p h j", j=64),
                                 hv, op=ALU.mult), waits=(t_s2,), inc=1)
            gset(f"scr{par}", t_qft)
            gset(f"hs{par}", t_qft)
            pend_q.append((qf_t, ch, t_qft))
            if ch + 1 < NCH:
                stq[ch + 1] = a_tr(ch + 1, sta.pop(ch + 1))
        while pend_q:
            qsrc, pch, tq = pend_q.pop(0)
            t_trq = transpose_chunk(qsrc.ap(), 16, None, (tq,),
                                    dst_contig=QFTT[pch].ap())
            gset(f"qfc{pch % 2}", t_trq)
        t_qftT_done = k.tok("scalar")

        pool.free("tch", "lnc", "qfc", "cs")
        KVP = pool.alloc("kvp", 8)
        k.q["sync"].append(lambda eng: eng.wait_ge(cc_sem, 1))
        t_a3_all = (k.tok("tensor"), k.tok("vector"), k.tok("scalar"))
        t_kvall = []
        for b in range(8):
            t_kvall.append(k.dma("sync", KVP[b].ap(),
                                 cc_out[b * 128:(b + 1) * 128, 0:2048],
                                 waits=t_a3_all, new_group=(b == 0)))
        t_kvf = []
        for b in range(8):
            t_kvf.append(k.dma("sync", kvfs[:, b * 32:(b + 1) * 32],
                               cc_out[b * 128:(b + 1) * 128, 2048:2080],
                               waits=t_a3_all, new_group=False))
        t_all16 = t_kvf[-1]
        t_kvall = [t_all16] * 8
        t_kvf = [t_all16] * 8

        tap("kvall0", KVP[0].ap(), t_kvall[0])
        tap("qftt0", QFTT[0].ap(), t_qftT_done if False else k.tok("vector"))

        k.barrier()
        pool.free("bc", "wn", "lsce")
        SCB = pool.alloc("scb", 1)
        CMP = pool.alloc("cmp", 2)
        ATN = pool.alloc("atn", 8)

        t_scb = None
        t_ps4 = t_projps[0]
        for blk in range(4):
            tm = None
            for half, row in ((0, srow_sin), (1, srow_cos)):
                tm = k.op("tensor", lambda e, blk=blk, half=half, row=row: e.matmul(
                    ps[4][half * 64:(half + 1) * 64, :], onec_bf[0:1, 0:64],
                    row[0:1, blk * 512:(blk + 1) * 512], start=True, stop=True),
                    waits=(t_srs, t_src, t_ones, t_ps4), inc=(half == 1))
            t_scb = k.op("vector", lambda e, blk=blk: e.tensor_copy(
                SCB[0][:, blk * 512:(blk + 1) * 512], ps[4][:, :]),
                waits=(tm,), inc=1)
            t_ps4 = t_scb

        t_mainps = [None, None]
        t_denps = None
        t_atn = None
        for hg in range(2):
            for ch in range(NCH):
                b0 = ch // 2
                rot = ch % 2
                pm, pc_ = ps[rot], ps[2 + rot]
                cmp_ap = CMP[rot].ap()
                t_cmp = k.op("vector", lambda e, ch=ch, hg=hg, d=cmp_ap:
                             e.tensor_tensor(
                    d[:, 0:1024].rearrange("p (h j) -> p h j", j=128),
                    SCB[0][:, ch * 128:(ch + 1) * 128].unsqueeze(1)
                    .broadcast_to([128, 8, 128]),
                    QFTT[ch][:, hg * 1024:(hg + 1) * 1024]
                    .rearrange("p (h j) -> p h j", j=128),
                    op=ALU.subtract),
                    waits=(t_scb, t_qftT_done, gw(f"cmp{rot}")), inc=1)
                last_mm = None
                for hh in range(8):
                    h = hg * 8 + hh
                    lhs_m = QFTT[ch][:, h * 128:(h + 1) * 128]
                    lhs_c = cmp_ap[:, hh * 128:(hh + 1) * 128]
                    for br, (lhs, pb) in enumerate(((lhs_m, pm), (lhs_c, pc_))):
                        t = h * 2 + br
                        w = [t_cmp, t_kvall[b0], t_kvf[b0]]
                        if hh == 0 and br == 0:
                            w += [t_mainps[rot], t_denps]
                        k.op("tensor", lambda e, t=t, lhs=lhs, pb=pb, hh=hh, b0=b0:
                             e.matmul(pb[:, hh * 64:(hh + 1) * 64], lhs,
                                      KVP[b0][:, t * 64:(t + 1) * 64],
                                      start=True, stop=True), waits=w)
                        last_mm = k.op("tensor",
                                       lambda e, t=t, lhs=lhs, hh=hh, b0=b0, br=br:
                                       e.matmul(ps[4][:, hh * 2 + br:hh * 2 + br + 1],
                                                lhs,
                                                kvfs[:, b0 * 32 + t:b0 * 32 + t + 1],
                                                start=True, stop=True),
                                       inc=(1 if (hh == 7 and br == 1) else 0))
                gset(f"cmp{rot}", last_mm)
                dc_ = dencol[:, rot * 16:(rot + 1) * 16]
                zc_ = zcol[:, rot * 16:(rot + 1) * 16]
                t_dn = k.op("vector", lambda e, d=dc_: e.tensor_scalar(
                    d, ps[4][:, 0:16], EPS_Z, None, op0=ALU.max),
                    waits=(last_mm, gw(f"z{rot}")), inc=1)
                t_denps = k.op("vector", lambda e, d=dc_, z=zc_: e.reciprocal(z, d),
                               waits=(t_dn,), inc=1)
                u = f32v(SCR[rot])
                z1v = zc_[:, 0:16:2].unsqueeze(2).broadcast_to([128, 8, 64])
                z2v = zc_[:, 1:16:2].unsqueeze(2).broadcast_to([128, 8, 64])
                k.op("vector", lambda e, u=u, pc_=pc_, z2v=z2v: e.tensor_tensor(
                    u[:, 0:512].rearrange("p (h j) -> p h j", j=64),
                    pc_[:, :].rearrange("p (h j) -> p h j", j=64), z2v, op=ALU.mult),
                    waits=(gw(f"scr{rot}"), t_denps))
                t_u = k.op("vector", lambda e, u=u, pm=pm, z1v=z1v: e.tensor_tensor(
                    u[:, 512:1024].rearrange("p (h j) -> p h j", j=64),
                    pm[:, :].rearrange("p (h j) -> p h j", j=64), z1v, op=ALU.mult),
                    inc=1)
                t_mainps[rot] = t_u
                gset(f"z{rot}", t_u)
                piece = LSC[rot][:, 0:512]
                t_ab = k.op("vector", lambda e, u=u, piece=piece: e.tensor_add(
                    piece, u[:, 512:1024], u[:, 0:512]),
                    waits=(gw(f"lsc{rot}"), t_u), inc=1)
                gset(f"scr{rot}", t_ab)
                t_atn = transpose_chunk(
                    piece, 4,
                    lambda i, hg=hg, ch=ch: ATN[hg * 4 + i][:, ch * 128:(ch + 1) * 128],
                    (t_ab,))
                gset(f"lsc{rot}", t_atn)

        tap("atn0", ATN[0].ap(), t_atn)

        k.barrier()
        pool.free("qftt", "scb", "cmp", "kvp", "tch")
        BIG16 = pool.alloc("big16", 16)
        CS = pool.alloc("cs", 2)
        WN2 = pool.alloc("wn2", 4)
        BC = pool.alloc("bc", 1)
        NXT = pool.alloc("nxt", 8)
        LNC = pool.alloc("lnc", 1)
        WKV = pool.alloc("wkv", 1)
        BCv = [f32v(BC[0])]
        for key in list(G):
            if key.startswith("cs") or key.startswith("bc") or key.startswith("lnc"):
                G.pop(key)
        wo_ap = lambda ac: WN2[ac // 2][:, (ac % 2) * 1024:(ac % 2 + 1) * 1024]
        wo_t = []
        for ac in range(8):
            wo_t.append(k.dma("gpsimd", wo_ap(ac),
                              din["aWo"][ac * 128:(ac + 1) * 128, :],
                              new_group=(ac == 0)))
        wo_t = [wo_t[-1]] * 8
        t_bo = k.dma("gpsimd", rowA[:, :], din["abo"][0:1, :])
        prep_done = {}

        def prep_tok(row0, rows):
            best = None
            for r, (t, n) in prep_done.items():
                if row0 < r + n and r < row0 + rows:
                    if best is None or t[1] > best[1]:
                        best = t
            return best

        early_prep = [
            (din["fWkc"][:, :], R_FWKC, 256),
            (din["fWvc"][:, :], R_FWVC, 256),
            (din["fWq"][0:256, :], R_FWQ, 256),
            (din["fWq"][256:512, :], R_FWQ + 256, 256),
            (din["fWq"][512:768, :], R_FWQ + 512, 256),
            (din["fWq"][768:1024, :], R_FWQ + 768, 256),
        ]
        t_g1 = bias_bcast(BCv[0], din["g1"][0:1, :])
        t_projps = [t_g1, t_g1]
        t_nxt = None
        pend_x = []
        for ch in range(NCH):
            par = ch % 2
            csl = ch % 2
            half_ap = LNC[0][:, par * 1024:(par + 1) * 1024]
            t_xr = k.dma("gpsimd", half_ap, din["xr"][ch * 128:(ch + 1) * 128, :],
                         waits=(gw(f"half{par}"),))
            tmmo = [None, None]
            for hf in range(2):
                for ac in range(8):
                    w = [t_atn, wo_t[ac]] + ([t_projps[hf]] if ac == 0 else [])
                    k.op("tensor", lambda e, hf=hf, ac=ac, ch=ch: e.matmul(
                        ps[4 + hf][:, :], ATN[ac][:, ch * 128:(ch + 1) * 128],
                        wo_ap(ac)[:, hf * 512:(hf + 1) * 512],
                        start=(ac == 0), stop=False), waits=w)
                tmmo[hf] = k.op("tensor", lambda e, hf=hf: e.matmul(
                    ps[4 + hf][:, :], onec_bf[0:1, :],
                    rowA[0:1, hf * 512:(hf + 1) * 512],
                    start=False, stop=True), waits=(t_bo, t_ones), inc=1)
            if pend_x:
                ha, pch, tnx_p = pend_x.pop(0)
                t_nxt = transpose_chunk(ha, 8,
                                        lambda i, pch=pch: NXT[i][:, pch * 128:
                                                                  (pch + 1) * 128],
                                        (tnx_p,))
                gset(f"half{pch % 2}", t_nxt)
            oc_ = f32v(CS[csl])
            tev = None
            for hf in range(2):
                tev = k.op("vector", lambda e, hf=hf, d=oc_: e.tensor_copy(
                    d[:, hf * 512:(hf + 1) * 512], ps[4 + hf][:, :]),
                    waits=(tmmo[hf], gw(f"cs{csl}")), inc=(hf == 1))
            t_projps = [tev, tev]
            mean, rstd, tst = ln_stats(oc_, EPS_LN, par, (tev,))
            nrm = f32v(SCR[par])
            t_n = k.op("scalar", lambda e, s=oc_, m=mean, r=rstd, d=nrm:
                       e.activation(d, s, AF.Identity, scale=r, bias=m),
                       waits=(tst, gw(f"scr{par}")), inc=1)
            gset(f"cs{csl}", t_n)
            gset(f"cols{par}", t_n)
            t_x2 = k.op("vector", lambda e, d=f32v(BIG16[ch]), n=nrm, x=half_ap:
                        e.tensor_add(d, n, x), waits=(t_xr, t_n), inc=1)
            gset(f"scr{par}", t_x2)
            if "x2" in taps:
                tap("x2_pc", None, None)
                t_tap = k.dma("sync", tap_d["x2"][ch * 128:(ch + 1) * 128, :],
                              f32v(BIG16[ch]), waits=(t_x2,))
            else:
                t_tap = None
            t_spd = k.dma("gpsimd", xsp[ch * 128:(ch + 1) * 128, :], f32v(BIG16[ch]),
                          waits=(t_x2,))
            if early_prep:
                esrc, erow, erows = early_prep.pop(0)
                tsave = k.dma("gpsimd", wsc[erow:erow + erows, :], esrc)
                prep_done[erow] = (tsave, erows)
            mean2, rstd2_, tst2 = ln_stats(f32v(BIG16[ch]), EPS_LN, par, (t_x2,))
            t_ln2 = k.op("vector", lambda e, s=f32v(BIG16[ch]), m=mean2, r=rstd2_:
                         e.tensor_scalar(s, s, m, r, op0=ALU.add, op1=ALU.mult),
                         waits=(tst2, t_spd, t_tap), inc=1)
            gset(f"cols{par}", t_ln2)
            var2 = cols[:, par * 12 + 4: par * 12 + 5]
            ms_ = cols[:, par * 12 + 7: par * 12 + 8]
            rstd1 = cols[:, par * 12 + 8: par * 12 + 9]
            t_ = k.op("vector", lambda e, v=var2, r=rstd2_, d=ms_: e.tensor_tensor(
                d, v, r, op=ALU.mult), waits=(tst2,), inc=1)
            t_ = k.op("vector", lambda e, r=rstd2_, d=ms_: e.tensor_tensor(
                d, d, r, op=ALU.mult), waits=(t_,), inc=1)
            t_ = k.op("scalar", lambda e, d=rstd1, s=ms_: e.activation(
                d, s, AF.Ln, bias=cst[:, 1:2]), waits=(t_, t_cst), inc=1)
            trs = k.op("scalar", lambda e, d=rstd1: e.activation(
                d, d, AF.Exp, scale=-0.5), waits=(t_,), inc=1)
            scr = f32v(SCR[par])
            t_sc = k.op("vector", lambda e, s=f32v(BIG16[ch]), r=rstd1, d=scr:
                        e.tensor_scalar(d, s, r, None, op0=ALU.mult),
                        waits=(trs, t_ln2, gw(f"scr{par}")), inc=1)
            t_nx = k.op("vector", lambda e, s=scr, d=half_ap, b=BCv[0]: e.tensor_tensor(
                d, s, b, op=ALU.mult), waits=(t_g1, t_x2, t_sc), inc=1)
            gset(f"scr{par}", t_nx)
            gset(f"cols{par}", t_nx)
            pend_x.append((half_ap, ch, t_nx))
        while pend_x:
            ha, pch, tnx_p = pend_x.pop(0)
            t_nxt = transpose_chunk(ha, 8,
                                    lambda i, pch=pch: NXT[i][:, pch * 128:
                                                              (pch + 1) * 128],
                                    (tnx_p,))
            gset(f"half{pch % 2}", t_nxt)
        wkct = []
        for dc in range(8):
            wkct.append(k.dmat("sync", WKV[0][:, dc * 256:(dc + 1) * 256],
                               wsc[R_FWKC:R_FWKC + 256, dc * 128:(dc + 1) * 128],
                               waits=(prep_tok(R_FWKC, 256),), new_group=(dc == 0)))
        wkct = [wkct[-1]] * 8
        if "xff" in taps:
            pass

        late_jobs = [(din["fWo"][:, :], R_FWO, DM), (din["sW1"][:, :], R_SW1, DM),
                     (din["sW2"][:, :], R_SW2, DM), (din["sW3"][:, :], R_SW3, DM)]
        for e_i in range(NE):
            for nm, base in (("eW1", R_EW1), ("eW2", R_EW2), ("eW3", R_EW3)):
                late_jobs.append((din[nm][e_i, :, :], base + e_i * 1024, DM))
        for src, row, rows in late_jobs:
            tsave = k.dma("gpsimd", wsc[row:row + rows, :], src, dom="prep")
            prep_done[row] = (tsave, rows)

        k.barrier()
        pool.free("atn", "wn2", "cs", "bc", "lnc")
        KCT = pool.alloc("kct", 2)
        VX2 = pool.alloc("vx2", 4)
        QT = pool.alloc("qt", 8)
        WKV2 = pool.alloc("wkv2", 1)
        WQR = pool.alloc("wqr", 1)
        wvct = []
        for dc in range(8):
            wvct.append(k.dmat("sync", WKV2[0][:, dc * 256:(dc + 1) * 256],
                               wsc[R_FWVC:R_FWVC + 256, dc * 128:(dc + 1) * 128],
                               waits=(prep_tok(R_FWVC, 256),),
                               new_group=(dc == 0)))
        wvct = [wvct[-1]] * 8
        wq_pre = {}
        for oc in (0, 1):
            wdst_ = WQR[0][:, (oc % 2) * 1024:(oc % 2 + 1) * 1024]
            tq_ = []
            for dc in range(8):
                tq_.append(k.dmat("sync", wdst_[:, dc * 128:(dc + 1) * 128],
                                  wsc[R_FWQ + oc * 128:R_FWQ + (oc + 1) * 128,
                                      dc * 128:(dc + 1) * 128],
                                  waits=(prep_tok(R_FWQ + oc * 128, 128),),
                                  new_group=(dc == 0)))
            wq_pre[oc] = [tq_[-1]] * 8
        t_bps = [t_projps[0], t_projps[1]]
        for kc2 in range(2):
            for sb in range(4):
                rot = sb % 2
                last = None
                for dc in range(8):
                    w = [t_nxt, wkct[dc]] + ([t_bps[rot]] if dc == 0 else [])
                    last = k.op("tensor", lambda e, kc2=kc2, sb=sb, dc=dc, rot=rot:
                                e.matmul(ps[rot][:, :],
                                         WKV[0][:, dc * 256 + kc2 * 128:
                                                dc * 256 + (kc2 + 1) * 128],
                                         NXT[dc][:, sb * 512:(sb + 1) * 512],
                                         start=(dc == 0), stop=(dc == 7)),
                                waits=w, inc=(dc == 7))
                t_bps[rot] = k.op("vector", lambda e, kc2=kc2, sb=sb, rot=rot:
                                  e.tensor_copy(KCT[kc2][:, sb * 512:(sb + 1) * 512],
                                                ps[rot][:, :]),
                                  waits=(last,), inc=1)
                t_kc_last = last
        for ch in range(NCH):
            rot = ch % 2
            vx = VX2[ch // 4][:, (ch % 4) * 512:(ch % 4) * 512 + 257]
            last = None
            for dc in range(8):
                w = [t_nxt, wvct[dc]] + ([t_bps[rot]] if dc == 0 else [])
                last = k.op("tensor", lambda e, ch=ch, dc=dc, rot=rot: e.matmul(
                    ps[rot][:, 0:256], NXT[dc][:, ch * 128:(ch + 1) * 128],
                    WKV2[0][:, dc * 256:(dc + 1) * 256],
                    start=(dc == 0), stop=(dc == 7)), waits=w, inc=(dc == 7))
            k.op("vector", lambda e, vx=vx: e.memset(vx[:, 256:257], 1.0))
            t_bps[rot] = k.op("vector", lambda e, vx=vx, rot=rot: e.tensor_copy(
                vx[:, 0:256], ps[rot][:, 0:256]), waits=(last,), inc=1)
        for oc in range(8):
            wpar = oc % 2
            wdst = WQR[0][:, wpar * 1024:(wpar + 1) * 1024]
            if oc in wq_pre:
                tq = wq_pre[oc]
            else:
                stg = WKV[0][:, wpar * 1024:(wpar + 1) * 1024]
                w = [prep_tok(R_FWQ + oc * 128, 128), gw(f"wqs{wpar}")]
                if oc < 4:
                    w.append(t_kc_last)
                t_st = k.dma("sync", stg,
                             wsc[R_FWQ + oc * 128:R_FWQ + (oc + 1) * 128, :],
                             waits=w)
                t_tr8 = transpose_chunk(stg, 8,
                                        lambda i, wdst=wdst: wdst[:, i * 128:
                                                                  (i + 1) * 128],
                                        (t_st,), dst_guard=(gw(f"wqr{wpar}"),))
                gset(f"wqs{wpar}", t_tr8)
                tq = [t_tr8] * 8
            for sb in range(4):
                rot = sb % 2
                last = None
                for dc in range(8):
                    w = [t_nxt, tq[dc]] + ([t_bps[rot]] if dc == 0 else [])
                    last = k.op("tensor",
                                lambda e, sb=sb, dc=dc, rot=rot, wdst=wdst: e.matmul(
                                    ps[rot][:, :], wdst[:, dc * 128:(dc + 1) * 128],
                                    NXT[dc][:, sb * 512:(sb + 1) * 512],
                                    start=(dc == 0), stop=(dc == 7)),
                                waits=w, inc=(dc == 7))
                t_bps[rot] = k.op("vector", lambda e, oc=oc, sb=sb, rot=rot:
                                  e.tensor_copy(QT[oc][:, sb * 512:(sb + 1) * 512],
                                                ps[rot][:, :]),
                                  waits=(last,), inc=1)
                gset(f"wqr{wpar}", t_bps[rot])
        t_b1 = k.tok("vector")

        k.barrier()
        pool.free("nxt", "wkv", "wkv2", "wqr", "scr")
        PTL = pool.alloc("ptl", 4)
        A2T = pool.alloc("a2t", 8)
        t_scps = [None, None]
        t_avps = [None, None]
        t_ptl_guard = None
        t_a2t = None
        for h in range(H2):
            for sb in range(4):
                t_pexp = [None] * 16
                for tc in range(16):
                    rot = tc % 2
                    last = None
                    for kc2 in range(2):
                        w = [t_b1] + ([t_scps[rot], t_ptl_guard] if kc2 == 0 else [])
                        last = k.op("tensor",
                                    lambda e, h=h, sb=sb, tc=tc, kc2=kc2, rot=rot:
                                    e.matmul(ps[rot][:, :],
                                             KCT[kc2][:, tc * 128:(tc + 1) * 128],
                                             QT[2 * h + kc2][:, sb * 512:(sb + 1) * 512],
                                             start=(kc2 == 0), stop=(kc2 == 1)),
                                    waits=w, inc=(kc2 == 1))
                    dst = PTL[tc // 4][:, (tc % 4) * 512:(tc % 4 + 1) * 512]
                    t_pexp[tc] = k.op("scalar", lambda e, dst=dst, rot=rot: e.activation(
                        dst, ps[rot][:, :], AF.Exp, scale=0.0625),
                        waits=(last,), inc=1)
                    t_scps[rot] = t_pexp[tc]
                for ss in range(4):
                    ch = sb * 4 + ss
                    rot = ss % 2
                    last = None
                    for tc in range(16):
                        w = [t_pexp[tc]] + ([t_avps[rot]] if tc == 0 else [])
                        last = k.op("tensor", lambda e, ss=ss, tc=tc, rot=rot: e.matmul(
                            ps[2 + rot][:, 0:257],
                            PTL[tc // 4][:, (tc % 4) * 512 + ss * 128:
                                         (tc % 4) * 512 + (ss + 1) * 128],
                            VX2[tc // 4][:, (tc % 4) * 512:(tc % 4) * 512 + 257],
                            start=(tc == 0), stop=(tc == 15)), waits=w, inc=(tc == 15))
                    zz = avz[:, rot:rot + 1]
                    t_zz = k.op("vector", lambda e, zz=zz, rot=rot: e.reciprocal(
                        zz, ps[2 + rot][:, 256:257]),
                        waits=(last, gw(f"avz{rot}")), inc=1)
                    piece = LSC[rot][:, 0:256]
                    t_pc = k.op("vector", lambda e, piece=piece, zz=zz, rot=rot:
                                e.tensor_scalar(piece, ps[2 + rot][:, 0:256],
                                                zz, None, op0=ALU.mult),
                                waits=(gw(f"lsc{rot}"), t_zz), inc=1)
                    t_avps[rot] = t_pc
                    gset(f"avz{rot}", t_pc)
                    t_a2t = transpose_chunk(
                        piece, 2, None, (t_pc,),
                        dst_contig=A2T[ch // 2][:, (ch % 2) * 1024 + 2 * h * 128:
                                               (ch % 2) * 1024 + (2 * h + 2) * 128])
                    gset(f"lsc{rot}", t_a2t)
            t_ptl_guard = t_a2t

        k.barrier()
        pool.free("kct", "vx2", "qt", "ptl")
        WOT = pool.alloc("wot", 4)
        NX2T = pool.alloc("nx2t", 8)
        LNC = pool.alloc("lnc2", 1)
        SCR = pool.alloc("scr", 2)
        BC = pool.alloc("bc", 2)
        BCv = [f32v(BC[0]), f32v(BC[1])]
        for key_ in list(G):
            if key_.startswith("scr") or key_.startswith("bc"):
                G.pop(key_)
        wot_ap = lambda ac: WOT[ac // 2][:, (ac % 2) * 1024:(ac % 2 + 1) * 1024]
        wot_t = []
        for ac in range(8):
            wot_t.append(k.dmat("sync", wot_ap(ac),
                                wsc[R_FWO:R_FWO + 1024, ac * 128:(ac + 1) * 128],
                                waits=(prep_tok(R_FWO, 1024),), new_group=(ac == 0)))
        wot_t = [wot_t[-1]] * 8
        t_g2 = bias_bcast(BCv[1], din["g2"][0:1, :])
        tl = k.dma("sync", brow[:, 0:8], din["febias"][0:1, :], waits=(gw("brow"),))
        tm = k.op("tensor", lambda e: e.matmul(ps[4][:, 0:8], onec_f[0:1, :],
                                               brow[0:1, 0:8], start=True, stop=True),
                  waits=(tl, t_g2), inc=1)
        t_eb = k.op("vector", lambda e: e.tensor_copy(eb_bc[:, :], ps[4][:, 0:8]),
                    waits=(tm,), inc=1)
        gset("brow", t_eb)
        t_wr = k.dma("gpsimd", wrbf[:, :], din["fWr"][:, :])
        t_wrt = None
        for dc in range(8):
            tmw = k.op("tensor", lambda e, dc=dc: e.transpose(
                pt[:, 0:8], wrbf[:, dc * 128:(dc + 1) * 128], ident[0:8, 0:8]),
                waits=(t_wr, t_id, gw("pt0")), inc=1)
            t_wrt = k.op("vector", lambda e, dc=dc: e.tensor_copy(
                wrt[:, dc * 8:(dc + 1) * 8], pt[:, 0:8]), waits=(tmw,), inc=1)
            gset("pt0", t_wrt)

        t_xff = [None] * NCH
        st = {"rps": t_eb}
        t_comb_ch = [None] * NCH
        t_ct = None
        pend_b = []

        def c1_tail(pch, tnx_p, nxc_p):
            par = pch % 2
            ch = pch
            t_nx = tnx_p
            t_nx2 = transpose_chunk(nxc_p, 8,
                                    lambda i, ch=ch: NX2T[i][:, ch * 128:(ch + 1) * 128],
                                    (t_nx,))
            gset(f"l2c{par}", t_nx2)
            o = par * 96
            last = None
            for dc in range(8):
                w = [t_nx2, t_wrt] + ([st["rps"]] if dc == 0 else [])
                last = k.op("tensor", lambda e, ch=ch, dc=dc: e.matmul(
                    ps[4][:, 0:8], NX2T[dc][:, ch * 128:(ch + 1) * 128],
                    wrt[:, dc * 8:(dc + 1) * 8],
                    start=(dc == 0), stop=(dc == 7)), waits=w, inc=(dc == 7))
            lg = rt[:, o:o + 8]
            rw = rt[:, o + 8:o + 16]
            i1 = rt[:, o + 16:o + 24]
            rw2 = rt[:, o + 24:o + 32]
            i2 = rt[:, o + 32:o + 40]
            t3 = rt[:, o + 40:o + 48]
            nm = rt[:, o + 80:o + 81]
            sm = rt[:, o + 81:o + 82]
            rs = rt[:, o + 82:o + 83]
            m1 = rt[:, o + 83:o + 84]
            m2 = rt[:, o + 84:o + 85]
            dd = rt[:, o + 85:o + 86]
            w1 = rt[:, o + 86:o + 87]
            w2 = rt[:, o + 87:o + 88]
            t1_ = k.op("vector", lambda e, lg=lg: e.tensor_add(lg, ps[4][:, 0:8],
                                                               eb_bc[:, :]),
                       waits=(last, t_eb, gw(f"rt{par}")), inc=1)
            st["rps"] = t1_
            t_ = k.op("vector", lambda e, lg=lg, nm=nm: e.reduce_max(
                nm, lg, axis=AX.X, negate=True), waits=(t1_,), inc=1)
            tsf = k.op("scalar", lambda e, lg=lg, nm=nm, rw=rw, sm=sm: e.activation(
                rw, lg, AF.Exp, bias=nm, accum_out=sm), waits=(t_,), inc=1)
            t_ = k.op("vector", lambda e, sm=sm, rs=rs: e.reciprocal(rs, sm),
                      waits=(tsf,), inc=1)
            t_ = k.op("vector", lambda e, rw=rw, rs=rs: e.tensor_scalar(
                rw, rw, rs, None, op0=ALU.mult), waits=(t_,), inc=1)
            t_ = k.op("vector", lambda e, rw=rw, m1=m1: e.reduce_max(
                m1, rw, axis=AX.X), waits=(t_,), inc=1)
            t_ = k.op("vector", lambda e, rw=rw, m1=m1, i1=i1: e.tensor_scalar(
                i1, rw, m1, None, op0=ALU.is_equal), waits=(t_,), inc=1)
            t_ = k.op("vector", lambda e, i1=i1, rw2=rw2: e.tensor_scalar(
                rw2, i1, -1e9, None, op0=ALU.mult), waits=(t_,), inc=1)
            t_ = k.op("vector", lambda e, rw=rw, rw2=rw2: e.tensor_add(rw2, rw2, rw),
                      waits=(t_,), inc=1)
            t_ = k.op("vector", lambda e, rw2=rw2, m2=m2: e.reduce_max(
                m2, rw2, axis=AX.X), waits=(t_,), inc=1)
            t_ = k.op("vector", lambda e, rw2=rw2, m2=m2, i2=i2: e.tensor_scalar(
                i2, rw2, m2, None, op0=ALU.is_equal), waits=(t_,), inc=1)
            t2_ = k.op("vector", lambda e, m1=m1, m2=m2, dd=dd: e.tensor_sub(dd, m2, m1),
                       waits=(t_,), inc=1)
            ted = k.op("scalar", lambda e, dd=dd: e.activation(dd, dd, AF.Exp),
                       waits=(t2_,), inc=1)
            t_ = k.op("vector", lambda e, dd=dd: e.tensor_scalar(
                dd, dd, 1.0, None, op0=ALU.add), waits=(ted,), inc=1)
            t_ = k.op("vector", lambda e, dd=dd, w1=w1: e.reciprocal(w1, dd),
                      waits=(t_,), inc=1)
            t_ = k.op("vector", lambda e, w1=w1, w2=w2: e.tensor_scalar(
                w2, w1, -1.0, 1.0, op0=ALU.mult, op1=ALU.add), waits=(t_,), inc=1)
            cc_ = combine[:, ch * 8:(ch + 1) * 8]
            t_ = k.op("vector", lambda e, i1=i1, w1=w1, cc_=cc_: e.tensor_scalar(
                cc_, i1, w1, None, op0=ALU.mult), waits=(t_,), inc=1)
            t_ = k.op("vector", lambda e, i2=i2, w2=w2, t3=t3: e.tensor_scalar(
                t3, i2, w2, None, op0=ALU.mult), waits=(t_,), inc=1)
            t_comb = k.op("vector", lambda e, cc_=cc_, t3=t3: e.tensor_add(cc_, cc_, t3),
                          waits=(t_,), inc=1)
            gset(f"rt{par}", t_comb)
            t_comb_ch[ch] = t_comb

        for ch in range(NCH):
            par = ch % 2
            tofs = []
            for hf in range(2):
                rot = hf
                last = None
                for ac in range(8):
                    w = [t_a2t, wot_t[ac]] + ([t_bps[rot]] if ac == 0 else [])
                    last = k.op("tensor", lambda e, ch=ch, hf=hf, ac=ac, rot=rot:
                                e.matmul(ps[rot][:, :],
                                         A2T[ch // 2][:, (ch % 2) * 1024 + ac * 128:
                                                      (ch % 2) * 1024 + (ac + 1) * 128],
                                         wot_ap(ac)[:, hf * 512:(hf + 1) * 512],
                                         start=(ac == 0), stop=(ac == 7)),
                                waits=w, inc=(ac == 7))
                t_bps[rot] = k.op("vector", lambda e, ch=ch, hf=hf, rot=rot:
                                  e.tensor_add(
                                      f32v(BIG16[ch])[:, hf * 512:(hf + 1) * 512],
                                      f32v(BIG16[ch])[:, hf * 512:(hf + 1) * 512],
                                      ps[rot][:, :]),
                                  waits=(last,), inc=1)
                tofs.append(t_bps[rot])
            t_xff[ch] = tofs[1]
            if "xff" in taps:
                k.dma("sync", tap_d["xff"][ch * 128:(ch + 1) * 128, :], f32v(BIG16[ch]),
                      waits=(tofs[0], tofs[1]))
            if pend_b:
                c1_tail(*pend_b.pop(0))
            bsrc = f32v(BIG16[ch])
            rstd2, trs = rms_stats(bsrc, EPS_RMS, par, (t_xff[ch],))
            scr = f32v(SCR[par])
            t_sc = k.op("vector", lambda e, s=bsrc, r=rstd2, d=scr: e.tensor_scalar(
                d, s, r, None, op0=ALU.mult), waits=(trs, gw(f"scr{par}")), inc=1)
            nxc = LNC[0][:, par * 1024:(par + 1) * 1024]
            t_nx = k.op("vector", lambda e, s=scr, d=nxc, b=BCv[1]: e.tensor_tensor(
                d, s, b, op=ALU.mult), waits=(t_g2, gw(f"l2c{par}"), t_sc), inc=1)
            gset(f"scr{par}", t_nx)
            gset(f"cols{par}", t_nx)
            pend_b.append((ch, t_nx, nxc))
        while pend_b:
            c1_tail(*pend_b.pop(0))
        t_nx2 = k.tok("vector")
        if "combine" in taps:
            k.dma("sync", tap_d["combine"][:, :], combine[:, :],
                  waits=(t_comb_ch[NCH - 1],))

        k.barrier()
        pool.free("lnc2", "bc", "scr", "lsc", "a2t", "wot")
        W1TA = pool.alloc("w1ta", 4)
        W1TB = pool.alloc("w1tb", 4)
        W3T = pool.alloc("w3t", 4)
        W2T = pool.alloc("w2t", 4)
        SG = pool.alloc("sg", 2)
        HP = pool.alloc("hp", 2)
        TMP = pool.alloc("tmp", 1)
        tmpv = f32v(TMP[0])
        w_ap = lambda W, dc: W[dc // 2][:, (dc % 2) * 1024:(dc % 2 + 1) * 1024]

        t_wmm_guard = None
        t_w1_guard = None
        t_w3_guard = None
        t_w2_guard = None
        w1_pref = None
        t_cb = None
        t_gps = [None, None]
        t_ups = [None, None]
        t_eops = [st["rps"], st["rps"]]
        t_addg = [None, None]
        t_sg_guard = None
        t_hp_guard = None
        last_eo = None
        for w_i in range(9):
            shared = (w_i == 0)
            if shared:
                rows1, rows2, rows3 = R_SW1, R_SW2, R_SW3
            else:
                e_i = w_i - 1
                rows1 = R_EW1 + e_i * 1024
                rows2 = R_EW2 + e_i * 1024
                rows3 = R_EW3 + e_i * 1024
            W1T = W1TA if w_i % 2 == 0 else W1TB
            wt = {}
            if w_i == 0:
                tt_ = None
                for dc in range(8):
                    tt_ = k.dmat("sync", w_ap(W1T, dc),
                                 wsc[rows1:rows1 + 1024, dc * 128:(dc + 1) * 128],
                                 waits=(prep_tok(rows1, 1024),), new_group=(dc == 0))
                wt["w1"] = [tt_] * 8
            else:
                wt["w1"] = w1_pref
            for nm_, rows, W, grd in (("w3", rows3, W3T, t_w3_guard),
                                      ("w2", rows2, W2T, t_w2_guard)):
                tt_ = None
                for dc in range(8):
                    tt_ = k.dmat("sync", w_ap(W, dc),
                                 wsc[rows:rows + 1024, dc * 128:(dc + 1) * 128],
                                 waits=(prep_tok(rows, 1024), grd),
                                 new_group=(dc == 0))
                wt[nm_] = [tt_] * 8
            if w_i < 8:
                nrows1 = R_EW1 + w_i * 1024
                WN1 = W1TB if w_i % 2 == 0 else W1TA
                tt_ = None
                for dc in range(8):
                    tt_ = k.dmat(
                        "sync", w_ap(WN1, dc),
                        wsc[nrows1:nrows1 + 1024, dc * 128:(dc + 1) * 128],
                        waits=(prep_tok(nrows1, 1024), t_w1_guard),
                        new_group=(dc == 0))
                w1_pref = [tt_] * 8
            last_g_mm = None
            last_u_mm = None
            last_eo_mm = None
            for tb in range(4):
                t_hp = [None] * 8
                for fc in range(8):
                    rot = fc % 2
                    lastg = None
                    for dc in range(8):
                        w = [t_nx2, wt["w1"][dc]] + ([t_gps[rot]] if dc == 0 else [])
                        lastg = k.op("tensor",
                                     lambda e, fc=fc, dc=dc, tb=tb, rot=rot,
                                     wap=w_ap(W1T, dc): e.matmul(
                                         ps[rot][:, :],
                                         wap[:, fc * 128:(fc + 1) * 128],
                                         NX2T[dc][:, tb * 512:(tb + 1) * 512],
                                         start=(dc == 0), stop=(dc == 7)),
                                     waits=w, inc=(dc == 7))
                    last_g_mm = lastg
                    sg_ap = SG[fc // 4][:, (fc % 4) * 512:(fc % 4 + 1) * 512]
                    t_sg = k.op("scalar", lambda e, sg_ap=sg_ap, rot=rot: e.activation(
                        sg_ap, ps[rot][:, :], AF.Silu),
                        waits=(lastg, t_sg_guard), inc=1)
                    t_gps[rot] = t_sg
                    lastu = None
                    for dc in range(8):
                        w = [t_nx2, wt["w3"][dc]] + ([t_ups[rot]] if dc == 0 else [])
                        lastu = k.op("tensor",
                                     lambda e, fc=fc, dc=dc, tb=tb, rot=rot: e.matmul(
                                         ps[2 + rot][:, :],
                                         w_ap(W3T, dc)[:, fc * 128:(fc + 1) * 128],
                                         NX2T[dc][:, tb * 512:(tb + 1) * 512],
                                         start=(dc == 0), stop=(dc == 7)),
                                     waits=w, inc=(dc == 7))
                    last_u_mm = lastu
                    hp_ap = HP[fc // 4][:, (fc % 4) * 512:(fc % 4 + 1) * 512]
                    t_u = k.op("vector",
                               lambda e, hp_ap=hp_ap, sg_ap=sg_ap, rot=rot:
                               e.tensor_tensor(hp_ap, sg_ap, ps[2 + rot][:, :],
                                               op=ALU.mult),
                               waits=(lastu, t_sg, t_hp_guard), inc=1)
                    t_ups[rot] = t_u
                    t_hp[fc] = t_u
                for ss in range(4):
                    ch = tb * 4 + ss
                    for hf in range(2):
                        rot = (ss * 2 + hf) % 2
                        last = None
                        for fc in range(8):
                            w = [t_hp[fc], wt["w2"][fc]] + \
                                ([t_eops[rot]] if fc == 0 else [])
                            last = k.op("tensor",
                                        lambda e, ss=ss, hf=hf, fc=fc, rot=rot:
                                        e.matmul(
                                            ps[4 + rot][:, :],
                                            HP[fc // 4][:, (fc % 4) * 512 + ss * 128:
                                                        (fc % 4) * 512 + (ss + 1) * 128],
                                            w_ap(W2T, fc)[:, hf * 512:(hf + 1) * 512],
                                            start=(fc == 0), stop=(fc == 7)),
                                        waits=w, inc=(fc == 7))
                        last_eo_mm = last
                        if shared:
                            t_eops[rot] = k.op("vector",
                                               lambda e, ch=ch, hf=hf, rot=rot:
                                               e.tensor_add(
                                                   f32v(BIG16[ch])
                                                   [:, hf * 512:(hf + 1) * 512],
                                                   f32v(BIG16[ch])
                                                   [:, hf * 512:(hf + 1) * 512],
                                                   ps[4 + rot][:, :]),
                                               waits=(last,), inc=1)
                            t_addg[rot] = t_eops[rot]
                        else:
                            cbc = combine[:, ch * 8 + e_i:ch * 8 + e_i + 1]
                            th = tmpv[:, rot * 512:(rot + 1) * 512]
                            t_m = k.op("vector",
                                       lambda e, th=th, rot=rot, cbc=cbc:
                                       e.tensor_scalar(th, ps[4 + rot][:, :],
                                                       cbc, None, op0=ALU.mult),
                                       waits=(last, t_addg[rot]), inc=1)
                            t_eops[rot] = t_m
                            t_addg[rot] = k.op("vector",
                                               lambda e, ch=ch, hf=hf, th=th:
                                               e.tensor_add(
                                                   f32v(BIG16[ch])
                                                   [:, hf * 512:(hf + 1) * 512],
                                                   f32v(BIG16[ch])
                                                   [:, hf * 512:(hf + 1) * 512],
                                                   th), waits=(t_m,), inc=1)
                        last_eo = t_addg[rot]
                t_sg_guard = last_eo
                t_hp_guard = last_eo
            t_wmm_guard = last_eo
            t_w1_guard = last_g_mm
            t_w3_guard = last_u_mm
            t_w2_guard = last_eo_mm

        k.barrier()
        pool.free("w1ta", "w1tb", "w3t", "w2t", "sg", "hp", "tmp", "nx2t")
        SCR = pool.alloc("scr", 2)
        LSC = pool.alloc("lsc", 2)
        for key_ in list(G):
            if key_.startswith("scr") or key_.startswith("lsc"):
                G.pop(key_)
        XRL = pool.alloc("xrl", 1)
        CSO = pool.alloc("cso", 2)
        for ch in range(NCH):
            par = ch % 2
            xrl_ap = XRL[0][:, par * 1024:(par + 1) * 1024]
            t_x2l = k.dma("sync", xrl_ap, xsp[ch * 128:(ch + 1) * 128, :],
                          waits=(gw(f"xrl{par}"),))
            bsrc = f32v(BIG16[ch])
            mean, rstd, tst = ln_stats(bsrc, EPS_LN, par, (last_eo,))
            nrm = f32v(SCR[par])
            t_nr = k.op("scalar", lambda e, s=bsrc, m=mean, r=rstd, d=nrm:
                        e.activation(d, s, AF.Identity, scale=r, bias=m),
                        waits=(tst, gw(f"scr{par}")), inc=1)
            ot = f32v(CSO[par])
            t_o = k.op("vector", lambda e, n=nrm, x=xrl_ap, d=ot: e.tensor_add(d, n, x),
                       waits=(t_x2l, gw(f"cso{par}"), t_nr), inc=1)
            gset(f"xrl{par}", t_o)
            gset(f"scr{par}", t_o)
            gset(f"cols{par}", t_o)
            b0, jj = ch // 2, ch % 2
            off = (jj * 1024 + b0) * 1024
            w_ = [t_o] + ([k.lastg["sync"]] if ch == 0 else [])
            t_w = k.op("sync", lambda e, ot=ot, off=off: e.dma_start(
                out=bass.AP(out_d, off, [[8 * 1024, 128], [1, 1024]]), in_=ot),
                waits=w_, inc=16)
            k.lastg["sync"] = t_w
            gset(f"cso{par}", t_w)
        k.final_wait("sync")

        k.build(block)
    return nc


_CACHE = {}


def kernel(**inp):
    x = np.ascontiguousarray(inp["L0_attn_query"], dtype=np.float32)
    key = tuple(sorted(TAPS))
    if key not in _CACHE:
        _CACHE[key] = build_nc(tuple(TAPS))
    nc = _CACHE[key]

    idx = (np.pi / 2) * (np.arange(1, L + 1, dtype=np.float64) / L)
    sin_t = np.sin(idx).astype(np.float32)
    cos_t = np.cos(idx).astype(np.float32)

    w = {
        "aWq": inp["L0_attn_Wq"], "abq": inp["L0_attn_bq"].reshape(1, DM),
        "aWk": inp["L0_attn_Wk"], "abk": inp["L0_attn_bk"].reshape(1, DM),
        "aWv": inp["L0_attn_Wv"], "abv": inp["L0_attn_bv"].reshape(1, DM),
        "aWo": inp["L0_attn_Wo"], "abo": inp["L0_attn_bo"].reshape(1, DM),
        "g1": inp["L0_ffn_g1"].reshape(1, DM), "g2": inp["L0_ffn_g2"].reshape(1, DM),
        "fWq": inp["L0_ffn_Wq"], "fWkc": inp["L0_ffn_Wkc"], "fWvc": inp["L0_ffn_Wvc"],
        "fWo": inp["L0_ffn_Wo"], "fWr": inp["L0_ffn_Wr"],
        "febias": inp["L0_ffn_expert_bias"].reshape(1, NE),
        "sW1": inp["L0_ffn_sW1"], "sW2": inp["L0_ffn_sW2"], "sW3": inp["L0_ffn_sW3"],
        "eW1": inp["L0_ffn_eW1"], "eW2": inp["L0_ffn_eW2"], "eW3": inp["L0_ffn_eW3"],
    }
    w = {kk: np.ascontiguousarray(vv, dtype=np.float32) for kk, vv in w.items()}

    in_maps = []
    for c in range(8):
        xq = np.ascontiguousarray(x[:, c * 256:(c + 1) * 256, :].reshape(L, DM))
        xr = np.ascontiguousarray(
            x[c].reshape(256, 8, DM).transpose(1, 0, 2).reshape(L, DM))
        l0q = np.empty((128, 16), np.int64)
        l0k = np.empty((128, 16), np.int64)
        for ch in range(16):
            l0q[:, ch] = c * 256 + (ch % 2) * 128 + np.arange(128)
            l0k[:, ch] = ch * 128 + np.arange(128)
        sccq = np.concatenate([sin_t[l0q], cos_t[l0q]], axis=1).astype(np.float32)
        scck = np.concatenate([sin_t[l0k], cos_t[l0k]], axis=1).astype(np.float32)
        rq = c * 256 + (np.arange(L) % 256)
        srowq = np.stack([sin_t[rq], cos_t[rq]], axis=0).astype(np.float32)
        m = {"x": np.ascontiguousarray(x[c]), "xq": xq, "xr": xr,
             "sccq": np.ascontiguousarray(sccq), "scck": np.ascontiguousarray(scck),
             "srowq": np.ascontiguousarray(srowq)}
        m.update(w)
        in_maps.append(m)

    res = run_bass_kernel_spmd(nc, in_maps, core_ids=list(range(8)))
    out = np.stack([np.asarray(res.results[c]["out"], dtype=np.float32)
                    for c in range(8)], axis=0)
    if TAPS:
        kernel.tap_results = [{t: np.asarray(res.results[c]["tap_" + t])
                               for t in TAPS} for c in range(8)]
    return out


# version 33
# speedup vs baseline: 1.0338x; 1.0131x over previous
import numpy as np
import ml_dtypes
import concourse.bass as bass
import concourse.mybir as mybir
from concourse.bass_utils import run_bass_kernel_spmd

F32 = mybir.dt.float32
BF16 = mybir.dt.bfloat16
AF = mybir.ActivationFunctionType
ALU = mybir.AluOpType
AX = mybir.AxisListType
BFNP = ml_dtypes.bfloat16

L = 2048
DM = 1024
NCH = 16
NH = 16
H2 = 4
NE = 8
EPS_LN = 1e-5
EPS_RMS = 1e-6
EPS_Z = 1e-6
INV = 1.0 / DM

ENGS = ("sync", "gpsimd", "tensor", "vector", "scalar")

TAPS = ()


class KB:

    DOMS = ENGS + ("prep", "sdma")

    def __init__(self, nc):
        self.nc = nc
        self.sems = {e: nc.alloc_semaphore(name=f"sem_{e}") for e in self.DOMS}
        self.cnt = dict.fromkeys(self.DOMS, 0)
        self.wm = {e: dict.fromkeys(self.DOMS, 0) for e in ENGS}
        self.q = {e: [] for e in ENGS}
        self.lastg = dict.fromkeys(self.DOMS, None)

    def tok(self, e):
        return (e, self.cnt[e])

    def _wait(self, e, toks):
        for t in toks:
            if t is None:
                continue
            oe, v = t
            if v <= 0 or self.wm[e][oe] >= v:
                continue
            self.wm[e][oe] = v
            s = self.sems[oe]
            self.q[e].append(lambda eng, s=s, v=v: eng.wait_ge(s, v))

    def op(self, e, fn, waits=(), inc=0, dom=None):
        self._wait(e, waits)
        if inc:
            dom = dom or e
            self.cnt[dom] += inc
            s = self.sems[dom]
            self.q[e].append(lambda eng, fn=fn, s=s, i=inc: fn(eng).then_inc(s, i))
            return (dom, self.cnt[dom])
        self.q[e].append(fn)
        return None

    def dma(self, e, out, in_, waits=(), dom=None, new_group=True):
        dom = dom or e
        w = list(waits) + ([self.lastg[dom]] if new_group else [])
        t = self.op(e, lambda eng: eng.dma_start(out=out, in_=in_), w, inc=16, dom=dom)
        self.lastg[dom] = t
        return t

    def dmat(self, e, out, in_, waits=(), dom=None, new_group=True):
        dom = dom or e
        w = list(waits) + ([self.lastg[dom]] if new_group else [])
        t = self.op(e, lambda eng: eng.dma_start(out=out, in_=in_, transpose=True),
                    w, inc=16, dom=dom)
        self.lastg[dom] = t
        return t

    def barrier(self):
        toks = [self.tok(e) for e in ENGS]
        for e in ENGS:
            self._wait(e, toks)

    def final_wait(self, e):
        s = self.sems[e]
        v = self.cnt[e]
        self.q[e].append(lambda eng, s=s, v=v: eng.wait_ge(s, v))

    def build(self, block):
        for e in ENGS:
            q = self.q[e]
            getattr(block, e)(lambda eng, q=q: [f(eng) for f in q])


class Pool:
    def __init__(self, nc, n):
        self.tiles = [nc.alloc_sbuf_tensor(f"pl{i}", [128, 2048], BF16) for i in range(n)]
        self.n = n
        self.owner = [None] * n

    def alloc(self, name, count):
        idxs = [i for i in range(self.n) if self.owner[i] is None][:count]
        assert len(idxs) == count, \
            f"pool OOM: {name} wants {count}, free={self.owner.count(None)}"
        for i in idxs:
            self.owner[i] = name
        return [self.tiles[i] for i in idxs]

    def free(self, *names):
        for i in range(self.n):
            if self.owner[i] in names:
                self.owner[i] = None


def f32v(tile):
    return tile.ap().bitcast(F32)


def build_nc(taps=()):
    nc = bass.Bass("TRN2", target_bir_lowering=False)

    din = {}
    def inp(name, shape):
        din[name] = nc.dram_tensor(name, list(shape), F32, kind="ExternalInput")
        return din[name]

    for nm, sh in (("x", (L, DM)), ("xq", (L, DM)), ("xr", (L, DM)),
                   ("sccq", (128, 32)), ("scck", (128, 32)), ("srowq", (2, L)),
                   ("aWq", (DM, DM)), ("abq", (1, DM)), ("aWk", (DM, DM)),
                   ("abk", (1, DM)), ("aWv", (DM, DM)), ("abv", (1, DM)),
                   ("aWo", (DM, DM)), ("abo", (1, DM)), ("g1", (1, DM)),
                   ("g2", (1, DM)), ("fWq", (DM, DM)), ("fWkc", (256, DM)),
                   ("fWvc", (256, DM)), ("fWo", (DM, DM)), ("fWr", (NE, DM)),
                   ("febias", (1, NE)), ("sW1", (DM, DM)), ("sW2", (DM, DM)),
                   ("sW3", (DM, DM)), ("eW1", (NE, DM, DM)), ("eW2", (NE, DM, DM)),
                   ("eW3", (NE, DM, DM))):
        inp(nm, sh)

    out_d = nc.dram_tensor("out", [L, DM], F32, kind="ExternalOutput")
    TAP_SPECS = {
        "x2": ([L, DM], F32), "xff": ([L, DM], F32), "combine": ([128, 128], F32),
        "lnx0": ([128, 1024], BF16), "tch0": ([128, 1024], BF16),
        "k0": ([128, 1024], F32), "eq0": ([128, 1024], F32),
        "kft0": ([128, 2048], BF16), "vex0": ([128, 2048], BF16),
        "kvacc": ([128, 2048], F32), "kfacc32": ([128, 32], F32),
        "kvall0": ([128, 2048], BF16), "qftt0": ([128, 2048], BF16),
        "atn0": ([128, 2048], BF16), "nxt0": ([128, 2048], BF16),
        "qt0": ([128, 2048], BF16), "kct0": ([128, 2048], BF16),
        "a2t0": ([128, 2048], BF16), "cols0": ([128, 32], F32),
    }
    tap_d = {}
    for t in taps:
        shp, dt_ = TAP_SPECS[t]
        tap_d[t] = nc.dram_tensor("tap_" + t, shp, dt_, kind="ExternalOutput")

    R_FWQ, R_FWO, R_FWKC, R_FWVC = 0, 1024, 2048, 2304
    R_SW1, R_SW2, R_SW3 = 2560, 3584, 4608
    R_EW1, R_EW2, R_EW3 = 5632, 13824, 22016
    wsc = nc.dram_tensor("wsc", [30208, DM], BF16)
    cc_in = nc.dram_tensor("cc_in", [128, 2080], BF16)
    cc_out = nc.dram_tensor("cc_out", [1024, 2080], BF16)
    xsp = nc.dram_tensor("xsp", [L, DM], BF16)
    comb_dram = nc.dram_tensor("comb_dram", [NE, L], BF16)
    ident_d = nc.inline_tensor(np.eye(128, dtype=BFNP), name="ident_c")

    ident = nc.alloc_sbuf_tensor("ident", [128, 128], BF16)
    onec_bf = nc.alloc_sbuf_tensor("onec_bf", [1, 128], BF16)
    onec_f = nc.alloc_sbuf_tensor("onec_f", [1, 128], F32)
    brow = nc.alloc_sbuf_tensor("brow", [1, DM], F32)
    srow_sin = nc.alloc_sbuf_tensor("srow_sin", [1, L], BF16)
    srow_cos = nc.alloc_sbuf_tensor("srow_cos", [1, L], BF16)
    sccq_sb = nc.alloc_sbuf_tensor("sccq_sb", [128, 32], F32)
    scck_sb = nc.alloc_sbuf_tensor("scck_sb", [128, 32], F32)
    cols = nc.alloc_sbuf_tensor("cols", [128, 48], F32)
    hsum = nc.alloc_sbuf_tensor("hsum", [128, 96], F32)
    hrcp = nc.alloc_sbuf_tensor("hrcp", [128, 32], F32)
    kfacc = nc.alloc_sbuf_tensor("kfacc", [128, 32], F32)
    kfbf = nc.alloc_sbuf_tensor("kfbf", [128, 32], BF16)
    kvfs = nc.alloc_sbuf_tensor("kvfs", [128, 256], BF16)
    dencol = nc.alloc_sbuf_tensor("dencol", [128, 32], F32)
    zcol = nc.alloc_sbuf_tensor("zcol", [128, 32], F32)
    rt = nc.alloc_sbuf_tensor("rt", [128, 192], F32)
    combine = nc.alloc_sbuf_tensor("combine", [128, 128], F32)
    combine_bf = nc.alloc_sbuf_tensor("combine_bf", [128, 128], BF16)
    combT = nc.alloc_sbuf_tensor("combT", [8, L], BF16)
    wrbf = nc.alloc_sbuf_tensor("wrbf", [8, DM], BF16)
    wrt = nc.alloc_sbuf_tensor("wrt", [128, 64], BF16)
    eb_bc = nc.alloc_sbuf_tensor("eb_bc", [128, 8], F32)
    avz = nc.alloc_sbuf_tensor("avz", [128, 8], F32)
    cst = nc.alloc_sbuf_tensor("cst", [128, 2], F32)
    rowA = nc.alloc_sbuf_tensor("rowA", [1, DM], BF16)
    rowB = nc.alloc_sbuf_tensor("rowB", [1, DM], BF16)
    onescol = nc.alloc_sbuf_tensor("onescol", [128, 1], BF16)

    pool = Pool(nc, 45)

    _psg = []
    ps = []
    for i in range(6):
        g_ = nc.psum_tensor(f"ps{i}", [128, 512], F32)
        _psg.append(g_)
        ps.append(g_.__enter__())
    _ptg = nc.psum_tensor("pt", [128, 1024], BF16)
    _psg.append(_ptg)
    pt = _ptg.__enter__()
    _ps6g = nc.psum_tensor("ps6", [128, 512], F32)
    ps.append(_ps6g.__enter__())
    ptst = {"pt2": None, "w": None}

    k = KB(nc)
    cc_sem = nc.alloc_semaphore(name="cc_sem")

    with nc.Block() as block:
        G = {}

        def gw(name):
            return G.get(name)

        def tap(name, src_ap, tok_, guard=None):
            if name in taps:
                t_ = k.dma("sync", tap_d[name][:, :], src_ap, waits=(tok_,))
                if guard is not None:
                    gset(guard, t_)

        def gset(name, tok_):
            if tok_ is not None:
                G[name] = tok_

        CS = pool.alloc("cs", 4)
        WN = pool.alloc("wn", 8)
        TCH = pool.alloc("tch", 2)
        LNC = pool.alloc("lnc", 2)
        KFT = pool.alloc("kft", 2)
        KFC = pool.alloc("kfc", 2)
        VEX = pool.alloc("vex", 2)
        KVA = pool.alloc("kva", 2)
        BC = pool.alloc("bc", 2)
        SCR = pool.alloc("scr", 2)
        LSC = pool.alloc("lsc", 2) + pool.alloc("lsce", 2)

        BCv = [f32v(BC[0]), f32v(BC[1])]

        crow = brow.ap().bitcast(BF16)
        t_id = k.dma("sync", ident[:, :], ident_d[:, :])
        k.dma("sync", sccq_sb[:, :], din["sccq"][:, :])
        t_scck = k.dma("sync", scck_sb[:, :], din["scck"][:, :])
        t_ones = k.op("vector", lambda e: e.memset(onec_bf.ap(), 1.0), inc=0)
        k.op("vector", lambda e: e.memset(onescol.ap(), 1.0), inc=0)
        t_ones = k.op("vector", lambda e: e.memset(onec_f.ap(), 1.0), inc=1)
        t_srs = k.dma("gpsimd", srow_sin[:, :], din["srowq"][0:1, :])
        t_src = k.dma("gpsimd", srow_cos[:, :], din["srowq"][1:2, :])
        k.op("vector", lambda e: e.memset(cst[:, 0:1], EPS_LN))
        t_cst = k.op("vector", lambda e: e.memset(cst[:, 1:2], EPS_RMS), inc=1)
        k.op("vector", lambda e: e.memset(f32v(KVA[0]), 0.0))
        k.op("vector", lambda e: e.memset(f32v(KVA[1]), 0.0))
        t_kvz = k.op("vector", lambda e: e.memset(kfacc.ap(), 0.0), inc=1)

        def bias_bcast(dst_f32, row_in, psum_guards=(), expize=False):
            tl = k.dma("sync", brow[:, :], row_in, waits=(gw("brow"),))
            tms = []
            for h in range(2):
                tms.append(k.op("tensor", lambda e, h=h: e.matmul(
                    ps[4 + h][:, :], onec_f[0:1, :], brow[0:1, h * 512:(h + 1) * 512],
                    start=True, stop=True),
                    waits=[tl, t_ones] + list(psum_guards), inc=1))
            tv = None
            for h in range(2):
                if expize:
                    tv = k.op("scalar", lambda e, h=h, d=dst_f32: e.activation(
                        d[:, h * 512:(h + 1) * 512], ps[4 + h][:, :], AF.Exp),
                        waits=(tms[h],), inc=(h == 1))
                else:
                    tv = k.op("vector", lambda e, h=h, d=dst_f32: e.tensor_copy(
                        d[:, h * 512:(h + 1) * 512], ps[4 + h][:, :]),
                        waits=(tms[h],), inc=(h == 1))
            gset("brow", tv)
            return tv

        def ln_stats(src_ap, eps, par, waits):
            c = cols
            s1 = c[:, par * 12 + 0: par * 12 + 1]
            s2 = c[:, par * 12 + 1: par * 12 + 2]
            mean = c[:, par * 12 + 2: par * 12 + 3]
            msq = c[:, par * 12 + 3: par * 12 + 4]
            var = c[:, par * 12 + 4: par * 12 + 5]
            rstd = c[:, par * 12 + 5: par * 12 + 6]
            nmr = c[:, par * 12 + 6: par * 12 + 7]
            scrap = f32v(LSC[par])
            w = list(waits) + [gw(f"lsc{par}"), gw(f"cols{par}")]
            tv = k.op("vector", lambda e: e.reduce_sum(s1, src_ap, axis=AX.X,
                                                       negate=True),
                      waits=w, inc=1)
            tsq = k.op("scalar", lambda e: e.activation(scrap, src_ap, AF.Square,
                                                        accum_out=s2),
                       waits=w, inc=1)
            gset(f"lsc{par}", tsq)
            t_ = k.op("vector", lambda e: e.tensor_scalar(mean, s1, INV, None,
                                                          op0=ALU.mult),
                      waits=(tsq, tv), inc=1)
            t_ = k.op("vector", lambda e: e.tensor_tensor(msq, mean, mean, op=ALU.mult),
                      waits=(t_,), inc=1)
            t_ = k.op("vector", lambda e: e.tensor_scalar(var, s2, INV, None,
                                                          op0=ALU.mult),
                      waits=(t_,), inc=1)
            tv = k.op("vector", lambda e: e.tensor_sub(var, var, msq),
                      waits=(t_,), inc=1)
            eap = cst[:, 0:1] if eps == EPS_LN else cst[:, 1:2]
            t_ = k.op("scalar", lambda e: e.activation(rstd, var, AF.Ln, bias=eap),
                      waits=(tv, t_cst), inc=1)
            tl = k.op("scalar", lambda e: e.activation(rstd, rstd, AF.Exp, scale=-0.5),
                      waits=(t_,), inc=1)
            tl = k.op("vector", lambda e: e.tensor_tensor(nmr, mean, rstd, op=ALU.mult),
                      waits=(tl,), inc=1)
            return nmr, rstd, tl

        def rms_stats(src_ap, eps, par, waits):
            c = cols
            s2 = c[:, par * 12 + 7: par * 12 + 8]
            rstd = c[:, par * 12 + 8: par * 12 + 9]
            scrap = f32v(LSC[par])
            w = list(waits) + [gw(f"lsc{par}"), gw(f"cols{par}")]
            tsq = k.op("scalar", lambda e: e.activation(scrap, src_ap, AF.Square,
                                                        accum_out=s2), waits=w, inc=1)
            gset(f"lsc{par}", tsq)
            tv = k.op("vector", lambda e: e.tensor_scalar(rstd, s2, INV, None,
                                                          op0=ALU.mult),
                      waits=(tsq,), inc=1)
            eap = cst[:, 0:1] if eps == EPS_LN else cst[:, 1:2]
            t_ = k.op("scalar", lambda e: e.activation(rstd, rstd, AF.Ln, bias=eap),
                      waits=(tv, t_cst), inc=1)
            tl = k.op("scalar", lambda e: e.activation(rstd, rstd, AF.Exp, scale=-0.5),
                      waits=(t_,), inc=1)
            return rstd, tl

        def transpose_chunk(src_bf_ap, ngrp, dst_fn, src_ready, dst_guard=(),
                            dst_contig=None):
            pt2 = ptst["pt2"]
            tev = None
            tev_other = None
            first = True
            nr = (ngrp + 3) // 4
            for r in range(nr):
                q = r * 4
                n = min(4, ngrp - q)
                if pt2 is None:
                    bank, key = pt, "pt0"
                    eng = "scalar" if dst_contig is not None else "vector"
                else:
                    hb = r % 2
                    bank = pt if hb == 0 else pt2
                    key = f"pt{hb}"
                    eng = "scalar" if dst_contig is not None else "vector"
                tmm = None
                for i in range(n):
                    w = list(src_ready) + [t_id, gw(key)]
                    if bank is not pt:
                        w.append(ptst["w"])
                    tmm = k.op("tensor", lambda e, i=i, q=q, bank=bank: e.transpose(
                        bank[:, i * 128:(i + 1) * 128],
                        src_bf_ap[:, (q + i) * 128:(q + i + 1) * 128],
                        ident[:, :]), waits=w, inc=(1 if i == n - 1 else 0))
                xw = list(dst_guard) if first else []
                if r == nr - 1 and tev_other is not None:
                    xw.append(tev_other)
                if dst_contig is not None:
                    if eng == "scalar":
                        fn = lambda e, q=q, n=n, bank=bank: e.copy(
                            dst_contig[:, q * 128:(q + n) * 128], bank[:, 0:n * 128])
                    else:
                        fn = lambda e, q=q, n=n, bank=bank: e.tensor_copy(
                            dst_contig[:, q * 128:(q + n) * 128], bank[:, 0:n * 128])
                    tev_new = k.op(eng, fn, waits=[tmm] + xw, inc=1)
                else:
                    tev_new = None
                    for i in range(n):
                        w = [tmm] + (xw if i == 0 else [])
                        if eng == "scalar":
                            fn = lambda e, i=i, q=q, bank=bank: e.copy(
                                dst_fn(q + i), bank[:, i * 128:(i + 1) * 128])
                        else:
                            fn = lambda e, i=i, q=q, bank=bank: e.tensor_copy(
                                dst_fn(q + i), bank[:, i * 128:(i + 1) * 128])
                        tev_new = k.op(eng, fn, waits=w, inc=(1 if i == n - 1 else 0))
                tev_other = tev
                tev = tev_new
                gset(key, tev_new)
                first = False
            return tev

        wk_ap = lambda dc: WN[dc // 2][:, (dc % 2) * 1024:(dc % 2 + 1) * 1024]
        wv_ap = lambda dc: WN[4 + dc // 2][:, (dc % 2) * 1024:(dc % 2 + 1) * 1024]

        t_bk = k.dma("gpsimd", rowA[:, :], din["abk"][0:1, :])
        t_bv = k.dma("gpsimd", rowB[:, :], din["abv"][0:1, :])

        wk_t, wv_t = [], []
        for dc in range(8):
            wk_t.append(k.dma("gpsimd", wk_ap(dc), din["aWk"][dc * 128:(dc + 1) * 128, :],
                              new_group=(dc == 0)))
        wk_t = [wk_t[-1]] * 8
        for dc in range(8):
            wv_t.append(k.dma("gpsimd", wv_ap(dc), din["aWv"][dc * 128:(dc + 1) * 128, :],
                              new_group=(dc == 0)))
        wv_t = [wv_t[-1]] * 8

        t_projps = [t_bv, t_bv]
        t_vproj_ps = None
        t_kvps = [None, None]
        t_kfps = None

        def a_stats(ch, src, tap0=False):
            rg = ch % 4
            csl = ch % 4
            xc = f32v(CS[csl])
            tld = k.dma("sync", xc, src[ch * 128:(ch + 1) * 128, :],
                        waits=(gw(f"cs{csl}"),))
            mean, rstd, tst = ln_stats(xc, EPS_LN, rg, (tld,))
            lnc_ap = LNC[rg // 2][:, (rg % 2) * 1024:(rg % 2 + 1) * 1024]
            t_ln = k.op("scalar", lambda e, s=xc, m=mean, r=rstd, d=lnc_ap:
                        e.activation(d, s, AF.Identity, scale=r, bias=m),
                        waits=(tst, gw(f"lnc{rg}")), inc=1)
            gset(f"cs{csl}", t_ln)
            gset(f"cols{rg}", t_ln)
            if tap0 and ch == 0:
                tap("lnx0", lnc_ap, t_ln, guard=f"lnc{rg}")
                tap("cols0", cols[:, 0:32], tst, guard=f"cols{rg}")
            return lnc_ap, t_ln

        def a_tr(ch, st, tap0=False):
            rg = ch % 4
            lnc_ap, t_ln = st
            tch_ap = TCH[rg // 2][:, (rg % 2) * 1024:(rg % 2 + 1) * 1024]
            t_tr = transpose_chunk(lnc_ap, 8, None,
                                   (t_ln,), dst_guard=(gw(f"tch{rg}"),),
                                   dst_contig=tch_ap)
            gset(f"lnc{rg}", t_tr)
            if tap0 and ch == 0:
                tap("tch0", tch_ap, t_tr, guard=f"tch{rg}")
            return tch_ap, t_tr

        sta = {c_: a_stats(c_, din["x"], tap0=True) for c_ in range(3)}
        stq = {0: a_tr(0, sta.pop(0), tap0=True)}
        for ch in range(NCH):
            par = ch % 2
            csl = ch % 4
            if ch + 3 < NCH:
                sta[ch + 3] = a_stats(ch + 3, din["x"], tap0=True)
            tch_ap, t_tr = stq.pop(ch)
            tmmk = [None, None]
            for hf in range(2):
                for dc in range(8):
                    w = [t_tr, wk_t[dc]] + ([t_projps[hf]] if dc == 0 else [])
                    k.op("tensor", lambda e, hf=hf, dc=dc, a=tch_ap: e.matmul(
                        ps[4 + hf][:, :], a[:, dc * 128:(dc + 1) * 128],
                        wk_ap(dc)[:, hf * 512:(hf + 1) * 512],
                        start=(dc == 0), stop=False), waits=w)
                tmmk[hf] = k.op("tensor", lambda e, hf=hf: e.matmul(
                    ps[4 + hf][:, :], onec_bf[0:1, :],
                    rowA[0:1, hf * 512:(hf + 1) * 512],
                    start=False, stop=True), waits=(t_bk, t_ones), inc=1)
            eq = f32v(SCR[par])
            t_exp = None
            for hf in range(2):
                t_exp = k.op("scalar", lambda e, hf=hf, d=eq: e.activation(
                    d[:, hf * 512:(hf + 1) * 512], ps[4 + hf][:, :], AF.Exp),
                    waits=(tmmk[hf], gw(f"scr{par}") if hf == 0 else None),
                    inc=1)
            tevk = t_exp
            gset(f"tch{ch % 4}", None) if False else None
            if ch == 0:
                tap("eq0", eq, t_exp, guard=f"scr{par}")
            hs = hsum[:, par * 16:(par + 1) * 16]
            hr = hrcp[:, par * 16:(par + 1) * 16]
            t_hsr = k.op("vector", lambda e, s=eq, d=hs: e.reduce_sum(
                d.rearrange("p (g o) -> p g o", o=1),
                s.rearrange("p (g i) -> p g i", g=16), axis=AX.X),
                waits=(t_exp, gw(f"hs{par}")), inc=1)
            t_hr = k.op("vector", lambda e, s=hs, d=hr: e.reciprocal(d, s),
                        waits=(t_hsr,), inc=1)
            kft_t, kfc_t = KFT[par], KFC[par]
            sin_c = scck_sb[:, ch:ch + 1]
            cos_c = scck_sb[:, 16 + ch:16 + ch + 1]
            hr2 = hrcp[:, par * 16 + 32:(par + 1) * 16 + 32] if False else None
            t_kft = None
            for half, colp in ((0, sin_c), (1, cos_c)):
                hsc = hsum[:, 32 + par * 32 + half * 16:32 + par * 32 + (half + 1) * 16]
                t_s2 = k.op("vector", lambda e, d=hsc, c=colp, hr=hr: e.tensor_scalar(
                    d, hr, c, None, op0=ALU.mult),
                    waits=(t_hr, t_scck, gw(f"kft{par}")), inc=1)
                t_kft = k.op("vector", lambda e, s=eq, d=kft_t, half=half,
                             hv=hsc.unsqueeze(2).broadcast_to([128, 16, 64])
                             : e.tensor_tensor(
                                 d.ap().rearrange("p (h j) -> p h j", j=128)
                                 [:, :, half * 64:(half + 1) * 64],
                                 s.rearrange("p (h j) -> p h j", j=64),
                                 hv, op=ALU.mult), waits=(t_s2,), inc=1)
            gset(f"scr{par}", t_kft)
            gset(f"hs{par}", t_kft)
            t_projps = [tevk, tevk]
            t_kfc = None
            for half, colp in ((0, sin_c), (1, cos_c)):
                t_kfc = k.op("vector", lambda e, d=kfc_t, s=kft_t, c=colp, half=half:
                             e.tensor_scalar(
                                 d.ap().rearrange("p (h j) -> p h j", j=128)
                                 [:, :, half * 64:(half + 1) * 64],
                                 s.ap().rearrange("p (h j) -> p h j", j=128)
                                 [:, :, half * 64:(half + 1) * 64],
                                 c, -1.0, op0=ALU.subtract, op1=ALU.mult),
                             waits=(t_kft, gw(f"kfc{par}")), inc=(half == 1))
            if ch == 0:
                tap("kft0", kft_t.ap(), t_kfc, guard=f"kft{par}")
            tmmv = [None, None]
            for hf in range(2):
                for dc in range(8):
                    w = [t_tr, wv_t[dc]] + ([t_vproj_ps] if dc == 0 else [])
                    k.op("tensor", lambda e, hf=hf, dc=dc, a=tch_ap: e.matmul(
                        ps[2 + hf][:, :], a[:, dc * 128:(dc + 1) * 128],
                        wv_ap(dc)[:, hf * 512:(hf + 1) * 512],
                        start=(dc == 0), stop=False), waits=w)
                tmmv[hf] = k.op("tensor", lambda e, hf=hf: e.matmul(
                    ps[2 + hf][:, :], onec_bf[0:1, :],
                    rowB[0:1, hf * 512:(hf + 1) * 512],
                    start=False, stop=True), waits=(t_bv, t_ones), inc=1)
            vex_t = VEX[par]
            t_vx = None
            for hf in range(2):
                t_vx = k.op("scalar", lambda e, hf=hf, d=vex_t: e.copy(
                    d[:, hf * 512:(hf + 1) * 512], ps[2 + hf][:, :]),
                    waits=(tmmv[hf], gw(f"vex{par}") if hf == 0 else None),
                    inc=(hf == 1))
            t_vproj_ps = t_vx
            gset(f"tch{ch % 4}", t_vx)
            for batch in range(2):
                last_mm = None
                for hh in range(8):
                    h = batch * 8 + hh
                    for br in range(2):
                        t = h * 2 + br
                        tloc = hh * 2 + br
                        lhs = (kft_t if br == 0 else kfc_t)[:, h * 128:(h + 1) * 128]
                        w = [t_kfc, t_vx]
                        if hh == 0 and br == 0:
                            w += [t_kvps[0] if batch == 1 else t_kvps[1], t_kfps]
                        k.op("tensor", lambda e, tloc=tloc, lhs=lhs, h=h, v=vex_t:
                             e.matmul(
                            ps[tloc // 8][:, (tloc % 8) * 64:(tloc % 8 + 1) * 64],
                            lhs, v[:, h * 64:(h + 1) * 64],
                            start=True, stop=True), waits=w)
                        last_mm = k.op("tensor", lambda e, t=t, lhs=lhs: e.matmul(
                            ps[6][:, t:t + 1], lhs, onescol[:, 0:1],
                            start=True, stop=True),
                            inc=(1 if (hh == 7 and br == 1) else 0))
                tacc = None
                for b in range(2):
                    dst = f32v(KVA[batch])
                    off = b * 512
                    tacc = k.op("vector", lambda e, b=b, dst=dst, off=off:
                                e.tensor_add(dst[:, off:off + 512],
                                             dst[:, off:off + 512], ps[b][:, :]),
                                waits=(last_mm, t_kvz), inc=(b == 1))
                t_kvps[batch] = tacc
                if batch == 1:
                    t_kfps = k.op("vector", lambda e: e.tensor_add(
                        kfacc[:, :], kfacc[:, :], ps[6][:, 0:32]),
                        waits=(last_mm,), inc=1)
            if ch == 0:
                tap("vex0", vex_t.ap(), t_vx, guard=f"vex{par}")
            gset(f"kft{par}", t_kfps)
            gset(f"kfc{par}", t_kfps)
            gset(f"vex{par}", t_kfps)
            if ch + 1 < NCH:
                stq[ch + 1] = a_tr(ch + 1, sta.pop(ch + 1), tap0=True)

        if "kvacc" in taps:
            k.dma("sync", tap_d["kvacc"][:, 0:1024], f32v(KVA[0]),
                  waits=(t_kvps[0], t_kvps[1]))
            k.dma("sync", tap_d["kvacc"][:, 1024:2048], f32v(KVA[1]),
                  waits=(t_kvps[0], t_kvps[1]))
        tap("kfacc32", kfacc[:, :], t_kfps)
        kvbf0 = SCR[0]
        t1 = k.op("vector", lambda e: e.tensor_copy(kvbf0[:, 0:1024], f32v(KVA[0])),
                  waits=(t_kvps[0], t_kvps[1], gw("scr0")), inc=1)
        t1 = k.op("vector", lambda e: e.tensor_copy(kvbf0[:, 1024:2048], f32v(KVA[1])),
                  inc=1)
        t2 = k.op("vector", lambda e: e.tensor_copy(kfbf[:, :], kfacc[:, :]),
                  waits=(t_kfps,), inc=1)
        tci1 = k.dma("sync", cc_in[:, 0:2048], kvbf0[:, :], waits=(t1,))
        tci2 = k.dma("sync", cc_in[:, 2048:2080], kfbf[:, :], waits=(t2,),
                     new_group=False)
        gset("scr0", tci1)

        def do_cc(e):
            return e.collective_compute(
                "AllGather", ALU.bypass, replica_groups=[list(range(8))],
                ins=[cc_in.ap().opt()], outs=[cc_out.ap().opt()]).then_inc(cc_sem, 1)
        k.op("gpsimd", do_cc, waits=(tci1, tci2), inc=0)
        t_cc = None

        _ps6g.__exit__(None, None, None)
        _pt2g = nc.psum_tensor("pt2", [128, 1024], BF16)
        ptst["pt2"] = _pt2g.__enter__()
        ptst["w"] = t_kfps


        pool.free("kft", "kfc", "vex", "kva")
        QFTT = pool.alloc("qftt", 16)
        QFC = pool.alloc("qfc", 2)
        last_kv_mm = k.tok("tensor")
        wq_t = []
        for dc in range(8):
            wq_t.append(k.dma("gpsimd", wk_ap(dc),
                              din["aWq"][dc * 128:(dc + 1) * 128, :],
                              waits=(last_kv_mm,), new_group=(dc == 0)))
        wq_t = [wq_t[-1]] * 8
        t_bq = k.dma("gpsimd", rowA[:, :], din["abq"][0:1, :],
                     waits=(last_kv_mm,))
        sta = {c_: a_stats(c_, din["xq"]) for c_ in range(3)}
        stq = {0: a_tr(0, sta.pop(0))}
        pend_q = []
        for ch in range(NCH):
            par = ch % 2
            csl = ch % 4
            if ch + 3 < NCH:
                sta[ch + 3] = a_stats(ch + 3, din["xq"])
            tch_ap, t_tr = stq.pop(ch)
            tmmq = [None, None]
            for hf in range(2):
                for dc in range(8):
                    w = [t_tr, wq_t[dc]] + ([t_projps[hf]] if dc == 0 else [])
                    k.op("tensor", lambda e, hf=hf, dc=dc, a=tch_ap: e.matmul(
                        ps[4 + hf][:, :], a[:, dc * 128:(dc + 1) * 128],
                        wk_ap(dc)[:, hf * 512:(hf + 1) * 512],
                        start=(dc == 0), stop=False), waits=w)
                tmmq[hf] = k.op("tensor", lambda e, hf=hf: e.matmul(
                    ps[4 + hf][:, :], onec_bf[0:1, :],
                    rowA[0:1, hf * 512:(hf + 1) * 512],
                    start=False, stop=True), waits=(t_bq, t_ones), inc=1)
            if pend_q:
                qsrc, pch, tq = pend_q.pop(0)
                t_trq = transpose_chunk(qsrc.ap(), 16, None, (tq,),
                                        dst_contig=QFTT[pch].ap())
                gset(f"qfc{pch % 2}", t_trq)
            eq = f32v(SCR[par])
            t_exp = None
            for hf in range(2):
                t_exp = k.op("scalar", lambda e, hf=hf, d=eq: e.activation(
                    d[:, hf * 512:(hf + 1) * 512], ps[4 + hf][:, :], AF.Exp),
                    waits=(tmmq[hf], gw(f"scr{par}") if hf == 0 else None),
                    inc=1)
            tevq = t_exp
            t_projps = [tevq, tevq]
            gset(f"tch{ch % 4}", tevq)
            hs = hsum[:, par * 16:(par + 1) * 16]
            hr = hrcp[:, par * 16:(par + 1) * 16]
            t_hsr = k.op("vector", lambda e, s=eq, d=hs: e.reduce_sum(
                d.rearrange("p (g o) -> p g o", o=1),
                s.rearrange("p (g i) -> p g i", g=16), axis=AX.X),
                waits=(t_exp, gw(f"hs{par}")), inc=1)
            t_hr = k.op("vector", lambda e, s=hs, d=hr: e.reciprocal(d, s),
                        waits=(t_hsr,), inc=1)
            qf_t = QFC[par]
            sin_c = sccq_sb[:, ch:ch + 1]
            cos_c = sccq_sb[:, 16 + ch:16 + ch + 1]
            t_qft = None
            for half, colp in ((0, sin_c), (1, cos_c)):
                hsc = hsum[:, 32 + par * 32 + half * 16:32 + par * 32 + (half + 1) * 16]
                t_s2 = k.op("vector", lambda e, d=hsc, c=colp, hr=hr: e.tensor_scalar(
                    d, hr, c, None, op0=ALU.mult),
                    waits=(t_hr, gw(f"qfc{par}")), inc=1)
                t_qft = k.op("vector", lambda e, s=eq, d=qf_t, half=half,
                             hv=hsc.unsqueeze(2).broadcast_to([128, 16, 64])
                             : e.tensor_tensor(
                                 d.ap().rearrange("p (h j) -> p h j", j=128)
                                 [:, :, half * 64:(half + 1) * 64],
                                 s.rearrange("p (h j) -> p h j", j=64),
                                 hv, op=ALU.mult), waits=(t_s2,), inc=1)
            gset(f"scr{par}", t_qft)
            gset(f"hs{par}", t_qft)
            pend_q.append((qf_t, ch, t_qft))
            if ch + 1 < NCH:
                stq[ch + 1] = a_tr(ch + 1, sta.pop(ch + 1))
        while pend_q:
            qsrc, pch, tq = pend_q.pop(0)
            t_trq = transpose_chunk(qsrc.ap(), 16, None, (tq,),
                                    dst_contig=QFTT[pch].ap())
            gset(f"qfc{pch % 2}", t_trq)
        t_qftT_done = k.tok("scalar")

        pool.free("tch", "lnc", "qfc", "cs")
        KVP = pool.alloc("kvp", 8)
        k.q["sync"].append(lambda eng: eng.wait_ge(cc_sem, 1))
        t_a3_all = (k.tok("tensor"), k.tok("vector"), k.tok("scalar"))
        t_kvall = []
        for b in range(8):
            t_kvall.append(k.dma("sync", KVP[b].ap(),
                                 cc_out[b * 128:(b + 1) * 128, 0:2048],
                                 waits=t_a3_all, new_group=(b == 0)))
        t_kvf = []
        for b in range(8):
            t_kvf.append(k.dma("sync", kvfs[:, b * 32:(b + 1) * 32],
                               cc_out[b * 128:(b + 1) * 128, 2048:2080],
                               waits=t_a3_all, new_group=False))
        t_all16 = t_kvf[-1]
        t_kvall = [t_all16] * 8
        t_kvf = [t_all16] * 8

        tap("kvall0", KVP[0].ap(), t_kvall[0])
        tap("qftt0", QFTT[0].ap(), t_qftT_done if False else k.tok("vector"))

        k.barrier()
        pool.free("bc", "wn", "lsce")
        SCB = pool.alloc("scb", 1)
        CMP = pool.alloc("cmp", 2)
        ATN = pool.alloc("atn", 8)

        t_scb = None
        t_ps4 = t_projps[0]
        for blk in range(4):
            tm = None
            for half, row in ((0, srow_sin), (1, srow_cos)):
                tm = k.op("tensor", lambda e, blk=blk, half=half, row=row: e.matmul(
                    ps[4][half * 64:(half + 1) * 64, :], onec_bf[0:1, 0:64],
                    row[0:1, blk * 512:(blk + 1) * 512], start=True, stop=True),
                    waits=(t_srs, t_src, t_ones, t_ps4), inc=(half == 1))
            t_scb = k.op("vector", lambda e, blk=blk: e.tensor_copy(
                SCB[0][:, blk * 512:(blk + 1) * 512], ps[4][:, :]),
                waits=(tm,), inc=1)
            t_ps4 = t_scb

        t_mainps = [None, None]
        t_denps = None
        t_atn = None
        for hg in range(2):
            for ch in range(NCH):
                b0 = ch // 2
                rot = ch % 2
                pm, pc_ = ps[rot], ps[2 + rot]
                cmp_ap = CMP[rot].ap()
                t_cmp = k.op("vector", lambda e, ch=ch, hg=hg, d=cmp_ap:
                             e.tensor_tensor(
                    d[:, 0:1024].rearrange("p (h j) -> p h j", j=128),
                    SCB[0][:, ch * 128:(ch + 1) * 128].unsqueeze(1)
                    .broadcast_to([128, 8, 128]),
                    QFTT[ch][:, hg * 1024:(hg + 1) * 1024]
                    .rearrange("p (h j) -> p h j", j=128),
                    op=ALU.subtract),
                    waits=(t_scb, t_qftT_done, gw(f"cmp{rot}")), inc=1)
                last_mm = None
                for hh in range(8):
                    h = hg * 8 + hh
                    lhs_m = QFTT[ch][:, h * 128:(h + 1) * 128]
                    lhs_c = cmp_ap[:, hh * 128:(hh + 1) * 128]
                    for br, (lhs, pb) in enumerate(((lhs_m, pm), (lhs_c, pc_))):
                        t = h * 2 + br
                        w = [t_cmp, t_kvall[b0], t_kvf[b0]]
                        if hh == 0 and br == 0:
                            w += [t_mainps[rot], t_denps]
                        k.op("tensor", lambda e, t=t, lhs=lhs, pb=pb, hh=hh, b0=b0:
                             e.matmul(pb[:, hh * 64:(hh + 1) * 64], lhs,
                                      KVP[b0][:, t * 64:(t + 1) * 64],
                                      start=True, stop=True), waits=w)
                        last_mm = k.op("tensor",
                                       lambda e, t=t, lhs=lhs, hh=hh, b0=b0, br=br:
                                       e.matmul(ps[4][:, hh * 2 + br:hh * 2 + br + 1],
                                                lhs,
                                                kvfs[:, b0 * 32 + t:b0 * 32 + t + 1],
                                                start=True, stop=True),
                                       inc=(1 if (hh == 7 and br == 1) else 0))
                gset(f"cmp{rot}", last_mm)
                dc_ = dencol[:, rot * 16:(rot + 1) * 16]
                zc_ = zcol[:, rot * 16:(rot + 1) * 16]
                t_dn = k.op("vector", lambda e, d=dc_: e.tensor_scalar(
                    d, ps[4][:, 0:16], EPS_Z, None, op0=ALU.max),
                    waits=(last_mm, gw(f"z{rot}")), inc=1)
                t_denps = k.op("vector", lambda e, d=dc_, z=zc_: e.reciprocal(z, d),
                               waits=(t_dn,), inc=1)
                u = f32v(SCR[rot])
                z1v = zc_[:, 0:16:2].unsqueeze(2).broadcast_to([128, 8, 64])
                z2v = zc_[:, 1:16:2].unsqueeze(2).broadcast_to([128, 8, 64])
                k.op("vector", lambda e, u=u, pc_=pc_, z2v=z2v: e.tensor_tensor(
                    u[:, 0:512].rearrange("p (h j) -> p h j", j=64),
                    pc_[:, :].rearrange("p (h j) -> p h j", j=64), z2v, op=ALU.mult),
                    waits=(gw(f"scr{rot}"), t_denps))
                t_u = k.op("vector", lambda e, u=u, pm=pm, z1v=z1v: e.tensor_tensor(
                    u[:, 512:1024].rearrange("p (h j) -> p h j", j=64),
                    pm[:, :].rearrange("p (h j) -> p h j", j=64), z1v, op=ALU.mult),
                    inc=1)
                t_mainps[rot] = t_u
                gset(f"z{rot}", t_u)
                piece = LSC[rot][:, 0:512]
                t_ab = k.op("vector", lambda e, u=u, piece=piece: e.tensor_add(
                    piece, u[:, 512:1024], u[:, 0:512]),
                    waits=(gw(f"lsc{rot}"), t_u), inc=1)
                gset(f"scr{rot}", t_ab)
                t_atn = transpose_chunk(
                    piece, 4,
                    lambda i, hg=hg, ch=ch: ATN[hg * 4 + i][:, ch * 128:(ch + 1) * 128],
                    (t_ab,))
                gset(f"lsc{rot}", t_atn)

        tap("atn0", ATN[0].ap(), t_atn)

        k.barrier()
        pool.free("qftt", "scb", "cmp", "kvp", "tch")
        BIG16 = pool.alloc("big16", 16)
        CS = pool.alloc("cs", 2)
        WN2 = pool.alloc("wn2", 4)
        BC = pool.alloc("bc", 1)
        NXT = pool.alloc("nxt", 8)
        LNC = pool.alloc("lnc", 1)
        WKV = pool.alloc("wkv", 1)
        BCv = [f32v(BC[0])]
        for key in list(G):
            if key.startswith("cs") or key.startswith("bc") or key.startswith("lnc"):
                G.pop(key)
        wo_ap = lambda ac: WN2[ac // 2][:, (ac % 2) * 1024:(ac % 2 + 1) * 1024]
        wo_t = []
        for ac in range(8):
            wo_t.append(k.dma("gpsimd", wo_ap(ac),
                              din["aWo"][ac * 128:(ac + 1) * 128, :],
                              new_group=(ac == 0)))
        wo_t = [wo_t[-1]] * 8
        t_bo = k.dma("gpsimd", rowA[:, :], din["abo"][0:1, :])
        prep_done = {}

        def prep_tok(row0, rows):
            best = None
            for r, (t, n) in prep_done.items():
                if row0 < r + n and r < row0 + rows:
                    if best is None or t[1] > best[1]:
                        best = t
            return best

        early_prep = [
            (din["fWkc"][:, :], R_FWKC, 256),
            (din["fWvc"][:, :], R_FWVC, 256),
            (din["fWq"][0:256, :], R_FWQ, 256),
            (din["fWq"][256:512, :], R_FWQ + 256, 256),
            (din["fWq"][512:768, :], R_FWQ + 512, 256),
            (din["fWq"][768:1024, :], R_FWQ + 768, 256),
        ]
        t_g1 = bias_bcast(BCv[0], din["g1"][0:1, :])
        t_projps = [t_g1, t_g1]
        t_nxt = None
        pend_x = []
        for ch in range(NCH):
            par = ch % 2
            csl = ch % 2
            half_ap = LNC[0][:, par * 1024:(par + 1) * 1024]
            t_xr = k.dma("gpsimd", half_ap, din["xr"][ch * 128:(ch + 1) * 128, :],
                         waits=(gw(f"half{par}"),))
            tmmo = [None, None]
            for hf in range(2):
                for ac in range(8):
                    w = [t_atn, wo_t[ac]] + ([t_projps[hf]] if ac == 0 else [])
                    k.op("tensor", lambda e, hf=hf, ac=ac, ch=ch: e.matmul(
                        ps[4 + hf][:, :], ATN[ac][:, ch * 128:(ch + 1) * 128],
                        wo_ap(ac)[:, hf * 512:(hf + 1) * 512],
                        start=(ac == 0), stop=False), waits=w)
                tmmo[hf] = k.op("tensor", lambda e, hf=hf: e.matmul(
                    ps[4 + hf][:, :], onec_bf[0:1, :],
                    rowA[0:1, hf * 512:(hf + 1) * 512],
                    start=False, stop=True), waits=(t_bo, t_ones), inc=1)
            if pend_x:
                ha, pch, tnx_p = pend_x.pop(0)
                t_nxt = transpose_chunk(ha, 8,
                                        lambda i, pch=pch: NXT[i][:, pch * 128:
                                                                  (pch + 1) * 128],
                                        (tnx_p,))
                gset(f"half{pch % 2}", t_nxt)
            oc_ = f32v(CS[csl])
            tev = None
            for hf in range(2):
                tev = k.op("vector", lambda e, hf=hf, d=oc_: e.tensor_copy(
                    d[:, hf * 512:(hf + 1) * 512], ps[4 + hf][:, :]),
                    waits=(tmmo[hf], gw(f"cs{csl}")), inc=(hf == 1))
            t_projps = [tev, tev]
            mean, rstd, tst = ln_stats(oc_, EPS_LN, par, (tev,))
            nrm = f32v(SCR[par])
            t_n = k.op("scalar", lambda e, s=oc_, m=mean, r=rstd, d=nrm:
                       e.activation(d, s, AF.Identity, scale=r, bias=m),
                       waits=(tst, gw(f"scr{par}")), inc=1)
            gset(f"cs{csl}", t_n)
            gset(f"cols{par}", t_n)
            t_x2 = k.op("vector", lambda e, d=f32v(BIG16[ch]), n=nrm, x=half_ap:
                        e.tensor_add(d, n, x), waits=(t_xr, t_n), inc=1)
            gset(f"scr{par}", t_x2)
            if "x2" in taps:
                tap("x2_pc", None, None)
                t_tap = k.dma("sync", tap_d["x2"][ch * 128:(ch + 1) * 128, :],
                              f32v(BIG16[ch]), waits=(t_x2,))
            else:
                t_tap = None
            t_spd = k.dma("gpsimd", xsp[ch * 128:(ch + 1) * 128, :], f32v(BIG16[ch]),
                          waits=(t_x2,))
            if early_prep:
                esrc, erow, erows = early_prep.pop(0)
                tsave = k.dma("gpsimd", wsc[erow:erow + erows, :], esrc)
                prep_done[erow] = (tsave, erows)
            mean2, rstd2_, tst2 = ln_stats(f32v(BIG16[ch]), EPS_LN, par, (t_x2,))
            t_ln2 = k.op("vector", lambda e, s=f32v(BIG16[ch]), m=mean2, r=rstd2_:
                         e.tensor_scalar(s, s, m, r, op0=ALU.add, op1=ALU.mult),
                         waits=(tst2, t_spd, t_tap), inc=1)
            gset(f"cols{par}", t_ln2)
            var2 = cols[:, par * 12 + 4: par * 12 + 5]
            ms_ = cols[:, par * 12 + 7: par * 12 + 8]
            rstd1 = cols[:, par * 12 + 8: par * 12 + 9]
            t_ = k.op("vector", lambda e, v=var2, r=rstd2_, d=ms_: e.tensor_tensor(
                d, v, r, op=ALU.mult), waits=(tst2,), inc=1)
            t_ = k.op("vector", lambda e, r=rstd2_, d=ms_: e.tensor_tensor(
                d, d, r, op=ALU.mult), waits=(t_,), inc=1)
            t_ = k.op("scalar", lambda e, d=rstd1, s=ms_: e.activation(
                d, s, AF.Ln, bias=cst[:, 1:2]), waits=(t_, t_cst), inc=1)
            trs = k.op("scalar", lambda e, d=rstd1: e.activation(
                d, d, AF.Exp, scale=-0.5), waits=(t_,), inc=1)
            scr = f32v(SCR[par])
            t_sc = k.op("vector", lambda e, s=f32v(BIG16[ch]), r=rstd1, d=scr:
                        e.tensor_scalar(d, s, r, None, op0=ALU.mult),
                        waits=(trs, t_ln2, gw(f"scr{par}")), inc=1)
            t_nx = k.op("vector", lambda e, s=scr, d=half_ap, b=BCv[0]: e.tensor_tensor(
                d, s, b, op=ALU.mult), waits=(t_g1, t_x2, t_sc), inc=1)
            gset(f"scr{par}", t_nx)
            gset(f"cols{par}", t_nx)
            pend_x.append((half_ap, ch, t_nx))
        while pend_x:
            ha, pch, tnx_p = pend_x.pop(0)
            t_nxt = transpose_chunk(ha, 8,
                                    lambda i, pch=pch: NXT[i][:, pch * 128:
                                                              (pch + 1) * 128],
                                    (tnx_p,))
            gset(f"half{pch % 2}", t_nxt)
        wkct = []
        for dc in range(8):
            wkct.append(k.dmat("sync", WKV[0][:, dc * 256:(dc + 1) * 256],
                               wsc[R_FWKC:R_FWKC + 256, dc * 128:(dc + 1) * 128],
                               waits=(prep_tok(R_FWKC, 256),), new_group=(dc == 0)))
        wkct = [wkct[-1]] * 8
        if "xff" in taps:
            pass

        late_jobs = [(din["fWo"][:, :], R_FWO, DM), (din["sW1"][:, :], R_SW1, DM),
                     (din["sW2"][:, :], R_SW2, DM), (din["sW3"][:, :], R_SW3, DM)]
        for e_i in range(NE):
            for nm, base in (("eW1", R_EW1), ("eW2", R_EW2), ("eW3", R_EW3)):
                late_jobs.append((din[nm][e_i, :, :], base + e_i * 1024, DM))
        for src, row, rows in late_jobs:
            tsave = k.dma("gpsimd", wsc[row:row + rows, :], src, dom="prep")
            prep_done[row] = (tsave, rows)

        k.barrier()
        pool.free("atn", "wn2", "cs", "bc", "lnc")
        KCT = pool.alloc("kct", 2)
        VX2 = pool.alloc("vx2", 4)
        QT = pool.alloc("qt", 8)
        WKV2 = pool.alloc("wkv2", 1)
        WQR = pool.alloc("wqr", 1)
        wvct = []
        for dc in range(8):
            wvct.append(k.dmat("sync", WKV2[0][:, dc * 256:(dc + 1) * 256],
                               wsc[R_FWVC:R_FWVC + 256, dc * 128:(dc + 1) * 128],
                               waits=(prep_tok(R_FWVC, 256),),
                               new_group=(dc == 0)))
        wvct = [wvct[-1]] * 8
        wq_pre = {}
        for oc in (0, 1):
            wdst_ = WQR[0][:, (oc % 2) * 1024:(oc % 2 + 1) * 1024]
            tq_ = []
            for dc in range(8):
                tq_.append(k.dmat("sync", wdst_[:, dc * 128:(dc + 1) * 128],
                                  wsc[R_FWQ + oc * 128:R_FWQ + (oc + 1) * 128,
                                      dc * 128:(dc + 1) * 128],
                                  waits=(prep_tok(R_FWQ + oc * 128, 128),),
                                  new_group=(dc == 0)))
            wq_pre[oc] = [tq_[-1]] * 8
        t_bps = [t_projps[0], t_projps[1]]
        for kc2 in range(2):
            for sb in range(4):
                rot = sb % 2
                last = None
                for dc in range(8):
                    w = [t_nxt, wkct[dc]] + ([t_bps[rot]] if dc == 0 else [])
                    last = k.op("tensor", lambda e, kc2=kc2, sb=sb, dc=dc, rot=rot:
                                e.matmul(ps[rot][:, :],
                                         WKV[0][:, dc * 256 + kc2 * 128:
                                                dc * 256 + (kc2 + 1) * 128],
                                         NXT[dc][:, sb * 512:(sb + 1) * 512],
                                         start=(dc == 0), stop=(dc == 7)),
                                waits=w, inc=(dc == 7))
                t_bps[rot] = k.op("vector", lambda e, kc2=kc2, sb=sb, rot=rot:
                                  e.tensor_copy(KCT[kc2][:, sb * 512:(sb + 1) * 512],
                                                ps[rot][:, :]),
                                  waits=(last,), inc=1)
                t_kc_last = last
        for ch in range(NCH):
            rot = ch % 2
            vx = VX2[ch // 4][:, (ch % 4) * 512:(ch % 4) * 512 + 257]
            last = None
            for dc in range(8):
                w = [t_nxt, wvct[dc]] + ([t_bps[rot]] if dc == 0 else [])
                last = k.op("tensor", lambda e, ch=ch, dc=dc, rot=rot: e.matmul(
                    ps[rot][:, 0:256], NXT[dc][:, ch * 128:(ch + 1) * 128],
                    WKV2[0][:, dc * 256:(dc + 1) * 256],
                    start=(dc == 0), stop=(dc == 7)), waits=w, inc=(dc == 7))
            k.op("vector", lambda e, vx=vx: e.memset(vx[:, 256:257], 1.0))
            t_bps[rot] = k.op("vector", lambda e, vx=vx, rot=rot: e.tensor_copy(
                vx[:, 0:256], ps[rot][:, 0:256]), waits=(last,), inc=1)
        for oc in range(8):
            wpar = oc % 2
            wdst = WQR[0][:, wpar * 1024:(wpar + 1) * 1024]
            if oc in wq_pre:
                tq = wq_pre[oc]
            else:
                stg = WKV[0][:, wpar * 1024:(wpar + 1) * 1024]
                w = [prep_tok(R_FWQ + oc * 128, 128), gw(f"wqs{wpar}")]
                if oc < 4:
                    w.append(t_kc_last)
                t_st = k.dma("sync", stg,
                             wsc[R_FWQ + oc * 128:R_FWQ + (oc + 1) * 128, :],
                             waits=w)
                t_tr8 = transpose_chunk(stg, 8,
                                        lambda i, wdst=wdst: wdst[:, i * 128:
                                                                  (i + 1) * 128],
                                        (t_st,), dst_guard=(gw(f"wqr{wpar}"),))
                gset(f"wqs{wpar}", t_tr8)
                tq = [t_tr8] * 8
            for sb in range(4):
                rot = sb % 2
                last = None
                for dc in range(8):
                    w = [t_nxt, tq[dc]] + ([t_bps[rot]] if dc == 0 else [])
                    last = k.op("tensor",
                                lambda e, sb=sb, dc=dc, rot=rot, wdst=wdst: e.matmul(
                                    ps[rot][:, :], wdst[:, dc * 128:(dc + 1) * 128],
                                    NXT[dc][:, sb * 512:(sb + 1) * 512],
                                    start=(dc == 0), stop=(dc == 7)),
                                waits=w, inc=(dc == 7))
                t_bps[rot] = k.op("vector", lambda e, oc=oc, sb=sb, rot=rot:
                                  e.tensor_copy(QT[oc][:, sb * 512:(sb + 1) * 512],
                                                ps[rot][:, :]),
                                  waits=(last,), inc=1)
                gset(f"wqr{wpar}", t_bps[rot])
        t_b1 = k.tok("vector")

        k.barrier()
        pool.free("nxt", "wkv", "wkv2", "wqr", "scr")
        PTL = pool.alloc("ptl", 4)
        A2T = pool.alloc("a2t", 8)
        t_scps = [None, None]
        t_avps = [None, None]
        t_ptl_guard = None
        t_a2t = None
        for h in range(H2):
            for sb in range(4):
                t_pexp = [None] * 16
                for tc in range(16):
                    rot = tc % 2
                    last = None
                    for kc2 in range(2):
                        w = [t_b1] + ([t_scps[rot], t_ptl_guard] if kc2 == 0 else [])
                        last = k.op("tensor",
                                    lambda e, h=h, sb=sb, tc=tc, kc2=kc2, rot=rot:
                                    e.matmul(ps[rot][:, :],
                                             KCT[kc2][:, tc * 128:(tc + 1) * 128],
                                             QT[2 * h + kc2][:, sb * 512:(sb + 1) * 512],
                                             start=(kc2 == 0), stop=(kc2 == 1)),
                                    waits=w, inc=(kc2 == 1))
                    dst = PTL[tc // 4][:, (tc % 4) * 512:(tc % 4 + 1) * 512]
                    t_pexp[tc] = k.op("scalar", lambda e, dst=dst, rot=rot: e.activation(
                        dst, ps[rot][:, :], AF.Exp, scale=0.0625),
                        waits=(last,), inc=1)
                    t_scps[rot] = t_pexp[tc]
                for ss in range(4):
                    ch = sb * 4 + ss
                    rot = ss % 2
                    last = None
                    for tc in range(16):
                        w = [t_pexp[tc]] + ([t_avps[rot]] if tc == 0 else [])
                        last = k.op("tensor", lambda e, ss=ss, tc=tc, rot=rot: e.matmul(
                            ps[2 + rot][:, 0:257],
                            PTL[tc // 4][:, (tc % 4) * 512 + ss * 128:
                                         (tc % 4) * 512 + (ss + 1) * 128],
                            VX2[tc // 4][:, (tc % 4) * 512:(tc % 4) * 512 + 257],
                            start=(tc == 0), stop=(tc == 15)), waits=w, inc=(tc == 15))
                    zz = avz[:, rot:rot + 1]
                    t_zz = k.op("vector", lambda e, zz=zz, rot=rot: e.reciprocal(
                        zz, ps[2 + rot][:, 256:257]),
                        waits=(last, gw(f"avz{rot}")), inc=1)
                    piece = LSC[rot][:, 0:256]
                    t_pc = k.op("vector", lambda e, piece=piece, zz=zz, rot=rot:
                                e.tensor_scalar(piece, ps[2 + rot][:, 0:256],
                                                zz, None, op0=ALU.mult),
                                waits=(gw(f"lsc{rot}"), t_zz), inc=1)
                    t_avps[rot] = t_pc
                    gset(f"avz{rot}", t_pc)
                    t_a2t = transpose_chunk(
                        piece, 2, None, (t_pc,),
                        dst_contig=A2T[ch // 2][:, (ch % 2) * 1024 + 2 * h * 128:
                                               (ch % 2) * 1024 + (2 * h + 2) * 128])
                    gset(f"lsc{rot}", t_a2t)
            t_ptl_guard = t_a2t

        k.barrier()
        pool.free("kct", "vx2", "qt", "ptl")
        WOT = pool.alloc("wot", 4)
        NX2T = pool.alloc("nx2t", 8)
        LNC = pool.alloc("lnc2", 1)
        SCR = pool.alloc("scr", 2)
        BC = pool.alloc("bc", 2)
        BCv = [f32v(BC[0]), f32v(BC[1])]
        for key_ in list(G):
            if key_.startswith("scr") or key_.startswith("bc"):
                G.pop(key_)
        wot_ap = lambda ac: WOT[ac // 2][:, (ac % 2) * 1024:(ac % 2 + 1) * 1024]
        wot_t = []
        for ac in range(8):
            wot_t.append(k.dmat("sync", wot_ap(ac),
                                wsc[R_FWO:R_FWO + 1024, ac * 128:(ac + 1) * 128],
                                waits=(prep_tok(R_FWO, 1024),), new_group=(ac == 0)))
        wot_t = [wot_t[-1]] * 8
        t_g2 = bias_bcast(BCv[1], din["g2"][0:1, :])
        tl = k.dma("sync", brow[:, 0:8], din["febias"][0:1, :], waits=(gw("brow"),))
        tm = k.op("tensor", lambda e: e.matmul(ps[4][:, 0:8], onec_f[0:1, :],
                                               brow[0:1, 0:8], start=True, stop=True),
                  waits=(tl, t_g2), inc=1)
        t_eb = k.op("vector", lambda e: e.tensor_copy(eb_bc[:, :], ps[4][:, 0:8]),
                    waits=(tm,), inc=1)
        gset("brow", t_eb)
        t_wr = k.dma("gpsimd", wrbf[:, :], din["fWr"][:, :])
        t_wrt = None
        for dc in range(8):
            tmw = k.op("tensor", lambda e, dc=dc: e.transpose(
                pt[:, 0:8], wrbf[:, dc * 128:(dc + 1) * 128], ident[0:8, 0:8]),
                waits=(t_wr, t_id, gw("pt0")), inc=1)
            t_wrt = k.op("vector", lambda e, dc=dc: e.tensor_copy(
                wrt[:, dc * 8:(dc + 1) * 8], pt[:, 0:8]), waits=(tmw,), inc=1)
            gset("pt0", t_wrt)

        t_xff = [None] * NCH
        st = {"rps": t_eb}
        t_comb_ch = [None] * NCH
        t_ct = None
        pend_b = []

        def c1_tail(pch, tnx_p, nxc_p):
            par = pch % 2
            ch = pch
            t_nx = tnx_p
            t_nx2 = transpose_chunk(nxc_p, 8,
                                    lambda i, ch=ch: NX2T[i][:, ch * 128:(ch + 1) * 128],
                                    (t_nx,))
            gset(f"l2c{par}", t_nx2)
            o = par * 96
            last = None
            for dc in range(8):
                w = [t_nx2, t_wrt] + ([st["rps"]] if dc == 0 else [])
                last = k.op("tensor", lambda e, ch=ch, dc=dc: e.matmul(
                    ps[4][:, 0:8], NX2T[dc][:, ch * 128:(ch + 1) * 128],
                    wrt[:, dc * 8:(dc + 1) * 8],
                    start=(dc == 0), stop=(dc == 7)), waits=w, inc=(dc == 7))
            lg = rt[:, o:o + 8]
            rw = rt[:, o + 8:o + 16]
            i1 = rt[:, o + 16:o + 24]
            rw2 = rt[:, o + 24:o + 32]
            i2 = rt[:, o + 32:o + 40]
            t3 = rt[:, o + 40:o + 48]
            nm = rt[:, o + 80:o + 81]
            sm = rt[:, o + 81:o + 82]
            rs = rt[:, o + 82:o + 83]
            m1 = rt[:, o + 83:o + 84]
            m2 = rt[:, o + 84:o + 85]
            dd = rt[:, o + 85:o + 86]
            w1 = rt[:, o + 86:o + 87]
            w2 = rt[:, o + 87:o + 88]
            t1_ = k.op("vector", lambda e, lg=lg: e.tensor_add(lg, ps[4][:, 0:8],
                                                               eb_bc[:, :]),
                       waits=(last, t_eb, gw(f"rt{par}")), inc=1)
            st["rps"] = t1_
            t_ = k.op("vector", lambda e, lg=lg, nm=nm: e.reduce_max(
                nm, lg, axis=AX.X, negate=True), waits=(t1_,), inc=1)
            tsf = k.op("scalar", lambda e, lg=lg, nm=nm, rw=rw, sm=sm: e.activation(
                rw, lg, AF.Exp, bias=nm, accum_out=sm), waits=(t_,), inc=1)
            t_ = k.op("vector", lambda e, sm=sm, rs=rs: e.reciprocal(rs, sm),
                      waits=(tsf,), inc=1)
            t_ = k.op("vector", lambda e, rw=rw, rs=rs: e.tensor_scalar(
                rw, rw, rs, None, op0=ALU.mult), waits=(t_,), inc=1)
            t_ = k.op("vector", lambda e, rw=rw, m1=m1: e.reduce_max(
                m1, rw, axis=AX.X), waits=(t_,), inc=1)
            t_ = k.op("vector", lambda e, rw=rw, m1=m1, i1=i1: e.tensor_scalar(
                i1, rw, m1, None, op0=ALU.is_equal), waits=(t_,), inc=1)
            t_ = k.op("vector", lambda e, i1=i1, rw2=rw2: e.tensor_scalar(
                rw2, i1, -1e9, None, op0=ALU.mult), waits=(t_,), inc=1)
            t_ = k.op("vector", lambda e, rw=rw, rw2=rw2: e.tensor_add(rw2, rw2, rw),
                      waits=(t_,), inc=1)
            t_ = k.op("vector", lambda e, rw2=rw2, m2=m2: e.reduce_max(
                m2, rw2, axis=AX.X), waits=(t_,), inc=1)
            t_ = k.op("vector", lambda e, rw2=rw2, m2=m2, i2=i2: e.tensor_scalar(
                i2, rw2, m2, None, op0=ALU.is_equal), waits=(t_,), inc=1)
            t2_ = k.op("vector", lambda e, m1=m1, m2=m2, dd=dd: e.tensor_sub(dd, m2, m1),
                       waits=(t_,), inc=1)
            ted = k.op("scalar", lambda e, dd=dd: e.activation(dd, dd, AF.Exp),
                       waits=(t2_,), inc=1)
            t_ = k.op("vector", lambda e, dd=dd: e.tensor_scalar(
                dd, dd, 1.0, None, op0=ALU.add), waits=(ted,), inc=1)
            t_ = k.op("vector", lambda e, dd=dd, w1=w1: e.reciprocal(w1, dd),
                      waits=(t_,), inc=1)
            t_ = k.op("vector", lambda e, w1=w1, w2=w2: e.tensor_scalar(
                w2, w1, -1.0, 1.0, op0=ALU.mult, op1=ALU.add), waits=(t_,), inc=1)
            cc_ = combine[:, ch * 8:(ch + 1) * 8]
            t_ = k.op("vector", lambda e, i1=i1, w1=w1, cc_=cc_: e.tensor_scalar(
                cc_, i1, w1, None, op0=ALU.mult), waits=(t_,), inc=1)
            t_ = k.op("vector", lambda e, i2=i2, w2=w2, t3=t3: e.tensor_scalar(
                t3, i2, w2, None, op0=ALU.mult), waits=(t_,), inc=1)
            t_comb = k.op("vector", lambda e, cc_=cc_, t3=t3: e.tensor_add(cc_, cc_, t3),
                          waits=(t_,), inc=1)
            gset(f"rt{par}", t_comb)
            t_comb_ch[ch] = t_comb

        for ch in range(NCH):
            par = ch % 2
            tofs = []
            for hf in range(2):
                rot = hf
                last = None
                for ac in range(8):
                    w = [t_a2t, wot_t[ac]] + ([t_bps[rot]] if ac == 0 else [])
                    last = k.op("tensor", lambda e, ch=ch, hf=hf, ac=ac, rot=rot:
                                e.matmul(ps[rot][:, :],
                                         A2T[ch // 2][:, (ch % 2) * 1024 + ac * 128:
                                                      (ch % 2) * 1024 + (ac + 1) * 128],
                                         wot_ap(ac)[:, hf * 512:(hf + 1) * 512],
                                         start=(ac == 0), stop=(ac == 7)),
                                waits=w, inc=(ac == 7))
                t_bps[rot] = k.op("vector", lambda e, ch=ch, hf=hf, rot=rot:
                                  e.tensor_add(
                                      f32v(BIG16[ch])[:, hf * 512:(hf + 1) * 512],
                                      f32v(BIG16[ch])[:, hf * 512:(hf + 1) * 512],
                                      ps[rot][:, :]),
                                  waits=(last,), inc=1)
                tofs.append(t_bps[rot])
            t_xff[ch] = tofs[1]
            if "xff" in taps:
                k.dma("sync", tap_d["xff"][ch * 128:(ch + 1) * 128, :], f32v(BIG16[ch]),
                      waits=(tofs[0], tofs[1]))
            if pend_b:
                c1_tail(*pend_b.pop(0))
            bsrc = f32v(BIG16[ch])
            rstd2, trs = rms_stats(bsrc, EPS_RMS, par, (t_xff[ch],))
            scr = f32v(SCR[par])
            t_sc = k.op("vector", lambda e, s=bsrc, r=rstd2, d=scr: e.tensor_scalar(
                d, s, r, None, op0=ALU.mult), waits=(trs, gw(f"scr{par}")), inc=1)
            nxc = LNC[0][:, par * 1024:(par + 1) * 1024]
            t_nx = k.op("vector", lambda e, s=scr, d=nxc, b=BCv[1]: e.tensor_tensor(
                d, s, b, op=ALU.mult), waits=(t_g2, gw(f"l2c{par}"), t_sc), inc=1)
            gset(f"scr{par}", t_nx)
            gset(f"cols{par}", t_nx)
            pend_b.append((ch, t_nx, nxc))
        while pend_b:
            c1_tail(*pend_b.pop(0))
        t_nx2 = k.tok("vector")
        if "combine" in taps:
            k.dma("sync", tap_d["combine"][:, :], combine[:, :],
                  waits=(t_comb_ch[NCH - 1],))

        k.barrier()
        pool.free("lnc2", "bc", "scr", "lsc", "a2t", "wot")
        W1TA = pool.alloc("w1ta", 4)
        W1TB = pool.alloc("w1tb", 4)
        W3T = pool.alloc("w3t", 4)
        W2T = pool.alloc("w2t", 4)
        SG = pool.alloc("sg", 2)
        HP = pool.alloc("hp", 2)
        TMP = pool.alloc("tmp", 1)
        tmpv = f32v(TMP[0])
        w_ap = lambda W, dc: W[dc // 2][:, (dc % 2) * 1024:(dc % 2 + 1) * 1024]

        t_wmm_guard = None
        t_w1_guard = None
        t_w3_guard = None
        t_w2_guard = None
        w1_pref = None
        t_cb = None
        t_gps = [None, None]
        t_ups = [None, None]
        t_eops = [st["rps"], st["rps"]]
        t_addg = [None, None]
        t_sg_guard = None
        t_hp_guard = None
        last_eo = None
        for w_i in range(9):
            shared = (w_i == 0)
            if shared:
                rows1, rows2, rows3 = R_SW1, R_SW2, R_SW3
            else:
                e_i = w_i - 1
                rows1 = R_EW1 + e_i * 1024
                rows2 = R_EW2 + e_i * 1024
                rows3 = R_EW3 + e_i * 1024
            W1T = W1TA if w_i % 2 == 0 else W1TB
            wt = {}
            if w_i == 0:
                tt_ = None
                for dc in range(8):
                    tt_ = k.dmat("sync", w_ap(W1T, dc),
                                 wsc[rows1:rows1 + 1024, dc * 128:(dc + 1) * 128],
                                 waits=(prep_tok(rows1, 1024),), new_group=(dc == 0))
                wt["w1"] = [tt_] * 8
            else:
                wt["w1"] = w1_pref
            for nm_, rows, W, grd in (("w3", rows3, W3T, t_w3_guard),
                                      ("w2", rows2, W2T, t_w2_guard)):
                tt_ = None
                for dc in range(8):
                    tt_ = k.dmat("sync", w_ap(W, dc),
                                 wsc[rows:rows + 1024, dc * 128:(dc + 1) * 128],
                                 waits=(prep_tok(rows, 1024), grd),
                                 new_group=(dc == 0))
                wt[nm_] = [tt_] * 8
            if w_i < 8:
                nrows1 = R_EW1 + w_i * 1024
                WN1 = W1TB if w_i % 2 == 0 else W1TA
                tt_ = None
                for dc in range(8):
                    tt_ = k.dmat(
                        "sync", w_ap(WN1, dc),
                        wsc[nrows1:nrows1 + 1024, dc * 128:(dc + 1) * 128],
                        waits=(prep_tok(nrows1, 1024), t_w1_guard),
                        new_group=(dc == 0))
                w1_pref = [tt_] * 8
            last_g_mm = None
            last_u_mm = None
            last_eo_mm = None
            for tb in range(4):
                t_hp = [None] * 8
                for fc in range(8):
                    rot = fc % 2
                    lastg = None
                    for dc in range(8):
                        w = [t_nx2, wt["w1"][dc]] + ([t_gps[rot]] if dc == 0 else [])
                        lastg = k.op("tensor",
                                     lambda e, fc=fc, dc=dc, tb=tb, rot=rot,
                                     wap=w_ap(W1T, dc): e.matmul(
                                         ps[rot][:, :],
                                         wap[:, fc * 128:(fc + 1) * 128],
                                         NX2T[dc][:, tb * 512:(tb + 1) * 512],
                                         start=(dc == 0), stop=(dc == 7)),
                                     waits=w, inc=(dc == 7))
                    last_g_mm = lastg
                    sg_ap = SG[fc // 4][:, (fc % 4) * 512:(fc % 4 + 1) * 512]
                    t_sg = k.op("scalar", lambda e, sg_ap=sg_ap, rot=rot: e.activation(
                        sg_ap, ps[rot][:, :], AF.Silu),
                        waits=(lastg, t_sg_guard), inc=1)
                    t_gps[rot] = t_sg
                    lastu = None
                    for dc in range(8):
                        w = [t_nx2, wt["w3"][dc]] + ([t_ups[rot]] if dc == 0 else [])
                        lastu = k.op("tensor",
                                     lambda e, fc=fc, dc=dc, tb=tb, rot=rot: e.matmul(
                                         ps[2 + rot][:, :],
                                         w_ap(W3T, dc)[:, fc * 128:(fc + 1) * 128],
                                         NX2T[dc][:, tb * 512:(tb + 1) * 512],
                                         start=(dc == 0), stop=(dc == 7)),
                                     waits=w, inc=(dc == 7))
                    last_u_mm = lastu
                    hp_ap = HP[fc // 4][:, (fc % 4) * 512:(fc % 4 + 1) * 512]
                    t_u = k.op("vector",
                               lambda e, hp_ap=hp_ap, sg_ap=sg_ap, rot=rot:
                               e.tensor_tensor(hp_ap, sg_ap, ps[2 + rot][:, :],
                                               op=ALU.mult),
                               waits=(lastu, t_sg, t_hp_guard), inc=1)
                    t_ups[rot] = t_u
                    t_hp[fc] = t_u
                for ss in range(4):
                    ch = tb * 4 + ss
                    for hf in range(2):
                        rot = (ss * 2 + hf) % 2
                        last = None
                        for fc in range(8):
                            w = [t_hp[fc], wt["w2"][fc]] + \
                                ([t_eops[rot]] if fc == 0 else [])
                            last = k.op("tensor",
                                        lambda e, ss=ss, hf=hf, fc=fc, rot=rot:
                                        e.matmul(
                                            ps[4 + rot][:, :],
                                            HP[fc // 4][:, (fc % 4) * 512 + ss * 128:
                                                        (fc % 4) * 512 + (ss + 1) * 128],
                                            w_ap(W2T, fc)[:, hf * 512:(hf + 1) * 512],
                                            start=(fc == 0), stop=(fc == 7)),
                                        waits=w, inc=(fc == 7))
                        last_eo_mm = last
                        if shared:
                            t_eops[rot] = k.op("vector",
                                               lambda e, ch=ch, hf=hf, rot=rot:
                                               e.tensor_add(
                                                   f32v(BIG16[ch])
                                                   [:, hf * 512:(hf + 1) * 512],
                                                   f32v(BIG16[ch])
                                                   [:, hf * 512:(hf + 1) * 512],
                                                   ps[4 + rot][:, :]),
                                               waits=(last,), inc=1)
                            t_addg[rot] = t_eops[rot]
                        else:
                            cbc = combine[:, ch * 8 + e_i:ch * 8 + e_i + 1]
                            th = tmpv[:, rot * 512:(rot + 1) * 512]
                            t_m = k.op("vector",
                                       lambda e, th=th, rot=rot, cbc=cbc:
                                       e.tensor_scalar(th, ps[4 + rot][:, :],
                                                       cbc, None, op0=ALU.mult),
                                       waits=(last, t_addg[rot]), inc=1)
                            t_eops[rot] = t_m
                            t_addg[rot] = k.op("vector",
                                               lambda e, ch=ch, hf=hf, th=th:
                                               e.tensor_add(
                                                   f32v(BIG16[ch])
                                                   [:, hf * 512:(hf + 1) * 512],
                                                   f32v(BIG16[ch])
                                                   [:, hf * 512:(hf + 1) * 512],
                                                   th), waits=(t_m,), inc=1)
                        last_eo = t_addg[rot]
                t_sg_guard = last_eo
                t_hp_guard = last_eo
            t_wmm_guard = last_eo
            t_w1_guard = last_g_mm
            t_w3_guard = last_u_mm
            t_w2_guard = last_eo_mm

        k.barrier()
        pool.free("w1ta", "w1tb", "w3t", "w2t", "sg", "hp", "tmp", "nx2t")
        SCR = pool.alloc("scr", 2)
        LSC = pool.alloc("lsc", 2)
        for key_ in list(G):
            if key_.startswith("scr") or key_.startswith("lsc"):
                G.pop(key_)
        XRL = pool.alloc("xrl", 1)
        CSO = pool.alloc("cso", 2)
        for ch in range(NCH):
            par = ch % 2
            xrl_ap = XRL[0][:, par * 1024:(par + 1) * 1024]
            t_x2l = k.dma("sync", xrl_ap, xsp[ch * 128:(ch + 1) * 128, :],
                          waits=(gw(f"xrl{par}"),))
            bsrc = f32v(BIG16[ch])
            mean, rstd, tst = ln_stats(bsrc, EPS_LN, par, (last_eo,))
            nrm = f32v(SCR[par])
            t_nr = k.op("scalar", lambda e, s=bsrc, m=mean, r=rstd, d=nrm:
                        e.activation(d, s, AF.Identity, scale=r, bias=m),
                        waits=(tst, gw(f"scr{par}")), inc=1)
            ot = f32v(CSO[par])
            t_o = k.op("vector", lambda e, n=nrm, x=xrl_ap, d=ot: e.tensor_add(d, n, x),
                       waits=(t_x2l, gw(f"cso{par}"), t_nr), inc=1)
            gset(f"xrl{par}", t_o)
            gset(f"scr{par}", t_o)
            gset(f"cols{par}", t_o)
            b0, jj = ch // 2, ch % 2
            off = (jj * 1024 + b0) * 1024
            w_ = [t_o] + ([k.lastg["sync"]] if ch == 0 else [])
            t_w = k.op("sync", lambda e, ot=ot, off=off: e.dma_start(
                out=bass.AP(out_d, off, [[8 * 1024, 128], [1, 1024]]), in_=ot),
                waits=w_, inc=16)
            k.lastg["sync"] = t_w
            gset(f"cso{par}", t_w)
        k.final_wait("sync")

        k.build(block)
    return nc


_CACHE = {}


def kernel(**inp):
    x = np.ascontiguousarray(inp["L0_attn_query"], dtype=np.float32)
    key = tuple(sorted(TAPS))
    if key not in _CACHE:
        _CACHE[key] = build_nc(tuple(TAPS))
    nc = _CACHE[key]

    idx = (np.pi / 2) * (np.arange(1, L + 1, dtype=np.float64) / L)
    sin_t = np.sin(idx).astype(np.float32)
    cos_t = np.cos(idx).astype(np.float32)

    w = {
        "aWq": inp["L0_attn_Wq"], "abq": inp["L0_attn_bq"].reshape(1, DM),
        "aWk": inp["L0_attn_Wk"], "abk": inp["L0_attn_bk"].reshape(1, DM),
        "aWv": inp["L0_attn_Wv"], "abv": inp["L0_attn_bv"].reshape(1, DM),
        "aWo": inp["L0_attn_Wo"], "abo": inp["L0_attn_bo"].reshape(1, DM),
        "g1": inp["L0_ffn_g1"].reshape(1, DM), "g2": inp["L0_ffn_g2"].reshape(1, DM),
        "fWq": inp["L0_ffn_Wq"], "fWkc": inp["L0_ffn_Wkc"], "fWvc": inp["L0_ffn_Wvc"],
        "fWo": inp["L0_ffn_Wo"], "fWr": inp["L0_ffn_Wr"],
        "febias": inp["L0_ffn_expert_bias"].reshape(1, NE),
        "sW1": inp["L0_ffn_sW1"], "sW2": inp["L0_ffn_sW2"], "sW3": inp["L0_ffn_sW3"],
        "eW1": inp["L0_ffn_eW1"], "eW2": inp["L0_ffn_eW2"], "eW3": inp["L0_ffn_eW3"],
    }
    w = {kk: np.ascontiguousarray(vv, dtype=np.float32) for kk, vv in w.items()}

    in_maps = []
    for c in range(8):
        xq = np.ascontiguousarray(x[:, c * 256:(c + 1) * 256, :].reshape(L, DM))
        xr = np.ascontiguousarray(
            x[c].reshape(256, 8, DM).transpose(1, 0, 2).reshape(L, DM))
        l0q = np.empty((128, 16), np.int64)
        l0k = np.empty((128, 16), np.int64)
        for ch in range(16):
            l0q[:, ch] = c * 256 + (ch % 2) * 128 + np.arange(128)
            l0k[:, ch] = ch * 128 + np.arange(128)
        sccq = np.concatenate([sin_t[l0q], cos_t[l0q]], axis=1).astype(np.float32)
        scck = np.concatenate([sin_t[l0k], cos_t[l0k]], axis=1).astype(np.float32)
        rq = c * 256 + (np.arange(L) % 256)
        srowq = np.stack([sin_t[rq], cos_t[rq]], axis=0).astype(np.float32)
        m = {"x": np.ascontiguousarray(x[c]), "xq": xq, "xr": xr,
             "sccq": np.ascontiguousarray(sccq), "scck": np.ascontiguousarray(scck),
             "srowq": np.ascontiguousarray(srowq)}
        m.update(w)
        in_maps.append(m)

    res = run_bass_kernel_spmd(nc, in_maps, core_ids=list(range(8)))
    out = np.stack([np.asarray(res.results[c]["out"], dtype=np.float32)
                    for c in range(8)], axis=0)
    if TAPS:
        kernel.tap_results = [{t: np.asarray(res.results[c]["tap_" + t])
                               for t in TAPS} for c in range(8)]
    return out


# version 36
# speedup vs baseline: 1.0360x; 1.0021x over previous
import numpy as np
import ml_dtypes
import concourse.bass as bass
import concourse.mybir as mybir
from concourse.bass_utils import run_bass_kernel_spmd

F32 = mybir.dt.float32
BF16 = mybir.dt.bfloat16
AF = mybir.ActivationFunctionType
ALU = mybir.AluOpType
AX = mybir.AxisListType
BFNP = ml_dtypes.bfloat16

L = 2048
DM = 1024
NCH = 16
NH = 16
H2 = 4
NE = 8
EPS_LN = 1e-5
EPS_RMS = 1e-6
EPS_Z = 1e-6
INV = 1.0 / DM

ENGS = ("sync", "gpsimd", "tensor", "vector", "scalar")

TAPS = ()


class KB:

    DOMS = ENGS + ("prep", "sdma")

    def __init__(self, nc):
        self.nc = nc
        self.sems = {e: nc.alloc_semaphore(name=f"sem_{e}") for e in self.DOMS}
        self.cnt = dict.fromkeys(self.DOMS, 0)
        self.wm = {e: dict.fromkeys(self.DOMS, 0) for e in ENGS}
        self.q = {e: [] for e in ENGS}
        self.lastg = dict.fromkeys(self.DOMS, None)

    def tok(self, e):
        return (e, self.cnt[e])

    def _wait(self, e, toks):
        for t in toks:
            if t is None:
                continue
            oe, v = t
            if v <= 0 or self.wm[e][oe] >= v:
                continue
            self.wm[e][oe] = v
            s = self.sems[oe]
            self.q[e].append(lambda eng, s=s, v=v: eng.wait_ge(s, v))

    def op(self, e, fn, waits=(), inc=0, dom=None):
        self._wait(e, waits)
        if inc:
            dom = dom or e
            self.cnt[dom] += inc
            s = self.sems[dom]
            self.q[e].append(lambda eng, fn=fn, s=s, i=inc: fn(eng).then_inc(s, i))
            return (dom, self.cnt[dom])
        self.q[e].append(fn)
        return None

    def dma(self, e, out, in_, waits=(), dom=None, new_group=True):
        dom = dom or e
        w = list(waits) + ([self.lastg[dom]] if new_group else [])
        t = self.op(e, lambda eng: eng.dma_start(out=out, in_=in_), w, inc=16, dom=dom)
        self.lastg[dom] = t
        return t

    def dmat(self, e, out, in_, waits=(), dom=None, new_group=True):
        dom = dom or e
        w = list(waits) + ([self.lastg[dom]] if new_group else [])
        t = self.op(e, lambda eng: eng.dma_start(out=out, in_=in_, transpose=True),
                    w, inc=16, dom=dom)
        self.lastg[dom] = t
        return t

    def barrier(self):
        toks = [self.tok(e) for e in ENGS]
        for e in ENGS:
            self._wait(e, toks)

    def final_wait(self, e):
        s = self.sems[e]
        v = self.cnt[e]
        self.q[e].append(lambda eng, s=s, v=v: eng.wait_ge(s, v))

    def build(self, block):
        for e in ENGS:
            q = self.q[e]
            getattr(block, e)(lambda eng, q=q: [f(eng) for f in q])


class Pool:
    def __init__(self, nc, n):
        self.tiles = [nc.alloc_sbuf_tensor(f"pl{i}", [128, 2048], BF16) for i in range(n)]
        self.n = n
        self.owner = [None] * n

    def alloc(self, name, count):
        idxs = [i for i in range(self.n) if self.owner[i] is None][:count]
        assert len(idxs) == count, \
            f"pool OOM: {name} wants {count}, free={self.owner.count(None)}"
        for i in idxs:
            self.owner[i] = name
        return [self.tiles[i] for i in idxs]

    def free(self, *names):
        for i in range(self.n):
            if self.owner[i] in names:
                self.owner[i] = None


def f32v(tile):
    return tile.ap().bitcast(F32)


def build_nc(taps=()):
    nc = bass.Bass("TRN2", target_bir_lowering=False)

    din = {}
    def inp(name, shape):
        din[name] = nc.dram_tensor(name, list(shape), F32, kind="ExternalInput")
        return din[name]

    for nm, sh in (("x", (L, DM)), ("xq", (L, DM)), ("xr", (L, DM)),
                   ("sccq", (128, 32)), ("scck", (128, 32)), ("srowq", (2, L)),
                   ("aWq", (DM, DM)), ("abq", (1, DM)), ("aWk", (DM, DM)),
                   ("abk", (1, DM)), ("aWv", (DM, DM)), ("abv", (1, DM)),
                   ("aWo", (DM, DM)), ("abo", (1, DM)), ("g1", (1, DM)),
                   ("g2", (1, DM)), ("fWq", (DM, DM)), ("fWkc", (256, DM)),
                   ("fWvc", (256, DM)), ("fWo", (DM, DM)), ("fWr", (NE, DM)),
                   ("febias", (1, NE)), ("sW1", (DM, DM)), ("sW2", (DM, DM)),
                   ("sW3", (DM, DM)), ("eW1", (NE, DM, DM)), ("eW2", (NE, DM, DM)),
                   ("eW3", (NE, DM, DM))):
        inp(nm, sh)

    out_d = nc.dram_tensor("out", [L, DM], F32, kind="ExternalOutput")
    TAP_SPECS = {
        "x2": ([L, DM], F32), "xff": ([L, DM], F32), "combine": ([128, 128], F32),
        "lnx0": ([128, 1024], BF16), "tch0": ([128, 1024], BF16),
        "k0": ([128, 1024], F32), "eq0": ([128, 1024], F32),
        "kft0": ([128, 2048], BF16), "vex0": ([128, 2048], BF16),
        "kvacc": ([128, 2048], F32), "kfacc32": ([128, 32], F32),
        "kvall0": ([128, 2048], BF16), "qftt0": ([128, 2048], BF16),
        "atn0": ([128, 2048], BF16), "nxt0": ([128, 2048], BF16),
        "qt0": ([128, 2048], BF16), "kct0": ([128, 2048], BF16),
        "a2t0": ([128, 2048], BF16), "cols0": ([128, 32], F32),
    }
    tap_d = {}
    for t in taps:
        shp, dt_ = TAP_SPECS[t]
        tap_d[t] = nc.dram_tensor("tap_" + t, shp, dt_, kind="ExternalOutput")

    R_FWQ, R_FWO, R_FWKC, R_FWVC = 0, 1024, 2048, 2304
    R_SW1, R_SW2, R_SW3 = 2560, 3584, 4608
    R_EW1, R_EW2, R_EW3 = 5632, 13824, 22016
    wsc = nc.dram_tensor("wsc", [30208, DM], BF16)
    cc_in = nc.dram_tensor("cc_in", [128, 2080], BF16)
    cc_out = nc.dram_tensor("cc_out", [1024, 2080], BF16)
    xsp = nc.dram_tensor("xsp", [L, DM], BF16)
    comb_dram = nc.dram_tensor("comb_dram", [NE, L], BF16)
    ident_d = nc.inline_tensor(np.eye(128, dtype=BFNP), name="ident_c")

    ident = nc.alloc_sbuf_tensor("ident", [128, 128], BF16)
    onec_bf = nc.alloc_sbuf_tensor("onec_bf", [1, 128], BF16)
    onec_f = nc.alloc_sbuf_tensor("onec_f", [1, 128], F32)
    brow = nc.alloc_sbuf_tensor("brow", [1, DM], F32)
    srow_sin = nc.alloc_sbuf_tensor("srow_sin", [1, L], BF16)
    srow_cos = nc.alloc_sbuf_tensor("srow_cos", [1, L], BF16)
    sccq_sb = nc.alloc_sbuf_tensor("sccq_sb", [128, 32], F32)
    scck_sb = nc.alloc_sbuf_tensor("scck_sb", [128, 32], F32)
    cols = nc.alloc_sbuf_tensor("cols", [128, 48], F32)
    hsum = nc.alloc_sbuf_tensor("hsum", [128, 96], F32)
    hrcp = nc.alloc_sbuf_tensor("hrcp", [128, 32], F32)
    kfacc = nc.alloc_sbuf_tensor("kfacc", [128, 32], F32)
    kfbf = nc.alloc_sbuf_tensor("kfbf", [128, 32], BF16)
    kvfs = nc.alloc_sbuf_tensor("kvfs", [128, 256], BF16)
    dencol = nc.alloc_sbuf_tensor("dencol", [128, 32], F32)
    zcol = nc.alloc_sbuf_tensor("zcol", [128, 32], F32)
    rt = nc.alloc_sbuf_tensor("rt", [128, 192], F32)
    combine = nc.alloc_sbuf_tensor("combine", [128, 128], F32)
    combine_bf = nc.alloc_sbuf_tensor("combine_bf", [128, 128], BF16)
    combT = nc.alloc_sbuf_tensor("combT", [8, L], BF16)
    wrbf = nc.alloc_sbuf_tensor("wrbf", [8, DM], BF16)
    wrt = nc.alloc_sbuf_tensor("wrt", [128, 64], BF16)
    eb_bc = nc.alloc_sbuf_tensor("eb_bc", [128, 8], F32)
    avz = nc.alloc_sbuf_tensor("avz", [128, 8], F32)
    cst = nc.alloc_sbuf_tensor("cst", [128, 2], F32)
    rowA = nc.alloc_sbuf_tensor("rowA", [1, DM], BF16)
    rowB = nc.alloc_sbuf_tensor("rowB", [1, DM], BF16)
    onescol = nc.alloc_sbuf_tensor("onescol", [128, 1], BF16)

    pool = Pool(nc, 45)

    _psg = []
    ps = []
    for i in range(6):
        g_ = nc.psum_tensor(f"ps{i}", [128, 512], F32)
        _psg.append(g_)
        ps.append(g_.__enter__())
    _ptg = nc.psum_tensor("pt", [128, 1024], BF16)
    _psg.append(_ptg)
    pt = _ptg.__enter__()
    _ps6g = nc.psum_tensor("ps6", [128, 512], F32)
    ps.append(_ps6g.__enter__())
    ptst = {"pt2": None, "w": None}

    k = KB(nc)
    cc_sem = nc.alloc_semaphore(name="cc_sem")

    with nc.Block() as block:
        G = {}

        def gw(name):
            return G.get(name)

        def tap(name, src_ap, tok_, guard=None):
            if name in taps:
                t_ = k.dma("sync", tap_d[name][:, :], src_ap, waits=(tok_,))
                if guard is not None:
                    gset(guard, t_)

        def gset(name, tok_):
            if tok_ is not None:
                G[name] = tok_

        CS = pool.alloc("cs", 4)
        WN = pool.alloc("wn", 8)
        TCH = pool.alloc("tch", 2)
        LNC = pool.alloc("lnc", 2)
        KFT = pool.alloc("kft", 2)
        KFC = pool.alloc("kfc", 2)
        VEX = pool.alloc("vex", 2)
        KVA = pool.alloc("kva", 2)
        BC = pool.alloc("bc", 2)
        SCR = pool.alloc("scr", 2)
        LSC = pool.alloc("lsc", 2) + pool.alloc("lsce", 2)

        BCv = [f32v(BC[0]), f32v(BC[1])]

        crow = brow.ap().bitcast(BF16)
        t_id = k.dma("sync", ident[:, :], ident_d[:, :])
        k.dma("sync", sccq_sb[:, :], din["sccq"][:, :])
        t_scck = k.dma("sync", scck_sb[:, :], din["scck"][:, :])
        t_ones = k.op("vector", lambda e: e.memset(onec_bf.ap(), 1.0), inc=0)
        k.op("vector", lambda e: e.memset(onescol.ap(), 1.0), inc=0)
        t_ones = k.op("vector", lambda e: e.memset(onec_f.ap(), 1.0), inc=1)
        t_srs = k.dma("gpsimd", srow_sin[:, :], din["srowq"][0:1, :])
        t_src = k.dma("gpsimd", srow_cos[:, :], din["srowq"][1:2, :])
        k.op("vector", lambda e: e.memset(cst[:, 0:1], EPS_LN))
        t_cst = k.op("vector", lambda e: e.memset(cst[:, 1:2], EPS_RMS), inc=1)
        k.op("vector", lambda e: e.memset(f32v(KVA[0]), 0.0))
        k.op("vector", lambda e: e.memset(f32v(KVA[1]), 0.0))
        t_kvz = k.op("vector", lambda e: e.memset(kfacc.ap(), 0.0), inc=1)

        def bias_bcast(dst_f32, row_in, psum_guards=(), expize=False):
            tl = k.dma("sync", brow[:, :], row_in, waits=(gw("brow"),))
            tms = []
            for h in range(2):
                tms.append(k.op("tensor", lambda e, h=h: e.matmul(
                    ps[4 + h][:, :], onec_f[0:1, :], brow[0:1, h * 512:(h + 1) * 512],
                    start=True, stop=True),
                    waits=[tl, t_ones] + list(psum_guards), inc=1))
            tv = None
            for h in range(2):
                if expize:
                    tv = k.op("scalar", lambda e, h=h, d=dst_f32: e.activation(
                        d[:, h * 512:(h + 1) * 512], ps[4 + h][:, :], AF.Exp),
                        waits=(tms[h],), inc=(h == 1))
                else:
                    tv = k.op("vector", lambda e, h=h, d=dst_f32: e.tensor_copy(
                        d[:, h * 512:(h + 1) * 512], ps[4 + h][:, :]),
                        waits=(tms[h],), inc=(h == 1))
            gset("brow", tv)
            return tv

        def ln_stats(src_ap, eps, par, waits):
            c = cols
            s1 = c[:, par * 12 + 0: par * 12 + 1]
            s2 = c[:, par * 12 + 1: par * 12 + 2]
            mean = c[:, par * 12 + 2: par * 12 + 3]
            msq = c[:, par * 12 + 3: par * 12 + 4]
            var = c[:, par * 12 + 4: par * 12 + 5]
            rstd = c[:, par * 12 + 5: par * 12 + 6]
            nmr = c[:, par * 12 + 6: par * 12 + 7]
            scrap = f32v(LSC[par])
            w = list(waits) + [gw(f"lsc{par}"), gw(f"cols{par}")]
            tv = k.op("vector", lambda e: e.reduce_sum(s1, src_ap, axis=AX.X,
                                                       negate=True),
                      waits=w, inc=1)
            tsq = k.op("scalar", lambda e: e.activation(scrap, src_ap, AF.Square,
                                                        accum_out=s2),
                       waits=w, inc=1)
            gset(f"lsc{par}", tsq)
            t_ = k.op("vector", lambda e: e.tensor_scalar(mean, s1, INV, None,
                                                          op0=ALU.mult),
                      waits=(tsq, tv), inc=1)
            t_ = k.op("vector", lambda e: e.tensor_tensor(msq, mean, mean, op=ALU.mult),
                      waits=(t_,), inc=1)
            t_ = k.op("vector", lambda e: e.tensor_scalar(var, s2, INV, None,
                                                          op0=ALU.mult),
                      waits=(t_,), inc=1)
            tv = k.op("vector", lambda e: e.tensor_sub(var, var, msq),
                      waits=(t_,), inc=1)
            eap = cst[:, 0:1] if eps == EPS_LN else cst[:, 1:2]
            t_ = k.op("scalar", lambda e: e.activation(rstd, var, AF.Ln, bias=eap),
                      waits=(tv, t_cst), inc=1)
            tl = k.op("scalar", lambda e: e.activation(rstd, rstd, AF.Exp, scale=-0.5),
                      waits=(t_,), inc=1)
            tl = k.op("vector", lambda e: e.tensor_tensor(nmr, mean, rstd, op=ALU.mult),
                      waits=(tl,), inc=1)
            return nmr, rstd, tl

        def rms_stats(src_ap, eps, par, waits):
            c = cols
            s2 = c[:, par * 12 + 7: par * 12 + 8]
            rstd = c[:, par * 12 + 8: par * 12 + 9]
            scrap = f32v(LSC[par])
            w = list(waits) + [gw(f"lsc{par}"), gw(f"cols{par}")]
            tsq = k.op("scalar", lambda e: e.activation(scrap, src_ap, AF.Square,
                                                        accum_out=s2), waits=w, inc=1)
            gset(f"lsc{par}", tsq)
            tv = k.op("vector", lambda e: e.tensor_scalar(rstd, s2, INV, None,
                                                          op0=ALU.mult),
                      waits=(tsq,), inc=1)
            eap = cst[:, 0:1] if eps == EPS_LN else cst[:, 1:2]
            t_ = k.op("scalar", lambda e: e.activation(rstd, rstd, AF.Ln, bias=eap),
                      waits=(tv, t_cst), inc=1)
            tl = k.op("scalar", lambda e: e.activation(rstd, rstd, AF.Exp, scale=-0.5),
                      waits=(t_,), inc=1)
            return rstd, tl

        def transpose_chunk(src_bf_ap, ngrp, dst_fn, src_ready, dst_guard=(),
                            dst_contig=None):
            pt2 = ptst["pt2"]
            tev = None
            tev_other = None
            first = True
            nr = (ngrp + 3) // 4
            for r in range(nr):
                q = r * 4
                n = min(4, ngrp - q)
                if pt2 is None:
                    bank, key = pt, "pt0"
                    eng = "scalar" if dst_contig is not None else "vector"
                else:
                    hb = r % 2
                    bank = pt if hb == 0 else pt2
                    key = f"pt{hb}"
                    eng = "scalar" if dst_contig is not None else "vector"
                tmm = None
                for i in range(n):
                    w = list(src_ready) + [t_id, gw(key)]
                    if bank is not pt:
                        w.append(ptst["w"])
                    tmm = k.op("tensor", lambda e, i=i, q=q, bank=bank: e.transpose(
                        bank[:, i * 128:(i + 1) * 128],
                        src_bf_ap[:, (q + i) * 128:(q + i + 1) * 128],
                        ident[:, :]), waits=w, inc=(1 if i == n - 1 else 0))
                xw = list(dst_guard) if first else []
                if r == nr - 1 and tev_other is not None:
                    xw.append(tev_other)
                if dst_contig is not None:
                    if eng == "scalar":
                        fn = lambda e, q=q, n=n, bank=bank: e.copy(
                            dst_contig[:, q * 128:(q + n) * 128], bank[:, 0:n * 128])
                    else:
                        fn = lambda e, q=q, n=n, bank=bank: e.tensor_copy(
                            dst_contig[:, q * 128:(q + n) * 128], bank[:, 0:n * 128])
                    tev_new = k.op(eng, fn, waits=[tmm] + xw, inc=1)
                else:
                    tev_new = None
                    for i in range(n):
                        w = [tmm] + (xw if i == 0 else [])
                        if eng == "scalar":
                            fn = lambda e, i=i, q=q, bank=bank: e.copy(
                                dst_fn(q + i), bank[:, i * 128:(i + 1) * 128])
                        else:
                            fn = lambda e, i=i, q=q, bank=bank: e.tensor_copy(
                                dst_fn(q + i), bank[:, i * 128:(i + 1) * 128])
                        tev_new = k.op(eng, fn, waits=w, inc=(1 if i == n - 1 else 0))
                tev_other = tev
                tev = tev_new
                gset(key, tev_new)
                first = False
            return tev

        wk_ap = lambda dc: WN[dc // 2][:, (dc % 2) * 1024:(dc % 2 + 1) * 1024]
        wv_ap = lambda dc: WN[4 + dc // 2][:, (dc % 2) * 1024:(dc % 2 + 1) * 1024]

        t_bk = k.dma("gpsimd", rowA[:, :], din["abk"][0:1, :])
        t_bv = k.dma("gpsimd", rowB[:, :], din["abv"][0:1, :])

        wk_t, wv_t = [], []
        for dc in range(8):
            wk_t.append(k.dma("gpsimd", wk_ap(dc), din["aWk"][dc * 128:(dc + 1) * 128, :],
                              new_group=(dc == 0)))
        wk_t = [wk_t[-1]] * 8
        for dc in range(8):
            wv_t.append(k.dma("gpsimd", wv_ap(dc), din["aWv"][dc * 128:(dc + 1) * 128, :],
                              new_group=(dc == 0)))
        wv_t = [wv_t[-1]] * 8

        t_projps = [t_bv, t_bv]
        t_vproj_ps = None
        t_kvps = [None, None]
        t_kfps = None

        def a_stats(ch, src, tap0=False):
            rg = ch % 4
            csl = ch % 4
            xc = f32v(CS[csl])
            tld = k.dma("sync", xc, src[ch * 128:(ch + 1) * 128, :],
                        waits=(gw(f"cs{csl}"),))
            mean, rstd, tst = ln_stats(xc, EPS_LN, rg, (tld,))
            lnc_ap = LNC[rg // 2][:, (rg % 2) * 1024:(rg % 2 + 1) * 1024]
            t_ln = k.op("scalar", lambda e, s=xc, m=mean, r=rstd, d=lnc_ap:
                        e.activation(d, s, AF.Identity, scale=r, bias=m),
                        waits=(tst, gw(f"lnc{rg}")), inc=1)
            gset(f"cs{csl}", t_ln)
            gset(f"cols{rg}", t_ln)
            if tap0 and ch == 0:
                tap("lnx0", lnc_ap, t_ln, guard=f"lnc{rg}")
                tap("cols0", cols[:, 0:32], tst, guard=f"cols{rg}")
            return lnc_ap, t_ln

        def a_tr(ch, st, tap0=False):
            rg = ch % 4
            lnc_ap, t_ln = st
            tch_ap = TCH[rg // 2][:, (rg % 2) * 1024:(rg % 2 + 1) * 1024]
            t_tr = transpose_chunk(lnc_ap, 8, None,
                                   (t_ln,), dst_guard=(gw(f"tch{rg}"),),
                                   dst_contig=tch_ap)
            gset(f"lnc{rg}", t_tr)
            if tap0 and ch == 0:
                tap("tch0", tch_ap, t_tr, guard=f"tch{rg}")
            return tch_ap, t_tr

        sta = {c_: a_stats(c_, din["x"], tap0=True) for c_ in range(3)}
        stq = {0: a_tr(0, sta.pop(0), tap0=True)}
        for ch in range(NCH):
            par = ch % 2
            csl = ch % 4
            if ch + 3 < NCH:
                sta[ch + 3] = a_stats(ch + 3, din["x"], tap0=True)
            tch_ap, t_tr = stq.pop(ch)
            tmmk = [None, None]
            for hf in range(2):
                for dc in range(8):
                    w = [t_tr, wk_t[dc]] + ([t_projps[hf]] if dc == 0 else [])
                    k.op("tensor", lambda e, hf=hf, dc=dc, a=tch_ap: e.matmul(
                        ps[4 + hf][:, :], a[:, dc * 128:(dc + 1) * 128],
                        wk_ap(dc)[:, hf * 512:(hf + 1) * 512],
                        start=(dc == 0), stop=False), waits=w)
                tmmk[hf] = k.op("tensor", lambda e, hf=hf: e.matmul(
                    ps[4 + hf][:, :], onec_bf[0:1, :],
                    rowA[0:1, hf * 512:(hf + 1) * 512],
                    start=False, stop=True), waits=(t_bk, t_ones), inc=1)
            eq = f32v(SCR[par])
            t_exp = None
            for hf in range(2):
                t_exp = k.op("scalar", lambda e, hf=hf, d=eq: e.activation(
                    d[:, hf * 512:(hf + 1) * 512], ps[4 + hf][:, :], AF.Exp),
                    waits=(tmmk[hf], gw(f"scr{par}") if hf == 0 else None),
                    inc=1)
            tevk = t_exp
            gset(f"tch{ch % 4}", None) if False else None
            if ch == 0:
                tap("eq0", eq, t_exp, guard=f"scr{par}")
            hs = hsum[:, par * 16:(par + 1) * 16]
            hr = hrcp[:, par * 16:(par + 1) * 16]
            t_hsr = k.op("vector", lambda e, s=eq, d=hs: e.reduce_sum(
                d.rearrange("p (g o) -> p g o", o=1),
                s.rearrange("p (g i) -> p g i", g=16), axis=AX.X),
                waits=(t_exp, gw(f"hs{par}")), inc=1)
            t_hr = k.op("vector", lambda e, s=hs, d=hr: e.reciprocal(d, s),
                        waits=(t_hsr,), inc=1)
            kft_t, kfc_t = KFT[par], KFC[par]
            sin_c = scck_sb[:, ch:ch + 1]
            cos_c = scck_sb[:, 16 + ch:16 + ch + 1]
            hr2 = hrcp[:, par * 16 + 32:(par + 1) * 16 + 32] if False else None
            t_kft = None
            for half, colp in ((0, sin_c), (1, cos_c)):
                hsc = hsum[:, 32 + par * 32 + half * 16:32 + par * 32 + (half + 1) * 16]
                t_s2 = k.op("vector", lambda e, d=hsc, c=colp, hr=hr: e.tensor_scalar(
                    d, hr, c, None, op0=ALU.mult),
                    waits=(t_hr, t_scck, gw(f"kft{par}")), inc=1)
                t_kft = k.op("vector", lambda e, s=eq, d=kft_t, half=half,
                             hv=hsc.unsqueeze(2).broadcast_to([128, 16, 64])
                             : e.tensor_tensor(
                                 d.ap().rearrange("p (h j) -> p h j", j=128)
                                 [:, :, half * 64:(half + 1) * 64],
                                 s.rearrange("p (h j) -> p h j", j=64),
                                 hv, op=ALU.mult), waits=(t_s2,), inc=1)
            gset(f"scr{par}", t_kft)
            gset(f"hs{par}", t_kft)
            t_projps = [tevk, tevk]
            t_kfc = None
            for half, colp in ((0, sin_c), (1, cos_c)):
                t_kfc = k.op("vector", lambda e, d=kfc_t, s=kft_t, c=colp, half=half:
                             e.tensor_scalar(
                                 d.ap().rearrange("p (h j) -> p h j", j=128)
                                 [:, :, half * 64:(half + 1) * 64],
                                 s.ap().rearrange("p (h j) -> p h j", j=128)
                                 [:, :, half * 64:(half + 1) * 64],
                                 c, -1.0, op0=ALU.subtract, op1=ALU.mult),
                             waits=(t_kft, gw(f"kfc{par}")), inc=(half == 1))
            if ch == 0:
                tap("kft0", kft_t.ap(), t_kfc, guard=f"kft{par}")
            tmmv = [None, None]
            for hf in range(2):
                for dc in range(8):
                    w = [t_tr, wv_t[dc]] + ([t_vproj_ps] if dc == 0 else [])
                    k.op("tensor", lambda e, hf=hf, dc=dc, a=tch_ap: e.matmul(
                        ps[2 + hf][:, :], a[:, dc * 128:(dc + 1) * 128],
                        wv_ap(dc)[:, hf * 512:(hf + 1) * 512],
                        start=(dc == 0), stop=False), waits=w)
                tmmv[hf] = k.op("tensor", lambda e, hf=hf: e.matmul(
                    ps[2 + hf][:, :], onec_bf[0:1, :],
                    rowB[0:1, hf * 512:(hf + 1) * 512],
                    start=False, stop=True), waits=(t_bv, t_ones), inc=1)
            vex_t = VEX[par]
            t_vx = None
            for hf in range(2):
                t_vx = k.op("scalar", lambda e, hf=hf, d=vex_t: e.copy(
                    d[:, hf * 512:(hf + 1) * 512], ps[2 + hf][:, :]),
                    waits=(tmmv[hf], gw(f"vex{par}") if hf == 0 else None),
                    inc=(hf == 1))
            t_vproj_ps = t_vx
            gset(f"tch{ch % 4}", t_vx)
            for batch in range(2):
                last_mm = None
                for hh in range(8):
                    h = batch * 8 + hh
                    for br in range(2):
                        t = h * 2 + br
                        tloc = hh * 2 + br
                        lhs = (kft_t if br == 0 else kfc_t)[:, h * 128:(h + 1) * 128]
                        w = [t_kfc, t_vx]
                        if hh == 0 and br == 0:
                            w += [t_kvps[0] if batch == 1 else t_kvps[1], t_kfps]
                        k.op("tensor", lambda e, tloc=tloc, lhs=lhs, h=h, v=vex_t:
                             e.matmul(
                            ps[tloc // 8][:, (tloc % 8) * 64:(tloc % 8 + 1) * 64],
                            lhs, v[:, h * 64:(h + 1) * 64],
                            start=True, stop=True), waits=w)
                        last_mm = k.op("tensor", lambda e, t=t, lhs=lhs: e.matmul(
                            ps[6][:, t:t + 1], lhs, onescol[:, 0:1],
                            start=True, stop=True),
                            inc=(1 if (hh == 7 and br == 1) else 0))
                tacc = None
                for b in range(2):
                    dst = f32v(KVA[batch])
                    off = b * 512
                    tacc = k.op("vector", lambda e, b=b, dst=dst, off=off:
                                e.tensor_add(dst[:, off:off + 512],
                                             dst[:, off:off + 512], ps[b][:, :]),
                                waits=(last_mm, t_kvz), inc=(b == 1))
                t_kvps[batch] = tacc
                if batch == 1:
                    t_kfps = k.op("vector", lambda e: e.tensor_add(
                        kfacc[:, :], kfacc[:, :], ps[6][:, 0:32]),
                        waits=(last_mm,), inc=1)
            if ch == 0:
                tap("vex0", vex_t.ap(), t_vx, guard=f"vex{par}")
            gset(f"kft{par}", t_kfps)
            gset(f"kfc{par}", t_kfps)
            gset(f"vex{par}", t_kfps)
            if ch + 1 < NCH:
                stq[ch + 1] = a_tr(ch + 1, sta.pop(ch + 1), tap0=True)

        if "kvacc" in taps:
            k.dma("sync", tap_d["kvacc"][:, 0:1024], f32v(KVA[0]),
                  waits=(t_kvps[0], t_kvps[1]))
            k.dma("sync", tap_d["kvacc"][:, 1024:2048], f32v(KVA[1]),
                  waits=(t_kvps[0], t_kvps[1]))
        tap("kfacc32", kfacc[:, :], t_kfps)
        kvbf0 = SCR[0]
        t1 = k.op("vector", lambda e: e.tensor_copy(kvbf0[:, 0:1024], f32v(KVA[0])),
                  waits=(t_kvps[0], t_kvps[1], gw("scr0")), inc=1)
        t1 = k.op("vector", lambda e: e.tensor_copy(kvbf0[:, 1024:2048], f32v(KVA[1])),
                  inc=1)
        t2 = k.op("vector", lambda e: e.tensor_copy(kfbf[:, :], kfacc[:, :]),
                  waits=(t_kfps,), inc=1)
        tci1 = k.dma("sync", cc_in[:, 0:2048], kvbf0[:, :], waits=(t1,))
        tci2 = k.dma("sync", cc_in[:, 2048:2080], kfbf[:, :], waits=(t2,),
                     new_group=False)
        gset("scr0", tci1)

        def do_cc(e):
            return e.collective_compute(
                "AllGather", ALU.bypass, replica_groups=[list(range(8))],
                ins=[cc_in.ap().opt()], outs=[cc_out.ap().opt()]).then_inc(cc_sem, 1)
        k.op("gpsimd", do_cc, waits=(tci1, tci2), inc=0)
        t_cc = None

        _ps6g.__exit__(None, None, None)
        _pt2g = nc.psum_tensor("pt2", [128, 1024], BF16)
        ptst["pt2"] = _pt2g.__enter__()
        ptst["w"] = t_kfps


        pool.free("kft", "kfc", "vex", "kva")
        QFTT = pool.alloc("qftt", 16)
        QFC = pool.alloc("qfc", 2)
        last_kv_mm = k.tok("tensor")
        wq_t = []
        for dc in range(8):
            wq_t.append(k.dma("gpsimd", wk_ap(dc),
                              din["aWq"][dc * 128:(dc + 1) * 128, :],
                              waits=(last_kv_mm,), new_group=(dc == 0)))
        wq_t = [wq_t[-1]] * 8
        t_bq = k.dma("gpsimd", rowA[:, :], din["abq"][0:1, :],
                     waits=(last_kv_mm,))
        sta = {c_: a_stats(c_, din["xq"]) for c_ in range(3)}
        stq = {0: a_tr(0, sta.pop(0))}
        pend_q = []
        for ch in range(NCH):
            par = ch % 2
            csl = ch % 4
            if ch + 3 < NCH:
                sta[ch + 3] = a_stats(ch + 3, din["xq"])
            tch_ap, t_tr = stq.pop(ch)
            tmmq = [None, None]
            for hf in range(2):
                for dc in range(8):
                    w = [t_tr, wq_t[dc]] + ([t_projps[hf]] if dc == 0 else [])
                    k.op("tensor", lambda e, hf=hf, dc=dc, a=tch_ap: e.matmul(
                        ps[4 + hf][:, :], a[:, dc * 128:(dc + 1) * 128],
                        wk_ap(dc)[:, hf * 512:(hf + 1) * 512],
                        start=(dc == 0), stop=False), waits=w)
                tmmq[hf] = k.op("tensor", lambda e, hf=hf: e.matmul(
                    ps[4 + hf][:, :], onec_bf[0:1, :],
                    rowA[0:1, hf * 512:(hf + 1) * 512],
                    start=False, stop=True), waits=(t_bq, t_ones), inc=1)
            if pend_q:
                qsrc, pch, tq = pend_q.pop(0)
                t_trq = transpose_chunk(qsrc.ap(), 16, None, (tq,),
                                        dst_contig=QFTT[pch].ap())
                gset(f"qfc{pch % 2}", t_trq)
            eq = f32v(SCR[par])
            t_exp = None
            for hf in range(2):
                t_exp = k.op("scalar", lambda e, hf=hf, d=eq: e.activation(
                    d[:, hf * 512:(hf + 1) * 512], ps[4 + hf][:, :], AF.Exp),
                    waits=(tmmq[hf], gw(f"scr{par}") if hf == 0 else None),
                    inc=1)
            tevq = t_exp
            t_projps = [tevq, tevq]
            gset(f"tch{ch % 4}", tevq)
            hs = hsum[:, par * 16:(par + 1) * 16]
            hr = hrcp[:, par * 16:(par + 1) * 16]
            t_hsr = k.op("vector", lambda e, s=eq, d=hs: e.reduce_sum(
                d.rearrange("p (g o) -> p g o", o=1),
                s.rearrange("p (g i) -> p g i", g=16), axis=AX.X),
                waits=(t_exp, gw(f"hs{par}")), inc=1)
            t_hr = k.op("vector", lambda e, s=hs, d=hr: e.reciprocal(d, s),
                        waits=(t_hsr,), inc=1)
            qf_t = QFC[par]
            sin_c = sccq_sb[:, ch:ch + 1]
            cos_c = sccq_sb[:, 16 + ch:16 + ch + 1]
            t_qft = None
            for half, colp in ((0, sin_c), (1, cos_c)):
                hsc = hsum[:, 32 + par * 32 + half * 16:32 + par * 32 + (half + 1) * 16]
                t_s2 = k.op("vector", lambda e, d=hsc, c=colp, hr=hr: e.tensor_scalar(
                    d, hr, c, None, op0=ALU.mult),
                    waits=(t_hr, gw(f"qfc{par}")), inc=1)
                t_qft = k.op("vector", lambda e, s=eq, d=qf_t, half=half,
                             hv=hsc.unsqueeze(2).broadcast_to([128, 16, 64])
                             : e.tensor_tensor(
                                 d.ap().rearrange("p (h j) -> p h j", j=128)
                                 [:, :, half * 64:(half + 1) * 64],
                                 s.rearrange("p (h j) -> p h j", j=64),
                                 hv, op=ALU.mult), waits=(t_s2,), inc=1)
            gset(f"scr{par}", t_qft)
            gset(f"hs{par}", t_qft)
            pend_q.append((qf_t, ch, t_qft))
            if ch + 1 < NCH:
                stq[ch + 1] = a_tr(ch + 1, sta.pop(ch + 1))
        while pend_q:
            qsrc, pch, tq = pend_q.pop(0)
            t_trq = transpose_chunk(qsrc.ap(), 16, None, (tq,),
                                    dst_contig=QFTT[pch].ap())
            gset(f"qfc{pch % 2}", t_trq)
        t_qftT_done = k.tok("scalar")

        pool.free("tch", "lnc", "qfc", "cs")
        KVP = pool.alloc("kvp", 8)
        k.q["sync"].append(lambda eng: eng.wait_ge(cc_sem, 1))
        t_a3_all = (k.tok("tensor"), k.tok("vector"), k.tok("scalar"))
        t_kvall = []
        for b in range(8):
            t_kvall.append(k.dma("sync", KVP[b].ap(),
                                 cc_out[b * 128:(b + 1) * 128, 0:2048],
                                 waits=t_a3_all, new_group=(b == 0)))
        t_kvf = []
        for b in range(8):
            t_kvf.append(k.dma("sync", kvfs[:, b * 32:(b + 1) * 32],
                               cc_out[b * 128:(b + 1) * 128, 2048:2080],
                               waits=t_a3_all, new_group=False))
        t_all16 = t_kvf[-1]
        t_kvall = [t_all16] * 8
        t_kvf = [t_all16] * 8

        tap("kvall0", KVP[0].ap(), t_kvall[0])
        tap("qftt0", QFTT[0].ap(), t_qftT_done if False else k.tok("vector"))

        k.barrier()
        pool.free("bc", "wn", "lsce")
        WN2 = pool.alloc("wn2", 4)
        wo_ap = lambda ac: WN2[ac // 2][:, (ac % 2) * 1024:(ac % 2 + 1) * 1024]
        wo_t = []
        for ac in range(8):
            wo_t.append(k.dma("gpsimd", wo_ap(ac),
                              din["aWo"][ac * 128:(ac + 1) * 128, :],
                              new_group=(ac == 0)))
        wo_t = [wo_t[-1]] * 8
        t_bo = k.dma("gpsimd", rowA[:, :], din["abo"][0:1, :])
        SCB = pool.alloc("scb", 1)
        CMP = pool.alloc("cmp", 2)
        ATN = pool.alloc("atn", 8)

        t_scb = None
        t_ps4 = t_projps[0]
        for blk in range(4):
            tm = None
            for half, row in ((0, srow_sin), (1, srow_cos)):
                tm = k.op("tensor", lambda e, blk=blk, half=half, row=row: e.matmul(
                    ps[4][half * 64:(half + 1) * 64, :], onec_bf[0:1, 0:64],
                    row[0:1, blk * 512:(blk + 1) * 512], start=True, stop=True),
                    waits=(t_srs, t_src, t_ones, t_ps4), inc=(half == 1))
            t_scb = k.op("vector", lambda e, blk=blk: e.tensor_copy(
                SCB[0][:, blk * 512:(blk + 1) * 512], ps[4][:, :]),
                waits=(tm,), inc=1)
            t_ps4 = t_scb

        t_mainps = [None, None]
        t_denps = None
        t_atn = None
        for hg in range(2):
            for ch in range(NCH):
                b0 = ch // 2
                rot = ch % 2
                pm, pc_ = ps[rot], ps[2 + rot]
                cmp_ap = CMP[rot].ap()
                t_cmp = k.op("vector", lambda e, ch=ch, hg=hg, d=cmp_ap:
                             e.tensor_tensor(
                    d[:, 0:1024].rearrange("p (h j) -> p h j", j=128),
                    SCB[0][:, ch * 128:(ch + 1) * 128].unsqueeze(1)
                    .broadcast_to([128, 8, 128]),
                    QFTT[ch][:, hg * 1024:(hg + 1) * 1024]
                    .rearrange("p (h j) -> p h j", j=128),
                    op=ALU.subtract),
                    waits=(t_scb, t_qftT_done, gw(f"cmp{rot}")), inc=1)
                last_mm = None
                for hh in range(8):
                    h = hg * 8 + hh
                    lhs_m = QFTT[ch][:, h * 128:(h + 1) * 128]
                    lhs_c = cmp_ap[:, hh * 128:(hh + 1) * 128]
                    for br, (lhs, pb) in enumerate(((lhs_m, pm), (lhs_c, pc_))):
                        t = h * 2 + br
                        w = [t_cmp, t_kvall[b0], t_kvf[b0]]
                        if hh == 0 and br == 0:
                            w += [t_mainps[rot], t_denps]
                        k.op("tensor", lambda e, t=t, lhs=lhs, pb=pb, hh=hh, b0=b0:
                             e.matmul(pb[:, hh * 64:(hh + 1) * 64], lhs,
                                      KVP[b0][:, t * 64:(t + 1) * 64],
                                      start=True, stop=True), waits=w)
                        last_mm = k.op("tensor",
                                       lambda e, t=t, lhs=lhs, hh=hh, b0=b0, br=br:
                                       e.matmul(ps[4][:, hh * 2 + br:hh * 2 + br + 1],
                                                lhs,
                                                kvfs[:, b0 * 32 + t:b0 * 32 + t + 1],
                                                start=True, stop=True),
                                       inc=(1 if (hh == 7 and br == 1) else 0))
                gset(f"cmp{rot}", last_mm)
                dc_ = dencol[:, rot * 16:(rot + 1) * 16]
                zc_ = zcol[:, rot * 16:(rot + 1) * 16]
                t_dn = k.op("vector", lambda e, d=dc_: e.tensor_scalar(
                    d, ps[4][:, 0:16], EPS_Z, None, op0=ALU.max),
                    waits=(last_mm, gw(f"z{rot}")), inc=1)
                t_denps = k.op("vector", lambda e, d=dc_, z=zc_: e.reciprocal(z, d),
                               waits=(t_dn,), inc=1)
                u = f32v(SCR[rot])
                z1v = zc_[:, 0:16:2].unsqueeze(2).broadcast_to([128, 8, 64])
                z2v = zc_[:, 1:16:2].unsqueeze(2).broadcast_to([128, 8, 64])
                k.op("vector", lambda e, u=u, pc_=pc_, z2v=z2v: e.tensor_tensor(
                    u[:, 0:512].rearrange("p (h j) -> p h j", j=64),
                    pc_[:, :].rearrange("p (h j) -> p h j", j=64), z2v, op=ALU.mult),
                    waits=(gw(f"scr{rot}"), t_denps))
                t_u = k.op("vector", lambda e, u=u, pm=pm, z1v=z1v: e.tensor_tensor(
                    u[:, 512:1024].rearrange("p (h j) -> p h j", j=64),
                    pm[:, :].rearrange("p (h j) -> p h j", j=64), z1v, op=ALU.mult),
                    inc=1)
                t_mainps[rot] = t_u
                gset(f"z{rot}", t_u)
                piece = LSC[rot][:, 0:512]
                t_ab = k.op("vector", lambda e, u=u, piece=piece: e.tensor_add(
                    piece, u[:, 512:1024], u[:, 0:512]),
                    waits=(gw(f"lsc{rot}"), t_u), inc=1)
                gset(f"scr{rot}", t_ab)
                t_atn = transpose_chunk(
                    piece, 4,
                    lambda i, hg=hg, ch=ch: ATN[hg * 4 + i][:, ch * 128:(ch + 1) * 128],
                    (t_ab,))
                gset(f"lsc{rot}", t_atn)

        tap("atn0", ATN[0].ap(), t_atn)

        k.barrier()
        pool.free("qftt", "scb", "cmp", "kvp", "tch")
        BIG16 = pool.alloc("big16", 16)
        CS = pool.alloc("cs", 2)
        BC = pool.alloc("bc", 1)
        NXT = pool.alloc("nxt", 8)
        LNC = pool.alloc("lnc", 1)
        WKV = pool.alloc("wkv", 1)
        BCv = [f32v(BC[0])]
        for key in list(G):
            if key.startswith("cs") or key.startswith("bc") or key.startswith("lnc"):
                G.pop(key)
        prep_done = {}

        def prep_tok(row0, rows):
            best = None
            for r, (t, n) in prep_done.items():
                if row0 < r + n and r < row0 + rows:
                    if best is None or t[1] > best[1]:
                        best = t
            return best

        early_prep = [
            (din["fWkc"][:, :], R_FWKC, 256),
            (din["fWvc"][:, :], R_FWVC, 256),
            (din["fWq"][0:256, :], R_FWQ, 256),
            (din["fWq"][256:512, :], R_FWQ + 256, 256),
            (din["fWq"][512:768, :], R_FWQ + 512, 256),
            (din["fWq"][768:1024, :], R_FWQ + 768, 256),
        ]
        t_g1 = bias_bcast(BCv[0], din["g1"][0:1, :])
        t_projps = [t_g1, t_g1]
        t_nxt = None
        pend_x = []
        for ch in range(NCH):
            par = ch % 2
            csl = ch % 2
            half_ap = LNC[0][:, par * 1024:(par + 1) * 1024]
            t_xr = k.dma("gpsimd", half_ap, din["xr"][ch * 128:(ch + 1) * 128, :],
                         waits=(gw(f"half{par}"),))
            tmmo = [None, None]
            for hf in range(2):
                for ac in range(8):
                    w = [t_atn, wo_t[ac]] + ([t_projps[hf]] if ac == 0 else [])
                    k.op("tensor", lambda e, hf=hf, ac=ac, ch=ch: e.matmul(
                        ps[4 + hf][:, :], ATN[ac][:, ch * 128:(ch + 1) * 128],
                        wo_ap(ac)[:, hf * 512:(hf + 1) * 512],
                        start=(ac == 0), stop=False), waits=w)
                tmmo[hf] = k.op("tensor", lambda e, hf=hf: e.matmul(
                    ps[4 + hf][:, :], onec_bf[0:1, :],
                    rowA[0:1, hf * 512:(hf + 1) * 512],
                    start=False, stop=True), waits=(t_bo, t_ones), inc=1)
            if pend_x:
                ha, pch, tnx_p = pend_x.pop(0)
                t_nxt = transpose_chunk(ha, 8,
                                        lambda i, pch=pch: NXT[i][:, pch * 128:
                                                                  (pch + 1) * 128],
                                        (tnx_p,))
                gset(f"half{pch % 2}", t_nxt)
            oc_ = f32v(CS[csl])
            tev = None
            for hf in range(2):
                tev = k.op("vector", lambda e, hf=hf, d=oc_: e.tensor_copy(
                    d[:, hf * 512:(hf + 1) * 512], ps[4 + hf][:, :]),
                    waits=(tmmo[hf], gw(f"cs{csl}")), inc=(hf == 1))
            t_projps = [tev, tev]
            mean, rstd, tst = ln_stats(oc_, EPS_LN, par, (tev,))
            nrm = f32v(SCR[par])
            t_n = k.op("scalar", lambda e, s=oc_, m=mean, r=rstd, d=nrm:
                       e.activation(d, s, AF.Identity, scale=r, bias=m),
                       waits=(tst, gw(f"scr{par}")), inc=1)
            gset(f"cs{csl}", t_n)
            gset(f"cols{par}", t_n)
            t_x2 = k.op("vector", lambda e, d=f32v(BIG16[ch]), n=nrm, x=half_ap:
                        e.tensor_add(d, n, x), waits=(t_xr, t_n), inc=1)
            gset(f"scr{par}", t_x2)
            if "x2" in taps:
                tap("x2_pc", None, None)
                t_tap = k.dma("sync", tap_d["x2"][ch * 128:(ch + 1) * 128, :],
                              f32v(BIG16[ch]), waits=(t_x2,))
            else:
                t_tap = None
            t_spd = k.dma("gpsimd", xsp[ch * 128:(ch + 1) * 128, :], f32v(BIG16[ch]),
                          waits=(t_x2,))
            if early_prep:
                esrc, erow, erows = early_prep.pop(0)
                tsave = k.dma("gpsimd", wsc[erow:erow + erows, :], esrc)
                prep_done[erow] = (tsave, erows)
            mean2, rstd2_, tst2 = ln_stats(f32v(BIG16[ch]), EPS_LN, par, (t_x2,))
            t_ln2 = k.op("vector", lambda e, s=f32v(BIG16[ch]), m=mean2, r=rstd2_:
                         e.tensor_scalar(s, s, m, r, op0=ALU.add, op1=ALU.mult),
                         waits=(tst2, t_spd, t_tap), inc=1)
            gset(f"cols{par}", t_ln2)
            var2 = cols[:, par * 12 + 4: par * 12 + 5]
            ms_ = cols[:, par * 12 + 7: par * 12 + 8]
            rstd1 = cols[:, par * 12 + 8: par * 12 + 9]
            t_ = k.op("vector", lambda e, v=var2, r=rstd2_, d=ms_: e.tensor_tensor(
                d, v, r, op=ALU.mult), waits=(tst2,), inc=1)
            t_ = k.op("vector", lambda e, r=rstd2_, d=ms_: e.tensor_tensor(
                d, d, r, op=ALU.mult), waits=(t_,), inc=1)
            t_ = k.op("scalar", lambda e, d=rstd1, s=ms_: e.activation(
                d, s, AF.Ln, bias=cst[:, 1:2]), waits=(t_, t_cst), inc=1)
            trs = k.op("scalar", lambda e, d=rstd1: e.activation(
                d, d, AF.Exp, scale=-0.5), waits=(t_,), inc=1)
            scr = f32v(SCR[par])
            t_sc = k.op("vector", lambda e, s=f32v(BIG16[ch]), r=rstd1, d=scr:
                        e.tensor_scalar(d, s, r, None, op0=ALU.mult),
                        waits=(trs, t_ln2, gw(f"scr{par}")), inc=1)
            t_nx = k.op("vector", lambda e, s=scr, d=half_ap, b=BCv[0]: e.tensor_tensor(
                d, s, b, op=ALU.mult), waits=(t_g1, t_x2, t_sc), inc=1)
            gset(f"scr{par}", t_nx)
            gset(f"cols{par}", t_nx)
            pend_x.append((half_ap, ch, t_nx))
        while pend_x:
            ha, pch, tnx_p = pend_x.pop(0)
            t_nxt = transpose_chunk(ha, 8,
                                    lambda i, pch=pch: NXT[i][:, pch * 128:
                                                              (pch + 1) * 128],
                                    (tnx_p,))
            gset(f"half{pch % 2}", t_nxt)
        wkct = []
        for dc in range(8):
            wkct.append(k.dmat("sync", WKV[0][:, dc * 256:(dc + 1) * 256],
                               wsc[R_FWKC:R_FWKC + 256, dc * 128:(dc + 1) * 128],
                               waits=(prep_tok(R_FWKC, 256),), new_group=(dc == 0)))
        wkct = [wkct[-1]] * 8
        if "xff" in taps:
            pass

        late_jobs = [(din["fWo"][:, :], R_FWO, DM), (din["sW1"][:, :], R_SW1, DM),
                     (din["sW2"][:, :], R_SW2, DM), (din["sW3"][:, :], R_SW3, DM)]
        for e_i in range(NE):
            for nm, base in (("eW1", R_EW1), ("eW2", R_EW2), ("eW3", R_EW3)):
                late_jobs.append((din[nm][e_i, :, :], base + e_i * 1024, DM))
        for src, row, rows in late_jobs:
            tsave = k.dma("gpsimd", wsc[row:row + rows, :], src, dom="prep")
            prep_done[row] = (tsave, rows)

        k.barrier()
        pool.free("atn", "wn2", "cs", "bc", "lnc")
        KCT = pool.alloc("kct", 2)
        VX2 = pool.alloc("vx2", 4)
        QT = pool.alloc("qt", 8)
        WKV2 = pool.alloc("wkv2", 1)
        WQR = pool.alloc("wqr", 1)
        wvct = []
        for dc in range(8):
            wvct.append(k.dmat("sync", WKV2[0][:, dc * 256:(dc + 1) * 256],
                               wsc[R_FWVC:R_FWVC + 256, dc * 128:(dc + 1) * 128],
                               waits=(prep_tok(R_FWVC, 256),),
                               new_group=(dc == 0)))
        wvct = [wvct[-1]] * 8
        wq_pre = {}
        for oc in (0, 1):
            wdst_ = WQR[0][:, (oc % 2) * 1024:(oc % 2 + 1) * 1024]
            tq_ = []
            for dc in range(8):
                tq_.append(k.dmat("sync", wdst_[:, dc * 128:(dc + 1) * 128],
                                  wsc[R_FWQ + oc * 128:R_FWQ + (oc + 1) * 128,
                                      dc * 128:(dc + 1) * 128],
                                  waits=(prep_tok(R_FWQ + oc * 128, 128),),
                                  new_group=(dc == 0)))
            wq_pre[oc] = [tq_[-1]] * 8
        t_bps = [t_projps[0], t_projps[1]]
        for kc2 in range(2):
            for sb in range(4):
                rot = sb % 2
                last = None
                for dc in range(8):
                    w = [t_nxt, wkct[dc]] + ([t_bps[rot]] if dc == 0 else [])
                    last = k.op("tensor", lambda e, kc2=kc2, sb=sb, dc=dc, rot=rot:
                                e.matmul(ps[rot][:, :],
                                         WKV[0][:, dc * 256 + kc2 * 128:
                                                dc * 256 + (kc2 + 1) * 128],
                                         NXT[dc][:, sb * 512:(sb + 1) * 512],
                                         start=(dc == 0), stop=(dc == 7)),
                                waits=w, inc=(dc == 7))
                t_bps[rot] = k.op("vector", lambda e, kc2=kc2, sb=sb, rot=rot:
                                  e.tensor_copy(KCT[kc2][:, sb * 512:(sb + 1) * 512],
                                                ps[rot][:, :]),
                                  waits=(last,), inc=1)
                t_kc_last = last
        for ch in range(NCH):
            rot = ch % 2
            vx = VX2[ch // 4][:, (ch % 4) * 512:(ch % 4) * 512 + 257]
            last = None
            for dc in range(8):
                w = [t_nxt, wvct[dc]] + ([t_bps[rot]] if dc == 0 else [])
                last = k.op("tensor", lambda e, ch=ch, dc=dc, rot=rot: e.matmul(
                    ps[rot][:, 0:256], NXT[dc][:, ch * 128:(ch + 1) * 128],
                    WKV2[0][:, dc * 256:(dc + 1) * 256],
                    start=(dc == 0), stop=(dc == 7)), waits=w, inc=(dc == 7))
            k.op("vector", lambda e, vx=vx: e.memset(vx[:, 256:257], 1.0))
            t_bps[rot] = k.op("vector", lambda e, vx=vx, rot=rot: e.tensor_copy(
                vx[:, 0:256], ps[rot][:, 0:256]), waits=(last,), inc=1)
        for oc in range(8):
            wpar = oc % 2
            wdst = WQR[0][:, wpar * 1024:(wpar + 1) * 1024]
            if oc in wq_pre:
                tq = wq_pre[oc]
            else:
                stg = WKV[0][:, wpar * 1024:(wpar + 1) * 1024]
                w = [prep_tok(R_FWQ + oc * 128, 128), gw(f"wqs{wpar}")]
                if oc < 4:
                    w.append(t_kc_last)
                t_st = k.dma("sync", stg,
                             wsc[R_FWQ + oc * 128:R_FWQ + (oc + 1) * 128, :],
                             waits=w)
                t_tr8 = transpose_chunk(stg, 8,
                                        lambda i, wdst=wdst: wdst[:, i * 128:
                                                                  (i + 1) * 128],
                                        (t_st,), dst_guard=(gw(f"wqr{wpar}"),))
                gset(f"wqs{wpar}", t_tr8)
                tq = [t_tr8] * 8
            for sb in range(4):
                rot = sb % 2
                last = None
                for dc in range(8):
                    w = [t_nxt, tq[dc]] + ([t_bps[rot]] if dc == 0 else [])
                    last = k.op("tensor",
                                lambda e, sb=sb, dc=dc, rot=rot, wdst=wdst: e.matmul(
                                    ps[rot][:, :], wdst[:, dc * 128:(dc + 1) * 128],
                                    NXT[dc][:, sb * 512:(sb + 1) * 512],
                                    start=(dc == 0), stop=(dc == 7)),
                                waits=w, inc=(dc == 7))
                t_bps[rot] = k.op("vector", lambda e, oc=oc, sb=sb, rot=rot:
                                  e.tensor_copy(QT[oc][:, sb * 512:(sb + 1) * 512],
                                                ps[rot][:, :]),
                                  waits=(last,), inc=1)
                gset(f"wqr{wpar}", t_bps[rot])
        t_b1 = k.tok("vector")

        k.barrier()
        pool.free("nxt", "wkv", "wkv2", "wqr", "scr")
        PTL = pool.alloc("ptl", 4)
        A2T = pool.alloc("a2t", 8)
        t_scps = [None, None]
        t_avps = [None, None]
        t_ptl_guard = None
        t_a2t = None
        for h in range(H2):
            for sb in range(4):
                t_pexp = [None] * 16
                for tc in range(16):
                    rot = tc % 2
                    last = None
                    for kc2 in range(2):
                        w = [t_b1] + ([t_scps[rot], t_ptl_guard] if kc2 == 0 else [])
                        last = k.op("tensor",
                                    lambda e, h=h, sb=sb, tc=tc, kc2=kc2, rot=rot:
                                    e.matmul(ps[rot][:, :],
                                             KCT[kc2][:, tc * 128:(tc + 1) * 128],
                                             QT[2 * h + kc2][:, sb * 512:(sb + 1) * 512],
                                             start=(kc2 == 0), stop=(kc2 == 1)),
                                    waits=w, inc=(kc2 == 1))
                    dst = PTL[tc // 4][:, (tc % 4) * 512:(tc % 4 + 1) * 512]
                    t_pexp[tc] = k.op("scalar", lambda e, dst=dst, rot=rot: e.activation(
                        dst, ps[rot][:, :], AF.Exp, scale=0.0625),
                        waits=(last,), inc=1)
                    t_scps[rot] = t_pexp[tc]
                for ss in range(4):
                    ch = sb * 4 + ss
                    rot = ss % 2
                    last = None
                    for tc in range(16):
                        w = [t_pexp[tc]] + ([t_avps[rot]] if tc == 0 else [])
                        last = k.op("tensor", lambda e, ss=ss, tc=tc, rot=rot: e.matmul(
                            ps[2 + rot][:, 0:257],
                            PTL[tc // 4][:, (tc % 4) * 512 + ss * 128:
                                         (tc % 4) * 512 + (ss + 1) * 128],
                            VX2[tc // 4][:, (tc % 4) * 512:(tc % 4) * 512 + 257],
                            start=(tc == 0), stop=(tc == 15)), waits=w, inc=(tc == 15))
                    zz = avz[:, rot:rot + 1]
                    t_zz = k.op("vector", lambda e, zz=zz, rot=rot: e.reciprocal(
                        zz, ps[2 + rot][:, 256:257]),
                        waits=(last, gw(f"avz{rot}")), inc=1)
                    piece = LSC[rot][:, 0:256]
                    t_pc = k.op("vector", lambda e, piece=piece, zz=zz, rot=rot:
                                e.tensor_scalar(piece, ps[2 + rot][:, 0:256],
                                                zz, None, op0=ALU.mult),
                                waits=(gw(f"lsc{rot}"), t_zz), inc=1)
                    t_avps[rot] = t_pc
                    gset(f"avz{rot}", t_pc)
                    t_a2t = transpose_chunk(
                        piece, 2, None, (t_pc,),
                        dst_contig=A2T[ch // 2][:, (ch % 2) * 1024 + 2 * h * 128:
                                               (ch % 2) * 1024 + (2 * h + 2) * 128])
                    gset(f"lsc{rot}", t_a2t)
            t_ptl_guard = t_a2t

        k.barrier()
        pool.free("kct", "vx2", "qt", "ptl")
        WOT = pool.alloc("wot", 4)
        NX2T = pool.alloc("nx2t", 8)
        LNC = pool.alloc("lnc2", 1)
        SCR = pool.alloc("scr", 2)
        BC = pool.alloc("bc", 2)
        BCv = [f32v(BC[0]), f32v(BC[1])]
        for key_ in list(G):
            if key_.startswith("scr") or key_.startswith("bc"):
                G.pop(key_)
        wot_ap = lambda ac: WOT[ac // 2][:, (ac % 2) * 1024:(ac % 2 + 1) * 1024]
        wot_t = []
        for ac in range(8):
            wot_t.append(k.dmat("sync", wot_ap(ac),
                                wsc[R_FWO:R_FWO + 1024, ac * 128:(ac + 1) * 128],
                                waits=(prep_tok(R_FWO, 1024),), new_group=(ac == 0)))
        wot_t = [wot_t[-1]] * 8
        t_g2 = bias_bcast(BCv[1], din["g2"][0:1, :])
        tl = k.dma("sync", brow[:, 0:8], din["febias"][0:1, :], waits=(gw("brow"),))
        tm = k.op("tensor", lambda e: e.matmul(ps[4][:, 0:8], onec_f[0:1, :],
                                               brow[0:1, 0:8], start=True, stop=True),
                  waits=(tl, t_g2), inc=1)
        t_eb = k.op("vector", lambda e: e.tensor_copy(eb_bc[:, :], ps[4][:, 0:8]),
                    waits=(tm,), inc=1)
        gset("brow", t_eb)
        t_wr = k.dma("gpsimd", wrbf[:, :], din["fWr"][:, :])
        t_wrt = None
        for dc in range(8):
            tmw = k.op("tensor", lambda e, dc=dc: e.transpose(
                pt[:, 0:8], wrbf[:, dc * 128:(dc + 1) * 128], ident[0:8, 0:8]),
                waits=(t_wr, t_id, gw("pt0")), inc=1)
            t_wrt = k.op("vector", lambda e, dc=dc: e.tensor_copy(
                wrt[:, dc * 8:(dc + 1) * 8], pt[:, 0:8]), waits=(tmw,), inc=1)
            gset("pt0", t_wrt)

        t_xff = [None] * NCH
        st = {"rps": t_eb}
        t_comb_ch = [None] * NCH
        t_ct = None
        pend_b = []

        def c1_tail(pch, tnx_p, nxc_p):
            par = pch % 2
            ch = pch
            t_nx = tnx_p
            t_nx2 = transpose_chunk(nxc_p, 8,
                                    lambda i, ch=ch: NX2T[i][:, ch * 128:(ch + 1) * 128],
                                    (t_nx,))
            gset(f"l2c{par}", t_nx2)
            o = par * 96
            last = None
            for dc in range(8):
                w = [t_nx2, t_wrt] + ([st["rps"]] if dc == 0 else [])
                last = k.op("tensor", lambda e, ch=ch, dc=dc: e.matmul(
                    ps[4][:, 0:8], NX2T[dc][:, ch * 128:(ch + 1) * 128],
                    wrt[:, dc * 8:(dc + 1) * 8],
                    start=(dc == 0), stop=(dc == 7)), waits=w, inc=(dc == 7))
            lg = rt[:, o:o + 8]
            rw = rt[:, o + 8:o + 16]
            i1 = rt[:, o + 16:o + 24]
            rw2 = rt[:, o + 24:o + 32]
            i2 = rt[:, o + 32:o + 40]
            t3 = rt[:, o + 40:o + 48]
            nm = rt[:, o + 80:o + 81]
            sm = rt[:, o + 81:o + 82]
            rs = rt[:, o + 82:o + 83]
            m1 = rt[:, o + 83:o + 84]
            m2 = rt[:, o + 84:o + 85]
            dd = rt[:, o + 85:o + 86]
            w1 = rt[:, o + 86:o + 87]
            w2 = rt[:, o + 87:o + 88]
            t1_ = k.op("vector", lambda e, lg=lg: e.tensor_add(lg, ps[4][:, 0:8],
                                                               eb_bc[:, :]),
                       waits=(last, t_eb, gw(f"rt{par}")), inc=1)
            st["rps"] = t1_
            t_ = k.op("vector", lambda e, lg=lg, nm=nm: e.reduce_max(
                nm, lg, axis=AX.X, negate=True), waits=(t1_,), inc=1)
            tsf = k.op("scalar", lambda e, lg=lg, nm=nm, rw=rw, sm=sm: e.activation(
                rw, lg, AF.Exp, bias=nm, accum_out=sm), waits=(t_,), inc=1)
            t_ = k.op("vector", lambda e, sm=sm, rs=rs: e.reciprocal(rs, sm),
                      waits=(tsf,), inc=1)
            t_ = k.op("vector", lambda e, rw=rw, rs=rs: e.tensor_scalar(
                rw, rw, rs, None, op0=ALU.mult), waits=(t_,), inc=1)
            t_ = k.op("vector", lambda e, rw=rw, m1=m1: e.reduce_max(
                m1, rw, axis=AX.X), waits=(t_,), inc=1)
            t_ = k.op("vector", lambda e, rw=rw, m1=m1, i1=i1: e.tensor_scalar(
                i1, rw, m1, None, op0=ALU.is_equal), waits=(t_,), inc=1)
            t_ = k.op("vector", lambda e, i1=i1, rw2=rw2: e.tensor_scalar(
                rw2, i1, -1e9, None, op0=ALU.mult), waits=(t_,), inc=1)
            t_ = k.op("vector", lambda e, rw=rw, rw2=rw2: e.tensor_add(rw2, rw2, rw),
                      waits=(t_,), inc=1)
            t_ = k.op("vector", lambda e, rw2=rw2, m2=m2: e.reduce_max(
                m2, rw2, axis=AX.X), waits=(t_,), inc=1)
            t_ = k.op("vector", lambda e, rw2=rw2, m2=m2, i2=i2: e.tensor_scalar(
                i2, rw2, m2, None, op0=ALU.is_equal), waits=(t_,), inc=1)
            t2_ = k.op("vector", lambda e, m1=m1, m2=m2, dd=dd: e.tensor_sub(dd, m2, m1),
                       waits=(t_,), inc=1)
            ted = k.op("scalar", lambda e, dd=dd: e.activation(dd, dd, AF.Exp),
                       waits=(t2_,), inc=1)
            t_ = k.op("vector", lambda e, dd=dd: e.tensor_scalar(
                dd, dd, 1.0, None, op0=ALU.add), waits=(ted,), inc=1)
            t_ = k.op("vector", lambda e, dd=dd, w1=w1: e.reciprocal(w1, dd),
                      waits=(t_,), inc=1)
            t_ = k.op("vector", lambda e, w1=w1, w2=w2: e.tensor_scalar(
                w2, w1, -1.0, 1.0, op0=ALU.mult, op1=ALU.add), waits=(t_,), inc=1)
            cc_ = combine[:, ch * 8:(ch + 1) * 8]
            t_ = k.op("vector", lambda e, i1=i1, w1=w1, cc_=cc_: e.tensor_scalar(
                cc_, i1, w1, None, op0=ALU.mult), waits=(t_,), inc=1)
            t_ = k.op("vector", lambda e, i2=i2, w2=w2, t3=t3: e.tensor_scalar(
                t3, i2, w2, None, op0=ALU.mult), waits=(t_,), inc=1)
            t_comb = k.op("vector", lambda e, cc_=cc_, t3=t3: e.tensor_add(cc_, cc_, t3),
                          waits=(t_,), inc=1)
            gset(f"rt{par}", t_comb)
            t_comb_ch[ch] = t_comb

        for ch in range(NCH):
            par = ch % 2
            tofs = []
            for hf in range(2):
                rot = hf
                last = None
                for ac in range(8):
                    w = [t_a2t, wot_t[ac]] + ([t_bps[rot]] if ac == 0 else [])
                    last = k.op("tensor", lambda e, ch=ch, hf=hf, ac=ac, rot=rot:
                                e.matmul(ps[rot][:, :],
                                         A2T[ch // 2][:, (ch % 2) * 1024 + ac * 128:
                                                      (ch % 2) * 1024 + (ac + 1) * 128],
                                         wot_ap(ac)[:, hf * 512:(hf + 1) * 512],
                                         start=(ac == 0), stop=(ac == 7)),
                                waits=w, inc=(ac == 7))
                t_bps[rot] = k.op("vector", lambda e, ch=ch, hf=hf, rot=rot:
                                  e.tensor_add(
                                      f32v(BIG16[ch])[:, hf * 512:(hf + 1) * 512],
                                      f32v(BIG16[ch])[:, hf * 512:(hf + 1) * 512],
                                      ps[rot][:, :]),
                                  waits=(last,), inc=1)
                tofs.append(t_bps[rot])
            t_xff[ch] = tofs[1]
            if "xff" in taps:
                k.dma("sync", tap_d["xff"][ch * 128:(ch + 1) * 128, :], f32v(BIG16[ch]),
                      waits=(tofs[0], tofs[1]))
            if pend_b:
                c1_tail(*pend_b.pop(0))
            bsrc = f32v(BIG16[ch])
            rstd2, trs = rms_stats(bsrc, EPS_RMS, par, (t_xff[ch],))
            scr = f32v(SCR[par])
            t_sc = k.op("vector", lambda e, s=bsrc, r=rstd2, d=scr: e.tensor_scalar(
                d, s, r, None, op0=ALU.mult), waits=(trs, gw(f"scr{par}")), inc=1)
            nxc = LNC[0][:, par * 1024:(par + 1) * 1024]
            t_nx = k.op("vector", lambda e, s=scr, d=nxc, b=BCv[1]: e.tensor_tensor(
                d, s, b, op=ALU.mult), waits=(t_g2, gw(f"l2c{par}"), t_sc), inc=1)
            gset(f"scr{par}", t_nx)
            gset(f"cols{par}", t_nx)
            pend_b.append((ch, t_nx, nxc))
        while pend_b:
            c1_tail(*pend_b.pop(0))
        t_nx2 = k.tok("vector")
        if "combine" in taps:
            k.dma("sync", tap_d["combine"][:, :], combine[:, :],
                  waits=(t_comb_ch[NCH - 1],))

        k.barrier()
        pool.free("lnc2", "bc", "scr", "lsc", "a2t", "wot")
        W1TA = pool.alloc("w1ta", 4)
        W1TB = pool.alloc("w1tb", 4)
        W3T = pool.alloc("w3t", 4)
        W2T = pool.alloc("w2t", 4)
        SG = pool.alloc("sg", 2)
        HP = pool.alloc("hp", 2)
        TMP = pool.alloc("tmp", 1)
        tmpv = f32v(TMP[0])
        w_ap = lambda W, dc: W[dc // 2][:, (dc % 2) * 1024:(dc % 2 + 1) * 1024]

        t_wmm_guard = None
        t_w1_guard = None
        t_w3_guard = None
        t_w2_guard = None
        w1_pref = None
        t_cb = None
        t_gps = [None, None]
        t_ups = [None, None]
        t_eops = [st["rps"], st["rps"]]
        t_addg = [None, None]
        t_sg_guard = None
        t_hp_guard = None
        last_eo = None
        for w_i in range(9):
            shared = (w_i == 0)
            if shared:
                rows1, rows2, rows3 = R_SW1, R_SW2, R_SW3
            else:
                e_i = w_i - 1
                rows1 = R_EW1 + e_i * 1024
                rows2 = R_EW2 + e_i * 1024
                rows3 = R_EW3 + e_i * 1024
            W1T = W1TA if w_i % 2 == 0 else W1TB
            wt = {}
            if w_i == 0:
                tt_ = None
                for dc in range(8):
                    tt_ = k.dmat("sync", w_ap(W1T, dc),
                                 wsc[rows1:rows1 + 1024, dc * 128:(dc + 1) * 128],
                                 waits=(prep_tok(rows1, 1024),), new_group=(dc == 0))
                wt["w1"] = [tt_] * 8
            else:
                wt["w1"] = w1_pref
            for nm_, rows, W, grd in (("w3", rows3, W3T, t_w3_guard),
                                      ("w2", rows2, W2T, t_w2_guard)):
                tt_ = None
                for dc in range(8):
                    tt_ = k.dmat("sync", w_ap(W, dc),
                                 wsc[rows:rows + 1024, dc * 128:(dc + 1) * 128],
                                 waits=(prep_tok(rows, 1024), grd),
                                 new_group=(dc == 0))
                wt[nm_] = [tt_] * 8
            if w_i < 8:
                nrows1 = R_EW1 + w_i * 1024
                WN1 = W1TB if w_i % 2 == 0 else W1TA
                tt_ = None
                for dc in range(8):
                    tt_ = k.dmat(
                        "sync", w_ap(WN1, dc),
                        wsc[nrows1:nrows1 + 1024, dc * 128:(dc + 1) * 128],
                        waits=(prep_tok(nrows1, 1024), t_w1_guard),
                        new_group=(dc == 0))
                w1_pref = [tt_] * 8
            last_g_mm = None
            last_u_mm = None
            last_eo_mm = None
            for tb in range(4):
                t_hp = [None] * 8
                for fc in range(8):
                    rot = fc % 2
                    lastg = None
                    for dc in range(8):
                        w = [t_nx2, wt["w1"][dc]] + ([t_gps[rot]] if dc == 0 else [])
                        lastg = k.op("tensor",
                                     lambda e, fc=fc, dc=dc, tb=tb, rot=rot,
                                     wap=w_ap(W1T, dc): e.matmul(
                                         ps[rot][:, :],
                                         wap[:, fc * 128:(fc + 1) * 128],
                                         NX2T[dc][:, tb * 512:(tb + 1) * 512],
                                         start=(dc == 0), stop=(dc == 7)),
                                     waits=w, inc=(dc == 7))
                    last_g_mm = lastg
                    sg_ap = SG[fc // 4][:, (fc % 4) * 512:(fc % 4 + 1) * 512]
                    t_sg = k.op("scalar", lambda e, sg_ap=sg_ap, rot=rot: e.activation(
                        sg_ap, ps[rot][:, :], AF.Silu),
                        waits=(lastg, t_sg_guard), inc=1)
                    t_gps[rot] = t_sg
                    lastu = None
                    for dc in range(8):
                        w = [t_nx2, wt["w3"][dc]] + ([t_ups[rot]] if dc == 0 else [])
                        lastu = k.op("tensor",
                                     lambda e, fc=fc, dc=dc, tb=tb, rot=rot: e.matmul(
                                         ps[2 + rot][:, :],
                                         w_ap(W3T, dc)[:, fc * 128:(fc + 1) * 128],
                                         NX2T[dc][:, tb * 512:(tb + 1) * 512],
                                         start=(dc == 0), stop=(dc == 7)),
                                     waits=w, inc=(dc == 7))
                    last_u_mm = lastu
                    hp_ap = HP[fc // 4][:, (fc % 4) * 512:(fc % 4 + 1) * 512]
                    t_u = k.op("vector",
                               lambda e, hp_ap=hp_ap, sg_ap=sg_ap, rot=rot:
                               e.tensor_tensor(hp_ap, sg_ap, ps[2 + rot][:, :],
                                               op=ALU.mult),
                               waits=(lastu, t_sg, t_hp_guard), inc=1)
                    t_ups[rot] = t_u
                    t_hp[fc] = t_u
                for ss in range(4):
                    ch = tb * 4 + ss
                    for hf in range(2):
                        rot = (ss * 2 + hf) % 2
                        last = None
                        for fc in range(8):
                            w = [t_hp[fc], wt["w2"][fc]] + \
                                ([t_eops[rot]] if fc == 0 else [])
                            last = k.op("tensor",
                                        lambda e, ss=ss, hf=hf, fc=fc, rot=rot:
                                        e.matmul(
                                            ps[4 + rot][:, :],
                                            HP[fc // 4][:, (fc % 4) * 512 + ss * 128:
                                                        (fc % 4) * 512 + (ss + 1) * 128],
                                            w_ap(W2T, fc)[:, hf * 512:(hf + 1) * 512],
                                            start=(fc == 0), stop=(fc == 7)),
                                        waits=w, inc=(fc == 7))
                        last_eo_mm = last
                        if shared:
                            t_eops[rot] = k.op("vector",
                                               lambda e, ch=ch, hf=hf, rot=rot:
                                               e.tensor_add(
                                                   f32v(BIG16[ch])
                                                   [:, hf * 512:(hf + 1) * 512],
                                                   f32v(BIG16[ch])
                                                   [:, hf * 512:(hf + 1) * 512],
                                                   ps[4 + rot][:, :]),
                                               waits=(last,), inc=1)
                            t_addg[rot] = t_eops[rot]
                        else:
                            cbc = combine[:, ch * 8 + e_i:ch * 8 + e_i + 1]
                            th = tmpv[:, rot * 512:(rot + 1) * 512]
                            t_m = k.op("vector",
                                       lambda e, th=th, rot=rot, cbc=cbc:
                                       e.tensor_scalar(th, ps[4 + rot][:, :],
                                                       cbc, None, op0=ALU.mult),
                                       waits=(last, t_addg[rot]), inc=1)
                            t_eops[rot] = t_m
                            t_addg[rot] = k.op("vector",
                                               lambda e, ch=ch, hf=hf, th=th:
                                               e.tensor_add(
                                                   f32v(BIG16[ch])
                                                   [:, hf * 512:(hf + 1) * 512],
                                                   f32v(BIG16[ch])
                                                   [:, hf * 512:(hf + 1) * 512],
                                                   th), waits=(t_m,), inc=1)
                        last_eo = t_addg[rot]
                t_sg_guard = last_eo
                t_hp_guard = last_eo
            t_wmm_guard = last_eo
            t_w1_guard = last_g_mm
            t_w3_guard = last_u_mm
            t_w2_guard = last_eo_mm

        k.barrier()
        pool.free("w1ta", "w1tb", "w3t", "w2t", "sg", "hp", "tmp", "nx2t")
        SCR = pool.alloc("scr", 2)
        LSC = pool.alloc("lsc", 2)
        for key_ in list(G):
            if key_.startswith("scr") or key_.startswith("lsc"):
                G.pop(key_)
        XRL = pool.alloc("xrl", 1)
        CSO = pool.alloc("cso", 2)
        for ch in range(NCH):
            par = ch % 2
            xrl_ap = XRL[0][:, par * 1024:(par + 1) * 1024]
            t_x2l = k.dma("sync", xrl_ap, xsp[ch * 128:(ch + 1) * 128, :],
                          waits=(gw(f"xrl{par}"),))
            bsrc = f32v(BIG16[ch])
            mean, rstd, tst = ln_stats(bsrc, EPS_LN, par, (last_eo,))
            nrm = f32v(SCR[par])
            t_nr = k.op("scalar", lambda e, s=bsrc, m=mean, r=rstd, d=nrm:
                        e.activation(d, s, AF.Identity, scale=r, bias=m),
                        waits=(tst, gw(f"scr{par}")), inc=1)
            ot = f32v(CSO[par])
            t_o = k.op("vector", lambda e, n=nrm, x=xrl_ap, d=ot: e.tensor_add(d, n, x),
                       waits=(t_x2l, gw(f"cso{par}"), t_nr), inc=1)
            gset(f"xrl{par}", t_o)
            gset(f"scr{par}", t_o)
            gset(f"cols{par}", t_o)
            b0, jj = ch // 2, ch % 2
            off = (jj * 1024 + b0) * 1024
            w_ = [t_o] + ([k.lastg["sync"]] if ch == 0 else [])
            t_w = k.op("sync", lambda e, ot=ot, off=off: e.dma_start(
                out=bass.AP(out_d, off, [[8 * 1024, 128], [1, 1024]]), in_=ot),
                waits=w_, inc=16)
            k.lastg["sync"] = t_w
            gset(f"cso{par}", t_w)
        k.final_wait("sync")

        k.build(block)
    return nc


_CACHE = {}


def kernel(**inp):
    x = np.ascontiguousarray(inp["L0_attn_query"], dtype=np.float32)
    key = tuple(sorted(TAPS))
    if key not in _CACHE:
        _CACHE[key] = build_nc(tuple(TAPS))
    nc = _CACHE[key]

    idx = (np.pi / 2) * (np.arange(1, L + 1, dtype=np.float64) / L)
    sin_t = np.sin(idx).astype(np.float32)
    cos_t = np.cos(idx).astype(np.float32)

    w = {
        "aWq": inp["L0_attn_Wq"], "abq": inp["L0_attn_bq"].reshape(1, DM),
        "aWk": inp["L0_attn_Wk"], "abk": inp["L0_attn_bk"].reshape(1, DM),
        "aWv": inp["L0_attn_Wv"], "abv": inp["L0_attn_bv"].reshape(1, DM),
        "aWo": inp["L0_attn_Wo"], "abo": inp["L0_attn_bo"].reshape(1, DM),
        "g1": inp["L0_ffn_g1"].reshape(1, DM), "g2": inp["L0_ffn_g2"].reshape(1, DM),
        "fWq": inp["L0_ffn_Wq"], "fWkc": inp["L0_ffn_Wkc"], "fWvc": inp["L0_ffn_Wvc"],
        "fWo": inp["L0_ffn_Wo"], "fWr": inp["L0_ffn_Wr"],
        "febias": inp["L0_ffn_expert_bias"].reshape(1, NE),
        "sW1": inp["L0_ffn_sW1"], "sW2": inp["L0_ffn_sW2"], "sW3": inp["L0_ffn_sW3"],
        "eW1": inp["L0_ffn_eW1"], "eW2": inp["L0_ffn_eW2"], "eW3": inp["L0_ffn_eW3"],
    }
    w = {kk: np.ascontiguousarray(vv, dtype=np.float32) for kk, vv in w.items()}

    in_maps = []
    for c in range(8):
        xq = np.ascontiguousarray(x[:, c * 256:(c + 1) * 256, :].reshape(L, DM))
        xr = np.ascontiguousarray(
            x[c].reshape(256, 8, DM).transpose(1, 0, 2).reshape(L, DM))
        l0q = np.empty((128, 16), np.int64)
        l0k = np.empty((128, 16), np.int64)
        for ch in range(16):
            l0q[:, ch] = c * 256 + (ch % 2) * 128 + np.arange(128)
            l0k[:, ch] = ch * 128 + np.arange(128)
        sccq = np.concatenate([sin_t[l0q], cos_t[l0q]], axis=1).astype(np.float32)
        scck = np.concatenate([sin_t[l0k], cos_t[l0k]], axis=1).astype(np.float32)
        rq = c * 256 + (np.arange(L) % 256)
        srowq = np.stack([sin_t[rq], cos_t[rq]], axis=0).astype(np.float32)
        m = {"x": np.ascontiguousarray(x[c]), "xq": xq, "xr": xr,
             "sccq": np.ascontiguousarray(sccq), "scck": np.ascontiguousarray(scck),
             "srowq": np.ascontiguousarray(srowq)}
        m.update(w)
        in_maps.append(m)

    res = run_bass_kernel_spmd(nc, in_maps, core_ids=list(range(8)))
    out = np.stack([np.asarray(res.results[c]["out"], dtype=np.float32)
                    for c in range(8)], axis=0)
    if TAPS:
        kernel.tap_results = [{t: np.asarray(res.results[c]["tap_" + t])
                               for t in TAPS} for c in range(8)]
    return out
